# version 06
# baseline (speedup 1.0000x reference)
import math
from contextlib import ExitStack

import numpy as np
import ml_dtypes

import concourse.bass as bass
import concourse.mybir as mybir
from concourse.bass_utils import run_bass_kernel_spmd

F32 = mybir.dt.float32
BF16 = mybir.dt.bfloat16
AF = mybir.ActivationFunctionType
ALU = mybir.AluOpType
AX = mybir.AxisListType

B, T, C = 8, 2048, 1024
NCH = C // 128
QT = 4
TW = 512
NTT = T // 128
EPS = 1e-5

_DEBUG = []
_NTILES = NTT
_SKIP = set()
_P0CUT = "full"
_PHASES = {"p1", "p2", "p3", "p4"}
_P1CUT = "full"
_P4LIM = (4, 2, 16)



def _alibi_slopes(num_heads):
    closest_pow2 = 2 ** int(math.floor(math.log2(num_heads)))
    base = 2.0 ** (-(2.0 ** (-(math.log2(closest_pow2) - 3))))
    powers = np.arange(1, closest_pow2 + 1, dtype=np.float32)
    slopes = base ** powers
    if closest_pow2 != num_heads:
        eb = 2.0 ** (-(2.0 ** (-(math.log2(2 * closest_pow2) - 3))))
        ep = np.arange(1, 2 * (num_heads - closest_pow2) + 1, 2, dtype=np.float32)
        slopes = np.concatenate([slopes, eb ** ep])
    return slopes[:num_heads].astype(np.float32)


def _consts():
    c = {}
    c["identb"] = np.eye(128, dtype=ml_dtypes.bfloat16)
    c["identf"] = np.eye(128, dtype=np.float32)
    D = 64
    inv_freq = 1.0 / (10000.0 ** (np.arange(0, D, 2, dtype=np.float64) / D))
    freqs = np.outer(np.arange(T, dtype=np.float64), inv_freq)
    emb = np.concatenate([freqs, freqs], 1)
    cos = np.cos(emb).astype(np.float32)
    sin = np.sin(emb).astype(np.float32)
    ssin = sin.copy()
    ssin[:, :32] *= -1.0
    c["cosb"] = cos.T[np.tile(np.arange(64), 2)].astype(ml_dtypes.bfloat16)
    c["sinb"] = ssin.T[np.tile(np.arange(64), 2)].astype(ml_dtypes.bfloat16)
    tri = (np.arange(128)[:, None] <= np.arange(128)[None, :])
    c["tri01"] = tri.astype(ml_dtypes.bfloat16)
    c["masktri"] = np.where(tri, 0.0, np.float32(-1e30)).astype(np.float32)
    kidx = np.arange(T)
    c["kbias"] = np.stack([kidx // 64, kidx % 64]).astype(ml_dtypes.bfloat16)
    c["qbias"] = np.stack([-(kidx // 64), -(kidx % 64)]).astype(ml_dtypes.bfloat16)
    sel = np.zeros((2, 128), dtype=ml_dtypes.bfloat16)
    sel[0, 0:64] = 1.0
    sel[1, 64:128] = 1.0
    c["sel01"] = sel
    sl = _alibi_slopes(32)
    slb = sl.astype(ml_dtypes.bfloat16).astype(np.float64)
    qr_ = np.zeros((128, T), np.float64)
    kr_ = np.zeros((128, T), np.float64)
    for h in range(32):
        qr_[4 * h + 0] = 64.0 * slb[h]
        qr_[4 * h + 1] = slb[h]
        qr_[4 * h + 2] = -(kidx // 64)
        qr_[4 * h + 3] = -(kidx % 64)
        kr_[4 * h + 0] = kidx // 64
        kr_[4 * h + 1] = kidx % 64
        kr_[4 * h + 2] = 64.0 * slb[h]
        kr_[4 * h + 3] = slb[h]
    c["qextr"] = qr_.astype(ml_dtypes.bfloat16)
    c["kextr"] = kr_.astype(ml_dtypes.bfloat16)
    return c



class _Sem:
    def __init__(self, h):
        self.h = h
        self.n = 0


class _FakeInstr:
    def then_inc(self, *a, **k):
        return self


_FAKE = _FakeInstr()


class _Eng:
    def __init__(self, plan, name):
        self._p = plan
        self._n = name

    def wait(self, sem, val=None):
        v = sem.n if val is None else val
        if v <= 0:
            return
        r = self._p.real if self._p.active == self._n else None
        if r is not None:
            r.wait_ge(sem.h, v)

    def inc(self, instr, sem, k=1):
        sem.n += k
        if instr is not _FAKE:
            instr.then_inc(sem.h, k)
        return instr

    def __getattr__(self, op):
        if self._p.active == self._n:
            return getattr(self._p.real, op)
        return lambda *a, **k: _FAKE


class Plan:
    def __init__(self, active, real):
        self.active = active
        self.real = real
        self.t = _Eng(self, "tensor")
        self.v = _Eng(self, "vector")
        self.s = _Eng(self, "scalar")
        self.g = _Eng(self, "gpsimd")
        self.y = _Eng(self, "sync")



def build_nc():
    nc = bass.Bass("TRN2", target_bir_lowering=False, detect_race_conditions=False)

    io = {}
    io["x"] = nc.dram_tensor("x", [T, C], F32, kind="ExternalInput").ap()
    io["wqkv"] = nc.dram_tensor("wqkv", [C, 4096], F32, kind="ExternalInput").ap()
    io["wout"] = nc.dram_tensor("wout", [2048, C], F32, kind="ExternalInput").ap()
    io["ln1w"] = nc.dram_tensor("ln1w", [C], F32, kind="ExternalInput").ap()
    io["ln1b"] = nc.dram_tensor("ln1b", [C], F32, kind="ExternalInput").ap()
    io["ln2w"] = nc.dram_tensor("ln2w", [C], F32, kind="ExternalInput").ap()
    io["ln2b"] = nc.dram_tensor("ln2b", [C], F32, kind="ExternalInput").ap()
    io["convw"] = nc.dram_tensor("convw", [C, 3], F32, kind="ExternalInput").ap()
    io["wattn"] = nc.dram_tensor("wattn", [C, 3072], F32, kind="ExternalInput").ap()
    io["wproj"] = nc.dram_tensor("wproj", [C, C], F32, kind="ExternalInput").ap()
    io["wrout"] = nc.dram_tensor("wrout", [C, 4], F32, kind="ExternalInput").ap()
    io["wv"] = nc.dram_tensor("wv", [4, C, 8192], F32, kind="ExternalInput").ap()
    io["wp"] = nc.dram_tensor("wp", [4, 4096, C], F32, kind="ExternalInput").ap()
    io["out"] = nc.dram_tensor("out", [T, C], F32, kind="ExternalOutput").ap()

    cst = {k: nc.inline_tensor(np.ascontiguousarray(v), "c_" + k).ap()
           for k, v in _consts().items()}

    dr = {
        "xnbf": nc.dram_tensor("d_xnbf", [C, T], BF16).ap(),
        "h2": nc.dram_tensor("d_h2", [C, T], BF16).ap(),
        "apair": nc.dram_tensor("d_apair", [16, 128, T], BF16).ap(),
        "bpair": nc.dram_tensor("d_bpair", [8, 128, T], BF16).ap(),
        "masks": nc.dram_tensor("d_masks", [4, T], BF16).ap(),
    }

    dbg = {}
    def _dbg(name, shape, dtype=F32):
        dbg[name] = nc.dram_tensor("dbg_" + name, shape, dtype, kind="ExternalOutput").ap()
    if "p0" in _DEBUG:
        _dbg("xn_fm", [C, T])
        _dbg("h1_fm", [C, T], BF16)
    if "p1" in _DEBUG:
        _dbg("hc_fm", [C, T], BF16)
        _dbg("acc1", [C, T])
    if "p2" in _DEBUG:
        _dbg("h2", [C, T], BF16)
        _dbg("masks", [4, T], BF16)
    if "p3" in _DEBUG:
        _dbg("acc3", [C, T])
    if "p4" in _DEBUG:
        _dbg("acc4", [C, T])

    slopes = _alibi_slopes(32)

    with ExitStack() as ctx:
        def sb(name, shape, dtype):
            return ctx.enter_context(nc.sbuf_tensor(name, shape, dtype))

        def psum(name, shape, dtype=F32):
            return ctx.enter_context(nc.psum_tensor(name, shape, dtype))

        sems = {n: _Sem(ctx.enter_context(nc.semaphore(n)))
                for n in ["tS", "vS", "sS", "gD", "yD", "g2",
                          "yX0", "yX1", "yS0", "yS1", "gW0", "gW1", "gW2"]}

        ACC = sb("ACC", [128, NCH * T], F32)
        XNBF = sb("XNBF", [128, NCH * T], BF16)
        HC = sb("HC", [128, NCH * T], BF16)
        AR = sb("AR", [128, 24 * 1024], BF16)
        ARF = sb("ARF", [128, 4096], F32)
        IDB = sb("IDB", [128, 128], BF16)
        IDF = sb("IDF", [128, 128], F32)
        TRI = sb("TRI", [128, 128], BF16)
        MTRI = sb("MTRI", [128, 128], F32)
        SELA = sb("SELA", [1, 128], BF16)
        SELB = sb("SELB", [1, 128], BF16)
        ONESB = sb("ONESB", [128, 1], BF16)
        ONESF = sb("ONESF", [128, 1], F32)
        ONE1F = sb("ONE1F", [1, 128], F32)
        ONE1B = sb("ONE1B", [1, 128], BF16)
        LN1WC = sb("LN1WC", [128, NCH], F32)
        LN1BC = sb("LN1BC", [128, NCH], F32)
        LN2WC = sb("LN2WC", [128, NCH], F32)
        LN2BC = sb("LN2BC", [128, NCH], F32)
        CONVW = sb("CONVW", [128, 3 * NCH], F32)
        ROUTW = sb("ROUTW", [128, 4 * NCH], F32)
        MROW = sb("MROW", [1, T], BF16)
        EPSC = sb("EPSC", [128, 1], F32)
        STC = sb("STC", [128, 16], F32)
        SQD = sb("SQD", [128, C], F32)

        PS = [psum(f"PS{i}", [128, TW], F32) for i in range(8)]
        PD = PS[7]

        def fm(tile, ch, cols):
            return tile[:, 2048 * ch + cols.start: 2048 * ch + cols.stop]

        def plan(p: Plan):
            for s_ in sems.values():
                s_.n = 0
            t, v, s, g, y = p.t, p.v, p.s, p.g, p.y
            tS, vS, sS, gD, yD, g2 = (sems[n] for n in ["tS", "vS", "sS", "gD", "yD", "g2"])
            yX = [sems["yX0"], sems["yX1"]]
            yS = [sems["yS0"], sems["yS1"]]
            gW = [sems["gW0"], sems["gW1"], sems["gW2"]]

            def ydma_on(sem, dst, src):
                y.inc(y.dma_start(dst, src), sem, 16)
                return sem.n

            def ydma(dst, src):
                y.inc(y.dma_start(dst, src), yD, 16)
                return yD.n

            def gdma(dst, src):
                g.inc(g.dma_start(dst, src), gD, 16)
                return gD.n

            def ydma_nc(dst, src):
                y.inc(y.dma_start(dst, src, allow_slow_non_contiguous=True), yD, 16)
                return yD.n

            def vbar(instr):
                v.inc(instr, vS, 1)
                v.wait(vS, vS.n)

            def sbar(instr):
                s.inc(instr, sS, 1)
                s.wait(sS, sS.n)

            for dst, key in [(IDB, "identb"), (IDF, "identf"), (TRI, "tri01"),
                             (MTRI, "masktri")]:
                ydma(dst[:], cst[key])
            ydma(SELA[:], cst["sel01"][0:1, :])
            ydma(SELB[:], cst["sel01"][1:2, :])
            ydma_nc(LN1WC[:], io["ln1w"].rearrange("(c p) -> p c", p=128))
            ydma_nc(LN1BC[:], io["ln1b"].rearrange("(c p) -> p c", p=128))
            ydma_nc(LN2WC[:], io["ln2w"].rearrange("(c p) -> p c", p=128))
            ydma_nc(LN2BC[:], io["ln2b"].rearrange("(c p) -> p c", p=128))
            ydma_nc(CONVW[:].rearrange("p (c k) -> p c k", k=3),
                 io["convw"].rearrange("(c p) k -> p c k", p=128))
            ydma_nc(ROUTW[:].rearrange("p (c e) -> p c e", e=4),
                 io["wrout"].rearrange("(c p) e -> p c e", p=128))
            ld_consts = yD.n
            slab_free = [0, 0]
            bank_free_s = [0] * 8
            bank_free_v = [0] * 8
            v.memset(ONESB[:], 1.0)
            v.memset(ONESF[:], 1.0)
            v.memset(ONE1F[:], 1.0)
            v.memset(ONE1B[:], 1.0)
            v.inc(v.memset(EPSC[:], EPS), vS, 1)
            v_consts = vS.n
            t.wait(yD, ld_consts)
            t.wait(vS, v_consts)
            v.wait(yD, ld_consts)
            s.wait(yD, ld_consts)
            XT = [ARF[:, 0:1024], ARF[:, 1024:2048]]
            XN = ARF[:, 2048:3072]
            H1B = AR[:, 0:1024]
            H1FM_BASE = 8192

            def h1fm(ch, cols):
                o = H1FM_BASE + 2048 * ch
                return AR[:, o + cols.start: o + cols.stop]

            ld_x = [0] * (NTT + 1)
            ld_x[0] = ydma_on(yX[0], XT[0][:], io["x"][0:128, :])
            done_sx = [0] * NTT
            done_vx = [0] * NTT
            done_t = [0] * NTT
            done_evv = [0] * NTT
            done_evs = [0] * NTT

            for i in range(_NTILES):
                xt = XT[i % 2]
                if i + 1 < _NTILES:
                    if i >= 1:
                        y.wait(sS, done_sx[i - 1])
                        y.wait(vS, done_vx[i - 1])
                    ld_x[i + 1] = ydma_on(yX[(i + 1) % 2], XT[(i + 1) % 2][:],
                                          io["x"][128 * (i + 1):128 * (i + 2), :])
                v.wait(yX[i % 2], ld_x[i])
                vbar(v.reduce_sum(STC[:, 0:1], xt[:], axis=AX.X))
                done_vx[i] = vS.n
                s.wait(yX[i % 2], ld_x[i])
                if i >= 1:
                    s.wait(tS, done_t[i - 1])
                s.inc(s.activation(SQD[:], xt[:], AF.Square), sS, 1)
                v.wait(sS, sS.n)
                vbar(v.reduce_sum(STC[:, 1:2], SQD[:], axis=AX.X))
                vbar(v.tensor_scalar(STC[:, 2:3], STC[:, 0:1], 1.0 / C, None, ALU.mult))
                vbar(v.tensor_scalar(STC[:, 3:4], STC[:, 1:2], 1.0 / C, None, ALU.mult))
                vbar(v.tensor_mul(STC[:, 4:5], STC[:, 2:3], STC[:, 2:3]))
                v.inc(v.tensor_sub(STC[:, 3:4], STC[:, 3:4], STC[:, 4:5]), vS, 1)
                s.wait(vS, vS.n)
                s.inc(s.activation(STC[:, 4:5], STC[:, 3:4], AF.Sqrt, bias=EPSC[:]), sS, 1)
                v.wait(sS, sS.n)
                vbar(v.reciprocal(STC[:, 5:6], STC[:, 4:5]))
                vbar(v.tensor_mul(STC[:, 6:7], STC[:, 2:3], STC[:, 5:6]))
                v.inc(v.tensor_scalar(STC[:, 6:7], STC[:, 6:7], -1.0, None, ALU.mult), vS, 1)
                s.wait(vS, vS.n)
                if _P0CUT == "stats":
                    done_sx[i] = sS.n
                    done_t[i] = tS.n
                    done_evv[i] = vS.n
                    done_evs[i] = sS.n
                    continue
                sbar(s.activation(XN[:], xt[:], AF.Identity,
                                  scale=STC[:, 5:6], bias=STC[:, 6:7]))
                v.wait(sS, sS.n)
                vbar(v.reduce_sum(STC[:, 8:9], XN[:], axis=AX.X))
                sbar(s.activation(SQD[:], XN[:], AF.Square))
                done_sx[i] = sS.n
                v.wait(sS, sS.n)
                vbar(v.reduce_sum(STC[:, 9:10], SQD[:], axis=AX.X))
                vbar(v.tensor_scalar(STC[:, 10:11], STC[:, 8:9], 1.0 / C, None, ALU.mult))
                vbar(v.tensor_scalar(STC[:, 11:12], STC[:, 9:10], 1.0 / C, None, ALU.mult))
                vbar(v.tensor_mul(STC[:, 12:13], STC[:, 10:11], STC[:, 10:11]))
                v.inc(v.tensor_sub(STC[:, 11:12], STC[:, 11:12], STC[:, 12:13]), vS, 1)
                s.wait(vS, vS.n)
                s.inc(s.activation(STC[:, 12:13], STC[:, 11:12], AF.Sqrt, bias=EPSC[:]), sS, 1)
                v.wait(sS, sS.n)
                vbar(v.reciprocal(STC[:, 13:14], STC[:, 12:13]))
                vbar(v.tensor_mul(STC[:, 14:15], STC[:, 10:11], STC[:, 13:14]))
                v.inc(v.tensor_scalar(STC[:, 14:15], STC[:, 14:15], -1.0, None, ALU.mult), vS, 1)
                if _P0CUT == "xn":
                    done_sx[i] = sS.n
                    done_t[i] = tS.n
                    done_evv[i] = vS.n
                    done_evs[i] = sS.n
                    continue
                s.wait(vS, vS.n)
                s.inc(s.activation(H1B[:], XN[:], AF.Identity,
                                   scale=STC[:, 13:14], bias=STC[:, 14:15]), sS, 1)
                ready = sS.n
                if _P0CUT == "h1":
                    done_sx[i] = sS.n
                    done_t[i] = tS.n
                    done_evv[i] = vS.n
                    done_evs[i] = sS.n
                    continue
                t.wait(sS, ready)
                if i >= 1:
                    t.wait(vS, done_evv[i - 1])
                    t.wait(sS, done_evs[i - 1])
                for half, bank in [(0, PS[0]), (1, PS[1])]:
                    for k in range(4):
                        ch = 4 * half + k
                        t.inc(t.matmul(bank[:, 128 * k:128 * k + 128],
                                       XN[:, 128 * ch:128 * ch + 128], IDF[:],
                                       start=(k == 0), stop=(k == 3)), tS, 1)
                for half, bank in [(0, PS[2]), (1, PS[3])]:
                    for k in range(4):
                        ch = 4 * half + k
                        t.inc(t.matmul(bank[:, 128 * k:128 * k + 128],
                                       H1B[:, 128 * ch:128 * ch + 128], IDB[:],
                                       start=(k == 0), stop=(k == 3)), tS, 1)
                done_t[i] = tS.n
                if _P0CUT == "tp":
                    done_evv[i] = vS.n
                    done_evs[i] = sS.n
                    continue
                cw = slice(128 * i, 128 * i + 128)
                v.wait(tS, done_t[i])
                for half, bank in [(0, PS[0]), (1, PS[1])]:
                    for k in range(4):
                        ch = 4 * half + k
                        v.inc(v.tensor_copy(fm(ACC, ch, cw),
                                            bank[:, 128 * k:128 * k + 128]), vS, 1)
                        v.inc(v.tensor_copy(fm(XNBF, ch, cw),
                                            bank[:, 128 * k:128 * k + 128]), vS, 1)
                for half, bank in [(0, PS[2]), (1, PS[3])]:
                    for k in range(4):
                        ch = 4 * half + k
                        v.inc(v.tensor_scalar(h1fm(ch, cw), bank[:, 128 * k:128 * k + 128],
                                              LN1WC[:, ch:ch + 1], LN1BC[:, ch:ch + 1],
                                              ALU.mult, ALU.add), vS, 1)
                done_evv[i] = vS.n
                done_evs[i] = sS.n

            if "p0" in _DEBUG:
                y.wait(vS, vS.n)
                y.wait(sS, sS.n)
                ydma(dbg["xn_fm"].rearrange("(c p) t -> p c t", p=128),
                     ACC[:].rearrange("p (c t) -> p c t", c=NCH))
                ydma(dbg["h1_fm"].rearrange("(c p) t -> p c t", p=128),
                     AR[:, H1FM_BASE:H1FM_BASE + NCH * 2048].rearrange(
                         "p (c t) -> p c t", c=NCH))

            if "p1" in _PHASES:
                v.wait(vS, vS.n)
                for ch in range(NCH):
                    w0 = CONVW[:, 3 * ch + 0:3 * ch + 1]
                    w1 = CONVW[:, 3 * ch + 1:3 * ch + 2]
                    w2 = CONVW[:, 3 * ch + 2:3 * ch + 3]
                    h1c = lambda a, b_: AR[:, H1FM_BASE + 2048 * ch + a:
                                           H1FM_BASE + 2048 * ch + b_]
                    hcc = lambda a, b_: HC[:, 2048 * ch + a:2048 * ch + b_]
                    vbar(v.tensor_scalar(hcc(0, T), h1c(0, T), w2, None, ALU.mult))
                    vbar(v.scalar_tensor_tensor(hcc(1, T), h1c(0, T - 1), w1,
                                                hcc(1, T), ALU.mult, ALU.add))
                    vbar(v.scalar_tensor_tensor(hcc(2, T), h1c(0, T - 2), w0,
                                                hcc(2, T), ALU.mult, ALU.add))
                conv_done = vS.n
                if "p1" in _DEBUG:
                    y.wait(vS, conv_done)
                    ydma(dbg["hc_fm"].rearrange("(c p) t -> p c t", p=128),
                         HC[:].rearrange("p (c t) -> p c t", c=NCH))

                A_SLB = [0, 1024]
                A_QS, A_KS, A_VS = 2048, 4096, 6144
                A_COS, A_SIN = 8192, 10240
                A_RT, A_QR, A_KR, A_VT = 12288, 14336, 16384, 18432
                A_PB = [[20512 + 512 * k for k in range(4)],
                        [12288 + 512 * k for k in range(4)]]
                A_RC = 2048
                A_OUT = [3072, 3584]
                y.wait(vS, conv_done)
                ld_cos = ydma(AR[:, A_COS:A_COS + T], cst["cosb"])
                ld_cos = ydma(AR[:, A_SIN:A_SIN + T], cst["sinb"])

                slab_free[0] = slab_free[1] = tS.n
                spill = [0, 0]
                pending = [None]

                def qkv_chunk(oc, dst_base, slot, scale=1.0):
                    src = io["wattn"].rearrange("(kc p) n -> p kc n", p=128)
                    wsem = gW[0] if slot == 0 else gW[2]
                    g.wait(tS, slab_free[slot])
                    g.inc(g.dma_start(AR[:, A_SLB[slot]:A_SLB[slot] + 1024]
                                      .rearrange("p (kc n) -> p kc n", n=128),
                                      src[:, :, 128 * oc:128 * oc + 128]), wsem, 16)
                    ldw = wsem.n
                    for qt in range(QT):
                        bank = qt % 4
                        qtw = slice(512 * qt, 512 * qt + 512)
                        t.wait(wsem, ldw)
                        t.wait(vS, conv_done)
                        t.wait(sS, bank_free_s[bank])
                        t.wait(vS, bank_free_v[bank])
                        for kc in range(NCH):
                            t.inc(t.matmul(
                                PS[bank][:],
                                AR[:, A_SLB[slot] + 128 * kc:A_SLB[slot] + 128 * kc + 128],
                                fm(HC, kc, qtw),
                                start=(kc == 0), stop=(kc == NCH - 1)), tS, 1)
                        mm = tS.n
                        s.wait(tS, mm)
                        if scale == 1.0:
                            s.inc(s.copy(AR[:, dst_base + 512 * qt:dst_base + 512 * qt + 512],
                                         PS[bank][:]), sS, 1)
                        else:
                            s.inc(s.activation(
                                AR[:, dst_base + 512 * qt:dst_base + 512 * qt + 512],
                                PS[bank][:], AF.Copy, scale=scale), sS, 1)
                        bank_free_s[bank] = sS.n
                    slab_free[slot] = tS.n

                def rope(src_base, dst_base):
                    g.wait(sS, sS.n)
                    g.wait(tS, tS.n)
                    for (dp, sp_) in [(0, 32), (32, 0), (64, 96), (96, 64)]:
                        g.inc(g.dma_start(AR[dp:dp + 32, A_RT:A_RT + T],
                                          AR[sp_:sp_ + 32, src_base:src_base + T]),
                              gW[1], 16)
                    rot_ld = gW[1].n
                    v.wait(gW[1], rot_ld)
                    v.wait(sS, sS.n)
                    vbar(v.tensor_mul(AR[:, dst_base:dst_base + T],
                                      AR[:, src_base:src_base + T],
                                      AR[:, A_COS:A_COS + T]))
                    vbar(v.tensor_mul(AR[:, A_RT:A_RT + T],
                                      AR[:, A_RT:A_RT + T],
                                      AR[:, A_SIN:A_SIN + T]))
                    v.inc(v.tensor_add(AR[:, dst_base:dst_base + T],
                                       AR[:, dst_base:dst_base + T],
                                       AR[:, A_RT:A_RT + T]), vS, 1)

                npairs = 0 if _P1CUT == "conv" else (1 if _P1CUT in (
                    "qkv1", "rope1", "vt1", "attn1") else 8)
                for p_ in range(npairs):
                    qkv_chunk(p_, A_QS, 0, scale=0.125)
                    qkv_chunk(8 + p_, A_KS, 1)
                    qkv_chunk(16 + p_, A_VS, 0)
                    if _P1CUT == "qkv1":
                        continue
                    v.wait(yD, ld_cos)
                    rope(A_QS, A_QR)
                    rope(A_KS, A_KR)
                    rope_done = vS.n
                    if _P1CUT == "rope1":
                        continue
                    v.wait(sS, sS.n)
                    v.inc(v.memset(AR[:, A_VT + 64:A_VT + 64 + 1]
                                   .rearrange("p (a b) -> p a b", a=1) if False else
                                   AR[:, A_VT:A_VT + 2080]
                                   .rearrange("p (kt c) -> p kt c", c=130)[:, :, 64:65],
                                   1.0), vS, 1)
                    v.inc(v.memset(AR[:, A_VT:A_VT + 2080]
                                   .rearrange("p (kt c) -> p kt c", c=130)[:, :, 129:130],
                                   1.0), vS, 1)
                    ones_done = vS.n
                    for half in range(2):
                        bank = 4 + half
                        t.wait(sS, sS.n)
                        t.wait(vS, bank_free_v[bank])
                        for k in range(8):
                            kt = 8 * half + k
                            if k % 4 == 0:
                                t.wait(sS, bank_free_s[bank])
                            t.inc(t.matmul(PS[bank][:, 128 * (k % 4):128 * (k % 4) + 128],
                                           AR[:, A_VS + 128 * kt:A_VS + 128 * kt + 128],
                                           IDB[:],
                                           start=(k % 4 == 0), stop=(k % 4 == 3)), tS, 1)
                            if k % 4 == 3:
                                mm = tS.n
                                s.wait(tS, mm)
                                kt0 = kt - 3
                                for hh in range(2):
                                    s.inc(s.copy(
                                        AR[:, A_VT + 130 * kt0:A_VT + 130 * (kt0 + 4)]
                                        .rearrange("p (kt c) -> p kt c", c=130)
                                        [:, :, 65 * hh:65 * hh + 64],
                                        PS[bank][:].rearrange("p (kt c) -> p kt c", c=128)
                                        [:, :, 64 * hh:64 * hh + 64]), sS, 1)
                                bank_free_s[bank] = sS.n
                    vt_done = sS.n
                    if _P1CUT == "vt1":
                        continue
                    SCB = [[0, 1], [6, 7]]
                    for qt in range(QT):
                        qtw = slice(512 * qt, 512 * qt + 512)
                        nkt = 4 * qt + 4
                        pb_free = [[0] * 4, [0] * 4]
                        pvb = [[2, 4][qt % 2], [3, 5][qt % 2]]
                        exp_done_at = [[0] * 4, [0] * 4]
                        tri_done_at = [[None] * 4, [None] * 4]

                        def p1_pv_wave(kt):
                            off = 128 * (kt - 4 * qt) if kt >= 4 * qt else 0
                            for hh in range(2):
                                pv_bank = pvb[hh]
                                pb = A_PB[hh][kt % 4]
                                t.wait(sS, exp_done_at[hh][kt % 4])
                                tri = tri_done_at[hh][kt % 4]
                                if tri:
                                    t.wait(vS, tri)
                                if kt == 0:
                                    t.wait(sS, bank_free_s[pv_bank])
                                    t.wait(vS, bank_free_v[pv_bank])
                                t.inc(t.matmul(
                                    PS[pv_bank][0:65, slice(off, 512)],
                                    AR[:, A_VT + 130 * kt + 65 * hh:
                                       A_VT + 130 * kt + 65 * hh + 65],
                                    AR[:, pb + off:pb + 512],
                                    start=(kt == 0), stop=(kt == nkt - 1)), tS, 1)
                                pb_free[hh][kt % 4] = tS.n

                        for kt in range(nkt):
                            off = 128 * (kt - 4 * qt) if kt >= 4 * qt else 0
                            w_ = slice(off, 512)
                            for hh in range(2):
                                sc_bank = SCB[hh][kt % 2]
                                t.wait(sS, bank_free_s[sc_bank])
                                t.wait(vS, bank_free_v[sc_bank])
                                t.wait(sS, vt_done)
                                t.wait(vS, max(rope_done, ones_done))
                                t.inc(t.matmul(
                                    PS[sc_bank][:, w_],
                                    AR[64 * hh:64 * hh + 64,
                                       A_KR + 128 * kt:A_KR + 128 * kt + 128],
                                    AR[64 * hh:64 * hh + 64,
                                       A_QR + 512 * qt + off:A_QR + 512 * qt + 512],
                                    start=True, stop=True), tS, 1)
                                sc_mm = tS.n
                                s.wait(tS, sc_mm)
                                s.wait(tS, pb_free[hh][kt % 4])
                                pb = A_PB[hh][kt % 4]
                                s.inc(s.activation(AR[:, pb + off:pb + 512],
                                                   PS[sc_bank][:, w_], AF.Exp), sS, 1)
                                exp_done_at[hh][kt % 4] = sS.n
                                bank_free_s[sc_bank] = sS.n
                                if kt >= 4 * qt:
                                    v.wait(sS, sS.n)
                                    v.inc(v.tensor_mul(AR[:, pb + off:pb + off + 128],
                                                       AR[:, pb + off:pb + off + 128],
                                                       TRI[:]), vS, 1)
                                    tri_done_at[hh][kt % 4] = vS.n
                                else:
                                    tri_done_at[hh][kt % 4] = None
                            if kt >= 1:
                                p1_pv_wave(kt - 1)
                        p1_pv_wave(nkt - 1)
                        pv_all = tS.n

                        def mk_chain1(p_c, qt_c, pv_all_c, bA, bB):
                            def chain():
                                s.wait(tS, pv_all_c)
                                s.inc(s.activation(ARF[0:1, 512:1024],
                                                   PS[bA][64:65, :], AF.Ln), sS, 1)
                                sbar(s.activation(ARF[32:33, 512:1024],
                                                  PS[bB][64:65, :], AF.Ln))
                                s.inc(s.activation(ARF[0:1, 1024:1536],
                                                   ARF[0:1, 512:1024], AF.Exp,
                                                   scale=-1.0), sS, 1)
                                sbar(s.activation(ARF[32:33, 1024:1536],
                                                  ARF[32:33, 512:1024], AF.Exp,
                                                  scale=-1.0))
                                rc_sc = sS.n
                                v.wait(sS, rc_sc)
                                v.wait(tS, pv_all_c)
                                for rr, bb in ((0, bA), (32, bB)):
                                    v.inc(v.tensor_mul(ARF[rr:rr + 1, 1536:2048],
                                                       ARF[rr:rr + 1, 1024:1536],
                                                       PS[bb][64:65, :]), vS, 1)
                                    vbar(v.tensor_scalar(ARF[rr:rr + 1, 1536:2048],
                                                         ARF[rr:rr + 1, 1536:2048],
                                                         -1.0, 2.0, ALU.mult, ALU.add))
                                    vbar(v.tensor_mul(ARF[rr:rr + 1, 1024:1536],
                                                      ARF[rr:rr + 1, 1024:1536],
                                                      ARF[rr:rr + 1, 1536:2048]))
                                v.inc(v.tensor_copy(AR[0:1, A_RC:A_RC + 512],
                                                    ARF[0:1, 1024:1536]), vS, 1)
                                v.inc(v.tensor_copy(AR[0:1, A_RC + 512:A_RC + 1024],
                                                    ARF[32:33, 1024:1536]), vS, 1)
                                rc_vv = vS.n
                                outb = A_OUT[qt_c % 2]
                                v.wait(yS[qt_c % 2], spill[qt_c % 2])
                                v.inc(v.tensor_copy(AR[0:64, outb:outb + 512],
                                                    PS[bA][0:64, :]), vS, 1)
                                v.inc(v.tensor_copy(AR[64:128, outb:outb + 512],
                                                    PS[bB][0:64, :]), vS, 1)
                                ev_done = vS.n
                                bank_free_s[bB] = sS.n
                                bank_free_v[bB] = ev_done
                                t.wait(vS, max(ev_done, rc_vv))
                                t.inc(t.matmul(PS[bA][:], SELA[:],
                                               AR[0:1, A_RC:A_RC + 512],
                                               start=True, stop=False), tS, 1)
                                t.inc(t.matmul(PS[bA][:], SELB[:],
                                               AR[0:1, A_RC + 512:A_RC + 1024],
                                               start=False, stop=True), tS, 1)
                                bc_mm = tS.n
                                v.wait(tS, bc_mm)
                                v.inc(v.tensor_mul(AR[:, outb:outb + 512],
                                                   AR[:, outb:outb + 512],
                                                   PS[bA][:]), vS, 1)
                                div_done = vS.n
                                bank_free_s[bA] = sS.n
                                bank_free_v[bA] = div_done
                                y.wait(vS, div_done)
                                spill[qt_c % 2] = ydma_on(
                                    yS[qt_c % 2],
                                    dr["bpair"][p_c, :, 512 * qt_c:512 * qt_c + 512],
                                    AR[:, outb:outb + 512])
                            return chain

                        new_chain = mk_chain1(p_, qt, pv_all, pvb[0], pvb[1])
                        if pending[0] is not None:
                            prev = pending[0]
                            pending[0] = new_chain
                            prev()
                        else:
                            pending[0] = new_chain
                    if pending[0] is not None:
                        pending[0]()
                        pending[0] = None
                    s.wait(vS, vS.n)
                    s.wait(tS, tS.n)
                    s.wait(yS[0], spill[0])
                    s.wait(yS[1], spill[1])

                if pending[0] is not None:
                    pending[0]()
                    pending[0] = None
                if _P1CUT != "full":
                    return
                A_PJ, A_BP = 0, 8192
                g.wait(tS, tS.n)
                g.wait(vS, vS.n)
                ld_pj = gdma(AR[:, A_PJ:A_PJ + 8192]
                             .rearrange("p (r n) -> p r n", n=1024),
                             io["wproj"].rearrange("(r p) n -> p r n", p=128))
                for qt in range(QT):
                    qtw = slice(512 * qt, 512 * qt + 512)
                    y.wait(tS, tS.n)
                    ld_bp = 0
                    for p_ in range(8):
                        ld_bp = ydma(AR[:, A_BP + 512 * p_:A_BP + 512 * p_ + 512],
                                     dr["bpair"][p_, :, 512 * qt:512 * qt + 512])
                    for cg in range(2):
                        for cc in range(4):
                            c_ = 4 * cg + cc
                            bank = cc
                            t.wait(gD, ld_pj)
                            t.wait(yD, ld_bp)
                            t.wait(vS, bank_free_v[bank])
                            t.wait(sS, bank_free_s[bank])
                            for p_ in range(8):
                                t.inc(t.matmul(
                                    PS[bank][:],
                                    AR[:, A_PJ + 1024 * p_ + 128 * c_:
                                       A_PJ + 1024 * p_ + 128 * c_ + 128],
                                    AR[:, A_BP + 512 * p_:A_BP + 512 * p_ + 512],
                                    start=(p_ == 0), stop=(p_ == 7)), tS, 1)
                            mm = tS.n
                            v.wait(tS, mm)
                            v.inc(v.tensor_add(fm(ACC, c_, qtw), fm(ACC, c_, qtw),
                                               PS[bank][:]), vS, 1)
                            bank_free_v[bank] = vS.n

                if "p1" in _DEBUG:
                    y.wait(vS, vS.n)
                    ydma(dbg["acc1"].rearrange("(c p) t -> p c t", p=128),
                         ACC[:].rearrange("p (c t) -> p c t", c=NCH))


            if "p2" in _PHASES:
                A_H2F = [0, 512]
                A_XSQ = [1024, 1536]
                A_RW = 2048
                A_LG = 2048
                h2st_free = [0, 0]
                p2_t0 = tS.n
                v.wait(vS, vS.n)
                for qt in range(QT):
                    qtw = slice(512 * qt, 512 * qt + 512)
                    for bk in (0, 1, 2, 3, 4):
                        t.wait(sS, bank_free_s[bk])
                        t.wait(vS, bank_free_v[bk])
                    for c_ in range(NCH):
                        t.inc(t.matmul(PS[0][0:1, :], ONESF[:], fm(ACC, c_, qtw),
                                       start=(c_ == 0), stop=(c_ == NCH - 1)), tS, 1)
                    for c_ in range(NCH):
                        xq = ARF[:, A_XSQ[c_ % 2]:A_XSQ[c_ % 2] + 512]
                        if c_ >= 2:
                            v.wait(tS, tS.n)
                        v.inc(v.tensor_mul(xq, fm(ACC, c_, qtw), fm(ACC, c_, qtw)), vS, 1)
                        t.wait(vS, vS.n)
                        t.inc(t.matmul(PS[1][0:1, :], ONESF[:], xq,
                                       start=(c_ == 0), stop=(c_ == NCH - 1)), tS, 1)
                    stat_mm = tS.n
                    R = lambda j_: ARF[0:1, 512 * j_:512 * j_ + 512]
                    s.wait(tS, stat_mm)
                    s.inc(s.copy(R(0), PS[0][0:1, :]), sS, 1)
                    sbar(s.copy(R(1), PS[1][0:1, :]))
                    bank_free_s[0] = sS.n
                    bank_free_s[1] = sS.n
                    v.wait(sS, sS.n)
                    vbar(v.tensor_scalar(R(2), R(0), 1.0 / C, None, ALU.mult))
                    vbar(v.tensor_scalar(R(3), R(1), 1.0 / C, None, ALU.mult))
                    vbar(v.tensor_mul(R(4), R(2), R(2)))
                    v.inc(v.tensor_sub(R(3), R(3), R(4)), vS, 1)
                    s.wait(vS, vS.n)
                    sbar(s.activation(R(4), R(3), AF.Sqrt, bias=EPSC[0:1, :]))
                    v.wait(sS, sS.n)
                    vbar(v.reciprocal(R(5), R(4)))
                    vbar(v.tensor_mul(R(6), R(2), R(5)))
                    v.inc(v.tensor_scalar(R(6), R(6), -1.0, None, ALU.mult), vS, 1)
                    rb_done = vS.n
                    t.wait(vS, rb_done)
                    t.inc(t.matmul(PS[2][:], ONE1F[:], R(5),
                                   start=True, stop=True), tS, 1)
                    t.inc(t.matmul(PS[3][:], ONE1F[:], R(6),
                                   start=True, stop=True), tS, 1)
                    bc_mm = tS.n
                    for c_ in range(NCH):
                        hb = ARF[:, A_H2F[c_ % 2]:A_H2F[c_ % 2] + 512]
                        v.wait(tS, tS.n)
                        v.wait(tS, bc_mm)
                        v.inc(v.tensor_mul(hb, fm(ACC, c_, qtw), PS[2][:]), vS, 1)
                        vbar(v.tensor_add(hb, hb, PS[3][:]))
                        vbar(v.tensor_scalar(hb, hb, LN2WC[:, c_:c_ + 1],
                                             LN2BC[:, c_:c_ + 1], ALU.mult, ALU.add))
                        h2f_done = vS.n
                        t.wait(vS, h2f_done)
                        t.inc(t.matmul(PS[4][0:4, :], ROUTW[:, 4 * c_:4 * c_ + 4], hb,
                                       start=(c_ == 0), stop=(c_ == NCH - 1)), tS, 1)
                        st_slot = c_ % 2
                        s.wait(vS, h2f_done)
                        s.wait(yS[st_slot], h2st_free[st_slot])
                        s.inc(s.copy(AR[:, 512 * st_slot:512 * st_slot + 512], hb), sS, 1)
                        h2ev = sS.n
                        y.wait(sS, h2ev)
                        h2st_free[st_slot] = ydma_on(
                            yS[st_slot], dr["h2"][128 * c_:128 * c_ + 128, qtw],
                            AR[:, 512 * st_slot:512 * st_slot + 512])
                        if "p2" in _DEBUG:
                            h2st_free[st_slot] = ydma_on(
                                yS[st_slot], dbg["h2"][128 * c_:128 * c_ + 128, qtw],
                                AR[:, 512 * st_slot:512 * st_slot + 512])
                    rt_mm = tS.n
                    s.wait(tS, rt_mm)
                    s.wait(vS, vS.n)
                    sbar(s.copy(ARF[0:4, 0:512], PS[4][0:4, :]))
                    lg_ev = sS.n
                    bank_free_s[4] = sS.n
                    g.wait(sS, lg_ev)
                    for e_ in range(1, 4):
                        g.inc(g.dma_start(R(4 + e_ - 1), ARF[e_:e_ + 1, 0:512]),
                              gW[1], 16)
                    lg_dma = gW[1].n
                    L_ = [R(0), R(4), R(5), R(6)]
                    v.wait(gW[1], lg_dma)
                    vbar(v.tensor_max(R(1), L_[0], L_[1]))
                    vbar(v.tensor_max(R(2), L_[2], L_[3]))
                    vbar(v.tensor_max(R(1), R(1), R(2)))
                    for Le in L_:
                        vbar(v.tensor_tensor(Le, Le, R(1), ALU.is_ge))
                    vbar(v.tensor_scalar(R(2), L_[0], -1.0, 1.0, ALU.mult, ALU.add))
                    vbar(v.tensor_scalar(R(3), L_[1], -1.0, 1.0, ALU.mult, ALU.add))
                    vbar(v.tensor_mul(L_[1], L_[1], R(2)))
                    vbar(v.tensor_mul(R(2), R(2), R(3)))
                    vbar(v.tensor_scalar(R(3), L_[2], -1.0, 1.0, ALU.mult, ALU.add))
                    vbar(v.tensor_mul(L_[2], L_[2], R(2)))
                    vbar(v.tensor_mul(R(2), R(2), R(3)))
                    vbar(v.tensor_mul(L_[3], L_[3], R(2)))
                    mrow_sp = 0
                    for e_, src_ in [(0, L_[0]), (1, L_[1]), (2, L_[2]), (3, L_[3])]:
                        v.wait(yS[0], mrow_sp)
                        v.inc(v.tensor_copy(MROW[0:1, 0:512], src_), vS, 1)
                        mv = vS.n
                        y.wait(vS, mv)
                        mrow_sp = ydma_on(yS[0], dr["masks"][e_:e_ + 1, 512 * qt:512 * qt + 512],
                                          MROW[0:1, 0:512])
                        if "p2" in _DEBUG:
                            mrow_sp = ydma_on(yS[0], dbg["masks"][e_:e_ + 1, 512 * qt:512 * qt + 512],
                                              MROW[0:1, 0:512])
                    v.wait(yS[0], mrow_sp)
                    bank_free_v[2] = vS.n
                    bank_free_v[3] = vS.n

            if "p3" in _PHASES:
                qext = lambda j_: 2048 * j_
                kext = lambda j_: 8192 + 2048 * j_
                A3_VS, A3_VT = 2048, 4096
                A3_PB = [6176 + 512 * k for k in range(8)]
                A3_QER, A3_KER = 10272, 12320
                A3_RC = 3072
                A3_OUT = [2048, 2560]
                slab_free[0] = slab_free[1] = tS.n
                ld_er = ydma(AR[:, A3_QER:A3_QER + T], cst["qextr"])
                ld_er = ydma(AR[:, A3_KER:A3_KER + T], cst["kextr"])
                p3_sp = [0, 0]
                for g_ in range(8):
                    s.wait(yS[0], p3_sp[0])
                    s.wait(yS[1], p3_sp[1])
                    wsrc = io["wqkv"].rearrange("(kc p) n -> p kc n", p=128)
                    for idx, oc in enumerate([2 * g_, 2 * g_ + 1, 16 + g_, 24 + g_]):
                        slot = idx % 2
                        wsem = gW[0] if slot == 0 else gW[2]
                        g.wait(tS, slab_free[slot])
                        g.inc(g.dma_start(AR[:, A_SLB[slot]:A_SLB[slot] + 1024]
                                          .rearrange("p (kc n) -> p kc n", n=128),
                                          wsrc[:, :, 128 * oc:128 * oc + 128]), wsem, 16)
                        ldw = wsem.n
                        for qt in range(QT):
                            bank = qt % 2
                            qtw = slice(512 * qt, 512 * qt + 512)
                            t.wait(wsem, ldw)
                            t.wait(sS, bank_free_s[bank])
                            t.wait(vS, bank_free_v[bank])
                            for kc in range(NCH):
                                t.inc(t.matmul(
                                    PS[bank][:],
                                    AR[:, A_SLB[slot] + 128 * kc:A_SLB[slot] + 128 * kc + 128],
                                    fm(XNBF, kc, qtw),
                                    start=(kc == 0), stop=(kc == NCH - 1)), tS, 1)
                            mm = tS.n
                            s.wait(tS, mm)
                            if idx < 2:
                                for hh in range(2):
                                    j_ = 2 * idx + hh
                                    s.inc(s.activation(
                                        HC[0:64, qext(j_) + 512 * qt:qext(j_) + 512 * qt + 512],
                                        PS[bank][64 * hh:64 * hh + 64, :],
                                        AF.Copy, scale=0.125), sS, 1)
                            elif idx == 2:
                                for j_ in range(4):
                                    hh = j_ // 2
                                    s.inc(s.copy(
                                        HC[0:64, kext(j_) + 512 * qt:kext(j_) + 512 * qt + 512],
                                        PS[bank][64 * hh:64 * hh + 64, :]), sS, 1)
                            else:
                                s.inc(s.copy(AR[:, A3_VS + 512 * qt:A3_VS + 512 * qt + 512],
                                             PS[bank][:]), sS, 1)
                            bank_free_s[bank] = sS.n
                        slab_free[slot] = tS.n
                    qkv_ev = sS.n
                    g.wait(sS, qkv_ev)
                    g.wait(yD, ld_er)
                    for j_ in range(4):
                        h_ = 4 * g_ + j_
                        g.inc(g.dma_start(HC[64:68, qext(j_):qext(j_) + T],
                                          AR[4 * h_:4 * h_ + 4, A3_QER:A3_QER + T]),
                              gW[1], 16)
                        g.inc(g.dma_start(HC[64:68, kext(j_):kext(j_) + T],
                                          AR[4 * h_:4 * h_ + 4, A3_KER:A3_KER + T]),
                              gW[1], 16)
                    ext_dma = gW[1].n
                    v.wait(sS, sS.n)
                    v.inc(v.memset(AR[:, A3_VT:A3_VT + 2080]
                                   .rearrange("p (kt c) -> p kt c", c=130)[:, :, 64:65],
                                   1.0), vS, 1)
                    v.inc(v.memset(AR[:, A3_VT:A3_VT + 2080]
                                   .rearrange("p (kt c) -> p kt c", c=130)[:, :, 129:130],
                                   1.0), vS, 1)
                    ones_done = vS.n
                    for half in range(2):
                        bank = 2 + half
                        t.wait(sS, qkv_ev)
                        t.wait(vS, bank_free_v[bank])
                        for k in range(8):
                            kt = 8 * half + k
                            if k % 4 == 0:
                                t.wait(sS, bank_free_s[bank])
                            t.inc(t.matmul(PS[bank][:, 128 * (k % 4):128 * (k % 4) + 128],
                                           AR[:, A3_VS + 128 * kt:A3_VS + 128 * kt + 128],
                                           IDB[:],
                                           start=(k % 4 == 0), stop=(k % 4 == 3)), tS, 1)
                            if k % 4 == 3:
                                mm = tS.n
                                s.wait(tS, mm)
                                kt0 = kt - 3
                                for hh in range(2):
                                    s.inc(s.copy(
                                        AR[:, A3_VT + 130 * kt0:A3_VT + 130 * (kt0 + 4)]
                                        .rearrange("p (kt c) -> p kt c", c=130)
                                        [:, :, 65 * hh:65 * hh + 64],
                                        PS[bank][:].rearrange("p (kt c) -> p kt c", c=128)
                                        [:, :, 64 * hh:64 * hh + 64]), sS, 1)
                                bank_free_s[bank] = sS.n
                    vt_done = sS.n
                    SCBH = [0, 1, 4, 5]
                    PVH = [2, 3, 6, 7]
                    for qt in range(QT):
                        nkt = 4 * qt + 4
                        exp_free = [0] * 4
                        pb_free = [[0, 0] for _ in range(4)]
                        pv_last = [0] * 4
                        def p3_pv_wave(kt):
                            off = 128 * (kt - 4 * qt) if kt >= 4 * qt else 0
                            for j_ in range(4):
                                pv_bank = PVH[j_]
                                pb = A3_PB[2 * j_ + (kt % 2)]
                                t.wait(sS, exp_done_at[j_][kt % 2])
                                tri = tri_done_at[j_][kt % 2]
                                if tri:
                                    t.wait(vS, tri)
                                if kt == 0:
                                    t.wait(sS, bank_free_s[pv_bank])
                                    t.wait(vS, bank_free_v[pv_bank])
                                t.inc(t.matmul(
                                    PS[pv_bank][0:65, slice(off, 512)],
                                    AR[:, A3_VT + 130 * kt + 65 * (j_ // 2):
                                       A3_VT + 130 * kt + 65 * (j_ // 2) + 65],
                                    AR[:, pb + off:pb + 512],
                                    start=(kt == 0), stop=(kt == nkt - 1)), tS, 1)
                                pb_free[j_][kt % 2] = tS.n
                                pv_last[j_] = tS.n

                        exp_done_at = [[0, 0] for _ in range(4)]
                        tri_done_at = [[None, None] for _ in range(4)]
                        for kt in range(nkt):
                            off = 128 * (kt - 4 * qt) if kt >= 4 * qt else 0
                            w_ = slice(off, 512)
                            diag = kt >= 4 * qt
                            for j_ in range(4):
                                h_ = 4 * g_ + j_
                                sc_bank = SCBH[j_]
                                t.wait(sS, bank_free_s[sc_bank])
                                t.wait(vS, bank_free_v[sc_bank])
                                t.wait(gW[1], ext_dma)
                                t.wait(sS, vt_done)
                                t.wait(vS, ones_done)
                                t.inc(t.matmul(
                                    PS[sc_bank][:, w_],
                                    HC[0:68, kext(j_) + 128 * kt:kext(j_) + 128 * kt + 128],
                                    HC[0:68, qext(j_) + 512 * qt + off:qext(j_) + 512 * qt + 512],
                                    start=True, stop=True), tS, 1)
                                sc_mm = tS.n
                                if diag and h_ < 3:
                                    v.wait(tS, sc_mm)
                                    v.inc(v.tensor_add(PS[sc_bank][:, off:off + 128],
                                                       PS[sc_bank][:, off:off + 128],
                                                       MTRI[:]), vS, 1)
                                    mask_done = vS.n
                                else:
                                    mask_done = None
                                pb = A3_PB[2 * j_ + (kt % 2)]
                                s.wait(tS, sc_mm)
                                if mask_done is not None:
                                    s.wait(vS, mask_done)
                                s.wait(tS, pb_free[j_][kt % 2])
                                s.inc(s.activation(AR[:, pb + off:pb + 512],
                                                   PS[sc_bank][:, w_], AF.Exp), sS, 1)
                                exp_done_at[j_][kt % 2] = sS.n
                                bank_free_s[sc_bank] = sS.n
                                if diag and h_ >= 3:
                                    v.wait(sS, sS.n)
                                    v.inc(v.tensor_mul(AR[:, pb + off:pb + off + 128],
                                                       AR[:, pb + off:pb + off + 128],
                                                       TRI[:]), vS, 1)
                                    tri_done_at[j_][kt % 2] = vS.n
                                else:
                                    tri_done_at[j_][kt % 2] = None
                            if kt >= 1:
                                p3_pv_wave(kt - 1)
                        p3_pv_wave(nkt - 1)
                        for j_ in range(4):
                            rc = A3_RC + 512 * (j_ % 2)
                            s.wait(tS, pv_last[j_])
                            sbar(s.activation(ARF[32 * j_:32 * j_ + 1, 512:1024],
                                              PS[PVH[j_]][64:65, :], AF.Ln))
                            sbar(s.activation(ARF[32 * j_:32 * j_ + 1, 1024:1536],
                                              ARF[32 * j_:32 * j_ + 1, 512:1024],
                                              AF.Exp, scale=-1.0))
                            rc_sc = sS.n
                            rr = 32 * j_
                            v.wait(sS, rc_sc)
                            v.wait(tS, pv_last[j_])
                            v.inc(v.tensor_mul(ARF[rr:rr + 1, 1536:2048],
                                               ARF[rr:rr + 1, 1024:1536],
                                               PS[PVH[j_]][64:65, :]), vS, 1)
                            vbar(v.tensor_scalar(ARF[rr:rr + 1, 1536:2048],
                                                 ARF[rr:rr + 1, 1536:2048],
                                                 -1.0, 2.0, ALU.mult, ALU.add))
                            vbar(v.tensor_mul(ARF[rr:rr + 1, 1024:1536],
                                              ARF[rr:rr + 1, 1024:1536],
                                              ARF[rr:rr + 1, 1536:2048]))
                            v.inc(v.tensor_copy(AR[0:1, rc:rc + 512],
                                                ARF[rr:rr + 1, 1024:1536]), vS, 1)
                            rc_vv = vS.n
                            outb = A3_OUT[j_ // 2]
                            hrow = slice(64 * (j_ % 2), 64 * (j_ % 2) + 64)
                            if j_ % 2 == 0:
                                v.wait(yS[j_ // 2], p3_sp[j_ // 2])
                            v.wait(tS, pv_last[j_])
                            v.inc(v.tensor_copy(AR[hrow, outb:outb + 512],
                                                PS[PVH[j_]][0:64, :]), vS, 1)
                            ev_done = vS.n
                            t.wait(vS, max(ev_done, rc_vv))
                            t.inc(t.matmul(PS[PVH[j_]][:],
                                           SELA[:] if j_ % 2 == 0 else SELB[:],
                                           AR[0:1, rc:rc + 512],
                                           start=True, stop=True), tS, 1)
                            bc_mm = tS.n
                            v.wait(tS, bc_mm)
                            v.inc(v.tensor_mul(AR[hrow, outb:outb + 512],
                                               AR[hrow, outb:outb + 512],
                                               PS[PVH[j_]][hrow, :]), vS, 1)
                            bank_free_s[PVH[j_]] = sS.n
                            bank_free_v[PVH[j_]] = vS.n
                            if j_ % 2 == 1:
                                div_done = vS.n
                                y.wait(vS, div_done)
                                p3_sp[j_ // 2] = ydma_on(
                                    yS[j_ // 2],
                                    dr["apair"][2 * g_ + j_ // 2, :,
                                                512 * qt:512 * qt + 512],
                                    AR[:, outb:outb + 512])
                A3_WO, A3_BP = 0, 16384
                g.wait(tS, tS.n)
                g.wait(sS, sS.n)
                g.wait(vS, vS.n)
                ld_wo = gdma(AR[:, A3_WO:A3_WO + 16384]
                             .rearrange("p (r n) -> p r n", n=1024),
                             io["wout"].rearrange("(r p) n -> p r n", p=128))
                for qt in range(QT):
                    qtw = slice(512 * qt, 512 * qt + 512)
                    y.wait(tS, tS.n)
                    ld_bp = 0
                    for p_ in range(16):
                        ld_bp = ydma(AR[:, A3_BP + 512 * p_:A3_BP + 512 * p_ + 512],
                                     dr["apair"][p_, :, qtw])
                    for c_ in range(NCH):
                        bank = c_ % 4
                        t.wait(gD, ld_wo)
                        t.wait(yD, ld_bp)
                        t.wait(vS, bank_free_v[bank])
                        t.wait(sS, bank_free_s[bank])
                        for p_ in range(16):
                            t.inc(t.matmul(
                                PS[bank][:],
                                AR[:, A3_WO + 1024 * p_ + 128 * c_:
                                   A3_WO + 1024 * p_ + 128 * c_ + 128],
                                AR[:, A3_BP + 512 * p_:A3_BP + 512 * p_ + 512],
                                start=(p_ == 0), stop=(p_ == 15)), tS, 1)
                        mm = tS.n
                        v.wait(tS, mm)
                        v.inc(v.tensor_add(fm(ACC, c_, qtw), fm(ACC, c_, qtw),
                                           PS[bank][:]), vS, 1)
                        bank_free_v[bank] = vS.n
                if "p3" in _DEBUG:
                    y.wait(vS, vS.n)
                    ydma(dbg["acc3"].rearrange("(c p) t -> p c t", p=128),
                         ACC[:].rearrange("p (c t) -> p c t", c=NCH))

            if "p4" in _PHASES:
                A4_H2, A4_WV, A4_PJ, A4_MB = 0, 16384, 20480, 22528

                def abuf(jp16, t_):
                    base = 2048 * (jp16 % 8) + 512 * t_
                    tile = XNBF if jp16 < 8 else HC
                    return tile[:, base:base + 512]

                y.wait(tS, tS.n)
                y.wait(vS, vS.n)
                y.wait(sS, sS.n)
                slab_free[0] = slab_free[1] = tS.n
                ld_h2 = 0
                for c_ in range(NCH):
                    ld_h2 = ydma(AR[:, A4_H2 + 2048 * c_:A4_H2 + 2048 * (c_ + 1)],
                                 dr["h2"][128 * c_:128 * c_ + 128, :])
                wv_src = io["wv"].rearrange("e (kc p) n -> e p kc n", p=128)
                pj_src = io["wp"].rearrange("e (r p) n -> e p r n", p=128)
                pj_free = [tS.n, tS.n]
                mrow_ld = 0
                for e_ in range(_P4LIM[0]):
                    y.wait(tS, tS.n)
                    mrow_ld = ydma(MROW[0:1, :], dr["masks"][e_:e_ + 1, :])
                    for t_ in range(4):
                        t.wait(yD, mrow_ld)
                        t.wait(sS, bank_free_s[6])
                        t.wait(vS, bank_free_v[6])
                        t.inc(t.matmul(PS[6][:], ONE1B[:],
                                       MROW[0:1, 512 * t_:512 * t_ + 512],
                                       start=True, stop=True), tS, 1)
                        mm = tS.n
                        s.wait(tS, mm)
                        s.inc(s.copy(AR[:, A4_MB + 512 * t_:A4_MB + 512 * t_ + 512],
                                     PS[6][:]), sS, 1)
                        bank_free_s[6] = sS.n
                    mb_done = sS.n
                    for half in range(_P4LIM[1]):
                        for jp16 in range(_P4LIM[2]):
                            jp = 16 * half + jp16
                            slot = jp16 % 2
                            wsem = gW[0] if slot == 0 else gW[2]
                            g.wait(tS, slab_free[slot])
                            g.inc(g.dma_start(
                                AR[:, A4_WV + 2048 * slot:A4_WV + 2048 * slot + 1024]
                                .rearrange("p (kc n) -> p kc n", n=128),
                                wv_src[e_, :, :, 128 * jp:128 * jp + 128]), wsem, 16)
                            g.inc(g.dma_start(
                                AR[:, A4_WV + 2048 * slot + 1024:A4_WV + 2048 * slot + 2048]
                                .rearrange("p (kc n) -> p kc n", n=128),
                                wv_src[e_, :, :, 4096 + 128 * jp:4096 + 128 * jp + 128]),
                                wsem, 16)
                            ldw = wsem.n
                            for t_ in range(4):
                                gb, vb = t_ % 2, 2 + t_ % 2
                                t.wait(wsem, ldw)
                                t.wait(yD, ld_h2)
                                t.wait(sS, bank_free_s[gb])
                                t.wait(vS, bank_free_v[vb])
                                for kc in range(NCH):
                                    t.inc(t.matmul(
                                        PS[gb][:],
                                        AR[:, A4_WV + 2048 * slot + 128 * kc:
                                           A4_WV + 2048 * slot + 128 * kc + 128],
                                        AR[:, A4_H2 + 2048 * kc + 512 * t_:
                                           A4_H2 + 2048 * kc + 512 * t_ + 512],
                                        start=(kc == 0), stop=(kc == NCH - 1)), tS, 1)
                                for kc in range(NCH):
                                    t.inc(t.matmul(
                                        PS[vb][:],
                                        AR[:, A4_WV + 2048 * slot + 1024 + 128 * kc:
                                           A4_WV + 2048 * slot + 1024 + 128 * kc + 128],
                                        AR[:, A4_H2 + 2048 * kc + 512 * t_:
                                           A4_H2 + 2048 * kc + 512 * t_ + 512],
                                        start=(kc == 0), stop=(kc == NCH - 1)), tS, 1)
                                gv_mm = tS.n
                                s.wait(tS, gv_mm)
                                s.wait(vS, bank_free_v[gb])
                                s.inc(s.activation(abuf(jp16, t_), PS[gb][:],
                                                   AF.Silu), sS, 1)
                                silu_done = sS.n
                                bank_free_s[gb] = sS.n
                                v.wait(sS, silu_done)
                                v.wait(sS, mb_done)
                                v.inc(v.tensor_mul(abuf(jp16, t_), abuf(jp16, t_),
                                                   PS[vb][:]), vS, 1)
                                vbar(v.tensor_mul(abuf(jp16, t_), abuf(jp16, t_),
                                                  AR[:, A4_MB + 512 * t_:
                                                     A4_MB + 512 * t_ + 512]))
                                bank_free_v[vb] = vS.n
                            slab_free[slot] = tS.n
                        a_done = vS.n
                        pj_free[1] = max(pj_free[1], tS.n)
                        pj_base = [A4_PJ, A4_WV]
                        pjsem = [gW[1], gD]
                        for c_ in range(NCH):
                            sl = c_ % 2
                            g.wait(tS, pj_free[sl])
                            g.inc(g.dma_start(
                                AR[:, pj_base[sl]:pj_base[sl] + 2048]
                                .rearrange("p (r n) -> p r n", n=128),
                                pj_src[e_, :, 16 * half:16 * half + 16,
                                       128 * c_:128 * c_ + 128]), pjsem[sl], 16)
                            ld_pj = pjsem[sl].n
                            for t_ in range(4):
                                bank = 4 + (t_ % 4)
                                t.wait(pjsem[sl], ld_pj)
                                t.wait(vS, a_done)
                                t.wait(vS, bank_free_v[bank])
                                for jp16 in range(16):
                                    t.inc(t.matmul(
                                        PS[bank][:],
                                        AR[:, pj_base[sl] + 128 * jp16:
                                           pj_base[sl] + 128 * jp16 + 128],
                                        abuf(jp16, t_),
                                        start=(jp16 == 0), stop=(jp16 == 15)), tS, 1)
                                mm = tS.n
                                v.wait(tS, mm)
                                v.inc(v.tensor_add(
                                    fm(ACC, c_, slice(512 * t_, 512 * t_ + 512)),
                                    fm(ACC, c_, slice(512 * t_, 512 * t_ + 512)),
                                    PS[bank][:]), vS, 1)
                                bank_free_v[bank] = vS.n
                            pj_free[sl] = tS.n
                        slab_free[0] = slab_free[1] = tS.n
                if "p4" in _DEBUG:
                    y.wait(vS, vS.n)
                    ydma(dbg["acc4"].rearrange("(c p) t -> p c t", p=128),
                         ACC[:].rearrange("p (c t) -> p c t", c=NCH))

            if "p5" in _SKIP:
                y.wait(yD, yD.n)
                y.wait(yX[0], yX[0].n)
                y.wait(yX[1], yX[1].n)
                y.wait(vS, vS.n)
                y.wait(sS, sS.n)
                y.wait(tS, tS.n)
                return
            OUTS = [ARF[:, 2048:3072], ARF[:, 3072:4096]]
            p5_s0 = sS.n
            p5_v0 = vS.n
            st5 = [0] * NTT
            ev5 = [0] * NTT
            for i in range(NTT):
                xt = XT[i % 2]
                if i < 2:
                    y.wait(sS, p5_s0)
                    y.wait(vS, p5_v0)
                else:
                    y.wait(vS, ev5[i - 2])
                ld = ydma_on(yX[i % 2], xt[:], io["x"][128 * i:128 * i + 128, :])
                if i == 0:
                    t.wait(vS, p5_v0)
                    t.wait(sS, p5_s0)
                else:
                    t.wait(vS, ev5[i - 1])
                for half, bank in [(0, PS[2]), (1, PS[3])]:
                    for k in range(4):
                        ch = 4 * half + k
                        t.inc(t.matmul(bank[:, 128 * k:128 * k + 128],
                                       fm(ACC, ch, slice(128 * i, 128 * i + 128)),
                                       IDF[:], start=(k == 0), stop=(k == 3)), tS, 1)
                tp = tS.n
                v.wait(tS, tp)
                v.wait(yX[i % 2], ld)
                if i >= 2:
                    v.wait(yS[i % 2], st5[i - 2])
                outt = OUTS[i % 2]
                v.inc(v.tensor_add(outt[:, 0:512], PS[2][:], xt[:, 0:512]), vS, 1)
                v.inc(v.tensor_add(outt[:, 512:1024], PS[3][:], xt[:, 512:1024]), vS, 1)
                ev5[i] = vS.n
                y.wait(vS, ev5[i])
                st5[i] = ydma_on(yS[i % 2], io["out"][128 * i:128 * i + 128, :], outt[:])
            y.wait(yD, yD.n)
            for s_ in (yS[0], yS[1], yX[0], yX[1], gW[0], gW[1], gW[2], gD):
                y.wait(s_, s_.n)

        with nc.Block() as block:
            @block.tensor
            def _(eng):
                plan(Plan("tensor", eng))

            @block.vector
            def _(eng):
                plan(Plan("vector", eng))

            @block.scalar
            def _(eng):
                plan(Plan("scalar", eng))

            @block.gpsimd
            def _(eng):
                plan(Plan("gpsimd", eng))

            @block.sync
            def _(eng):
                plan(Plan("sync", eng))

    return nc


def _in_maps(inputs):
    w = {
        "wqkv": np.ascontiguousarray(inputs["L0_attn_W_qkv"]),
        "wout": np.ascontiguousarray(inputs["L0_attn_W_out"]),
        "ln1w": np.ascontiguousarray(inputs["L0_ffn_ln1_w"]).reshape(C),
        "ln1b": np.ascontiguousarray(inputs["L0_ffn_ln1_b"]).reshape(C),
        "ln2w": np.ascontiguousarray(inputs["L0_ffn_ln2_w"]).reshape(C),
        "ln2b": np.ascontiguousarray(inputs["L0_ffn_ln2_b"]).reshape(C),
        "convw": np.ascontiguousarray(inputs["L0_ffn_conv_w"]).reshape(C, 3),
        "wattn": np.ascontiguousarray(inputs["L0_ffn_attn_w"]),
        "wproj": np.ascontiguousarray(inputs["L0_ffn_proj_w"]),
        "wrout": np.ascontiguousarray(inputs["L0_ffn_router_w"]),
        "wv": np.ascontiguousarray(inputs["L0_ffn_expert_wv"]),
        "wp": np.ascontiguousarray(inputs["L0_ffn_expert_proj"]),
    }
    x = np.ascontiguousarray(inputs["L0_attn_inputs"])
    return [dict(w, x=x[b]) for b in range(B)]


def kernel(**inputs):
    nc = build_nc()
    maps = _in_maps(inputs)
    res = run_bass_kernel_spmd(nc, maps, core_ids=list(range(B)))
    return np.stack([np.asarray(r["out"], dtype=np.float32) for r in res.results])


# version 07
# speedup vs baseline: 1.0208x; 1.0208x over previous
import math
from contextlib import ExitStack

import numpy as np
import ml_dtypes

import concourse.bass as bass
import concourse.mybir as mybir
from concourse.bass_utils import run_bass_kernel_spmd

F32 = mybir.dt.float32
BF16 = mybir.dt.bfloat16
AF = mybir.ActivationFunctionType
ALU = mybir.AluOpType
AX = mybir.AxisListType

B, T, C = 8, 2048, 1024
NCH = C // 128
QT = 4
TW = 512
NTT = T // 128
EPS = 1e-5

_DEBUG = []
_NTILES = NTT
_SKIP = set()
_P0CUT = "full"
_PHASES = {"p1", "p2", "p3", "p4"}
_P1CUT = "full"
_P4LIM = (4, 2, 16)



def _alibi_slopes(num_heads):
    closest_pow2 = 2 ** int(math.floor(math.log2(num_heads)))
    base = 2.0 ** (-(2.0 ** (-(math.log2(closest_pow2) - 3))))
    powers = np.arange(1, closest_pow2 + 1, dtype=np.float32)
    slopes = base ** powers
    if closest_pow2 != num_heads:
        eb = 2.0 ** (-(2.0 ** (-(math.log2(2 * closest_pow2) - 3))))
        ep = np.arange(1, 2 * (num_heads - closest_pow2) + 1, 2, dtype=np.float32)
        slopes = np.concatenate([slopes, eb ** ep])
    return slopes[:num_heads].astype(np.float32)


def _consts():
    c = {}
    c["identb"] = np.eye(128, dtype=ml_dtypes.bfloat16)
    c["identf"] = np.eye(128, dtype=np.float32)
    D = 64
    inv_freq = 1.0 / (10000.0 ** (np.arange(0, D, 2, dtype=np.float64) / D))
    freqs = np.outer(np.arange(T, dtype=np.float64), inv_freq)
    emb = np.concatenate([freqs, freqs], 1)
    cos = np.cos(emb).astype(np.float32)
    sin = np.sin(emb).astype(np.float32)
    ssin = sin.copy()
    ssin[:, :32] *= -1.0
    c["cosb"] = cos.T[np.tile(np.arange(64), 2)].astype(ml_dtypes.bfloat16)
    c["sinb"] = ssin.T[np.tile(np.arange(64), 2)].astype(ml_dtypes.bfloat16)
    tri = (np.arange(128)[:, None] <= np.arange(128)[None, :])
    c["tri01"] = tri.astype(ml_dtypes.bfloat16)
    c["masktri"] = np.where(tri, 0.0, np.float32(-1e30)).astype(np.float32)
    kidx = np.arange(T)
    c["kbias"] = np.stack([kidx // 64, kidx % 64]).astype(ml_dtypes.bfloat16)
    c["qbias"] = np.stack([-(kidx // 64), -(kidx % 64)]).astype(ml_dtypes.bfloat16)
    sel = np.zeros((2, 128), dtype=ml_dtypes.bfloat16)
    sel[0, 0:64] = 1.0
    sel[1, 64:128] = 1.0
    c["sel01"] = sel
    sl = _alibi_slopes(32)
    slb = sl.astype(ml_dtypes.bfloat16).astype(np.float64)
    qr_ = np.zeros((128, T), np.float64)
    kr_ = np.zeros((128, T), np.float64)
    for h in range(32):
        qr_[4 * h + 0] = 64.0 * slb[h]
        qr_[4 * h + 1] = slb[h]
        qr_[4 * h + 2] = -(kidx // 64)
        qr_[4 * h + 3] = -(kidx % 64)
        kr_[4 * h + 0] = kidx // 64
        kr_[4 * h + 1] = kidx % 64
        kr_[4 * h + 2] = 64.0 * slb[h]
        kr_[4 * h + 3] = slb[h]
    c["qextr"] = qr_.astype(ml_dtypes.bfloat16)
    c["kextr"] = kr_.astype(ml_dtypes.bfloat16)
    return c



class _Sem:
    def __init__(self, h):
        self.h = h
        self.n = 0


class _FakeInstr:
    def then_inc(self, *a, **k):
        return self


_FAKE = _FakeInstr()


class _Eng:
    def __init__(self, plan, name):
        self._p = plan
        self._n = name

    def wait(self, sem, val=None):
        v = sem.n if val is None else val
        if v <= 0:
            return
        r = self._p.real if self._p.active == self._n else None
        if r is not None:
            r.wait_ge(sem.h, v)

    def inc(self, instr, sem, k=1):
        sem.n += k
        if instr is not _FAKE:
            instr.then_inc(sem.h, k)
        return instr

    def __getattr__(self, op):
        if self._p.active == self._n:
            return getattr(self._p.real, op)
        return lambda *a, **k: _FAKE


class Plan:
    def __init__(self, active, real):
        self.active = active
        self.real = real
        self.t = _Eng(self, "tensor")
        self.v = _Eng(self, "vector")
        self.s = _Eng(self, "scalar")
        self.g = _Eng(self, "gpsimd")
        self.y = _Eng(self, "sync")



def build_nc():
    nc = bass.Bass("TRN2", target_bir_lowering=False, detect_race_conditions=False)

    io = {}
    io["x"] = nc.dram_tensor("x", [T, C], F32, kind="ExternalInput").ap()
    io["wqkv"] = nc.dram_tensor("wqkv", [C, 4096], F32, kind="ExternalInput").ap()
    io["wout"] = nc.dram_tensor("wout", [2048, C], F32, kind="ExternalInput").ap()
    io["ln1w"] = nc.dram_tensor("ln1w", [C], F32, kind="ExternalInput").ap()
    io["ln1b"] = nc.dram_tensor("ln1b", [C], F32, kind="ExternalInput").ap()
    io["ln2w"] = nc.dram_tensor("ln2w", [C], F32, kind="ExternalInput").ap()
    io["ln2b"] = nc.dram_tensor("ln2b", [C], F32, kind="ExternalInput").ap()
    io["convw"] = nc.dram_tensor("convw", [C, 3], F32, kind="ExternalInput").ap()
    io["wattn"] = nc.dram_tensor("wattn", [C, 3072], F32, kind="ExternalInput").ap()
    io["wproj"] = nc.dram_tensor("wproj", [C, C], F32, kind="ExternalInput").ap()
    io["wrout"] = nc.dram_tensor("wrout", [C, 4], F32, kind="ExternalInput").ap()
    io["wv"] = nc.dram_tensor("wv", [4, C, 8192], F32, kind="ExternalInput").ap()
    io["wp"] = nc.dram_tensor("wp", [4, 4096, C], F32, kind="ExternalInput").ap()
    io["out"] = nc.dram_tensor("out", [T, C], F32, kind="ExternalOutput").ap()

    cst = {k: nc.inline_tensor(np.ascontiguousarray(v), "c_" + k).ap()
           for k, v in _consts().items()}

    dr = {
        "xnbf": nc.dram_tensor("d_xnbf", [C, T], BF16).ap(),
        "h2": nc.dram_tensor("d_h2", [C, T], BF16).ap(),
        "apair": nc.dram_tensor("d_apair", [16, 128, T], BF16).ap(),
        "bpair": nc.dram_tensor("d_bpair", [8, 128, T], BF16).ap(),
        "masks": nc.dram_tensor("d_masks", [4, T], BF16).ap(),
    }

    dbg = {}
    def _dbg(name, shape, dtype=F32):
        dbg[name] = nc.dram_tensor("dbg_" + name, shape, dtype, kind="ExternalOutput").ap()
    if "p0" in _DEBUG:
        _dbg("xn_fm", [C, T])
        _dbg("h1_fm", [C, T], BF16)
    if "p1" in _DEBUG:
        _dbg("hc_fm", [C, T], BF16)
        _dbg("acc1", [C, T])
    if "p2" in _DEBUG:
        _dbg("h2", [C, T], BF16)
        _dbg("masks", [4, T], BF16)
    if "p3" in _DEBUG:
        _dbg("acc3", [C, T])
    if "p4" in _DEBUG:
        _dbg("acc4", [C, T])

    slopes = _alibi_slopes(32)

    with ExitStack() as ctx:
        def sb(name, shape, dtype):
            return ctx.enter_context(nc.sbuf_tensor(name, shape, dtype))

        def psum(name, shape, dtype=F32):
            return ctx.enter_context(nc.psum_tensor(name, shape, dtype))

        sems = {n: _Sem(ctx.enter_context(nc.semaphore(n)))
                for n in ["tS", "vS", "sS", "gD", "yD", "g2",
                          "yX0", "yX1", "yS0", "yS1", "gW0", "gW1", "gW2"]}

        ACC = sb("ACC", [128, NCH * T], F32)
        XNBF = sb("XNBF", [128, NCH * T], BF16)
        HC = sb("HC", [128, NCH * T], BF16)
        AR = sb("AR", [128, 24 * 1024], BF16)
        ARF = sb("ARF", [128, 4096], F32)
        IDB = sb("IDB", [128, 128], BF16)
        IDF = sb("IDF", [128, 128], F32)
        TRI = sb("TRI", [128, 128], BF16)
        MTRI = sb("MTRI", [128, 128], F32)
        SELA = sb("SELA", [1, 128], BF16)
        SELB = sb("SELB", [1, 128], BF16)
        ONESB = sb("ONESB", [128, 1], BF16)
        ONESF = sb("ONESF", [128, 1], F32)
        ONE1F = sb("ONE1F", [1, 128], F32)
        ONE1B = sb("ONE1B", [1, 128], BF16)
        LN1WC = sb("LN1WC", [128, NCH], F32)
        LN1BC = sb("LN1BC", [128, NCH], F32)
        LN2WC = sb("LN2WC", [128, NCH], F32)
        LN2BC = sb("LN2BC", [128, NCH], F32)
        CONVW = sb("CONVW", [128, 3 * NCH], F32)
        ROUTW = sb("ROUTW", [128, 4 * NCH], F32)
        MROW = sb("MROW", [1, T], BF16)
        EPSC = sb("EPSC", [128, 1], F32)
        STC = sb("STC", [128, 16], F32)
        SQD = sb("SQD", [128, C], F32)

        PS = [psum(f"PS{i}", [128, TW], F32) for i in range(8)]
        PD = PS[7]

        def fm(tile, ch, cols):
            return tile[:, 2048 * ch + cols.start: 2048 * ch + cols.stop]

        def plan(p: Plan):
            for s_ in sems.values():
                s_.n = 0
            t, v, s, g, y = p.t, p.v, p.s, p.g, p.y
            tS, vS, sS, gD, yD, g2 = (sems[n] for n in ["tS", "vS", "sS", "gD", "yD", "g2"])
            yX = [sems["yX0"], sems["yX1"]]
            yS = [sems["yS0"], sems["yS1"]]
            gW = [sems["gW0"], sems["gW1"], sems["gW2"]]

            def ydma_on(sem, dst, src):
                y.inc(y.dma_start(dst, src), sem, 16)
                return sem.n

            def ydma(dst, src):
                y.inc(y.dma_start(dst, src), yD, 16)
                return yD.n

            def gdma(dst, src):
                g.inc(g.dma_start(dst, src), gD, 16)
                return gD.n

            def ydma_nc(dst, src):
                y.inc(y.dma_start(dst, src, allow_slow_non_contiguous=True), yD, 16)
                return yD.n

            def vbar(instr):
                v.inc(instr, vS, 1)
                v.wait(vS, vS.n)

            def sbar(instr):
                s.inc(instr, sS, 1)
                s.wait(sS, sS.n)

            for dst, key in [(IDB, "identb"), (IDF, "identf"), (TRI, "tri01"),
                             (MTRI, "masktri")]:
                ydma(dst[:], cst[key])
            ydma(SELA[:], cst["sel01"][0:1, :])
            ydma(SELB[:], cst["sel01"][1:2, :])
            ydma_nc(LN1WC[:], io["ln1w"].rearrange("(c p) -> p c", p=128))
            ydma_nc(LN1BC[:], io["ln1b"].rearrange("(c p) -> p c", p=128))
            ydma_nc(LN2WC[:], io["ln2w"].rearrange("(c p) -> p c", p=128))
            ydma_nc(LN2BC[:], io["ln2b"].rearrange("(c p) -> p c", p=128))
            ydma_nc(CONVW[:].rearrange("p (c k) -> p c k", k=3),
                 io["convw"].rearrange("(c p) k -> p c k", p=128))
            ydma_nc(ROUTW[:].rearrange("p (c e) -> p c e", e=4),
                 io["wrout"].rearrange("(c p) e -> p c e", p=128))
            ld_consts = yD.n
            slab_free = [0, 0]
            bank_free_s = [0] * 8
            bank_free_v = [0] * 8
            v.memset(ONESB[:], 1.0)
            v.memset(ONESF[:], 1.0)
            v.memset(ONE1F[:], 1.0)
            v.memset(ONE1B[:], 1.0)
            v.inc(v.memset(EPSC[:], EPS), vS, 1)
            v_consts = vS.n
            t.wait(yD, ld_consts)
            t.wait(vS, v_consts)
            v.wait(yD, ld_consts)
            s.wait(yD, ld_consts)
            XT = [ARF[:, 0:1024], ARF[:, 1024:2048]]
            XN = ARF[:, 2048:3072]
            H1B = AR[:, 0:1024]
            H1FM_BASE = 8192

            def h1fm(ch, cols):
                o = H1FM_BASE + 2048 * ch
                return AR[:, o + cols.start: o + cols.stop]

            ld_x = [0] * (NTT + 1)
            ld_x[0] = ydma_on(yX[0], XT[0][:], io["x"][0:128, :])
            done_sx = [0] * NTT
            done_vx = [0] * NTT
            done_t = [0] * NTT
            done_evv = [0] * NTT
            done_evs = [0] * NTT

            for i in range(_NTILES):
                xt = XT[i % 2]
                if i + 1 < _NTILES:
                    if i >= 1:
                        y.wait(sS, done_sx[i - 1])
                        y.wait(vS, done_vx[i - 1])
                    ld_x[i + 1] = ydma_on(yX[(i + 1) % 2], XT[(i + 1) % 2][:],
                                          io["x"][128 * (i + 1):128 * (i + 2), :])
                v.wait(yX[i % 2], ld_x[i])
                vbar(v.reduce_sum(STC[:, 0:1], xt[:], axis=AX.X))
                done_vx[i] = vS.n
                s.wait(yX[i % 2], ld_x[i])
                if i >= 1:
                    s.wait(tS, done_t[i - 1])
                s.inc(s.activation(SQD[:], xt[:], AF.Square), sS, 1)
                v.wait(sS, sS.n)
                vbar(v.reduce_sum(STC[:, 1:2], SQD[:], axis=AX.X))
                vbar(v.tensor_scalar(STC[:, 2:3], STC[:, 0:1], 1.0 / C, None, ALU.mult))
                vbar(v.tensor_scalar(STC[:, 3:4], STC[:, 1:2], 1.0 / C, None, ALU.mult))
                vbar(v.tensor_mul(STC[:, 4:5], STC[:, 2:3], STC[:, 2:3]))
                v.inc(v.tensor_sub(STC[:, 3:4], STC[:, 3:4], STC[:, 4:5]), vS, 1)
                s.wait(vS, vS.n)
                s.inc(s.activation(STC[:, 4:5], STC[:, 3:4], AF.Sqrt, bias=EPSC[:]), sS, 1)
                v.wait(sS, sS.n)
                vbar(v.reciprocal(STC[:, 5:6], STC[:, 4:5]))
                vbar(v.tensor_mul(STC[:, 6:7], STC[:, 2:3], STC[:, 5:6]))
                v.inc(v.tensor_scalar(STC[:, 6:7], STC[:, 6:7], -1.0, None, ALU.mult), vS, 1)
                s.wait(vS, vS.n)
                if _P0CUT == "stats":
                    done_sx[i] = sS.n
                    done_t[i] = tS.n
                    done_evv[i] = vS.n
                    done_evs[i] = sS.n
                    continue
                sbar(s.activation(XN[:], xt[:], AF.Identity,
                                  scale=STC[:, 5:6], bias=STC[:, 6:7]))
                v.wait(sS, sS.n)
                vbar(v.reduce_sum(STC[:, 8:9], XN[:], axis=AX.X))
                sbar(s.activation(SQD[:], XN[:], AF.Square))
                done_sx[i] = sS.n
                v.wait(sS, sS.n)
                vbar(v.reduce_sum(STC[:, 9:10], SQD[:], axis=AX.X))
                vbar(v.tensor_scalar(STC[:, 10:11], STC[:, 8:9], 1.0 / C, None, ALU.mult))
                vbar(v.tensor_scalar(STC[:, 11:12], STC[:, 9:10], 1.0 / C, None, ALU.mult))
                vbar(v.tensor_mul(STC[:, 12:13], STC[:, 10:11], STC[:, 10:11]))
                v.inc(v.tensor_sub(STC[:, 11:12], STC[:, 11:12], STC[:, 12:13]), vS, 1)
                s.wait(vS, vS.n)
                s.inc(s.activation(STC[:, 12:13], STC[:, 11:12], AF.Sqrt, bias=EPSC[:]), sS, 1)
                v.wait(sS, sS.n)
                vbar(v.reciprocal(STC[:, 13:14], STC[:, 12:13]))
                vbar(v.tensor_mul(STC[:, 14:15], STC[:, 10:11], STC[:, 13:14]))
                v.inc(v.tensor_scalar(STC[:, 14:15], STC[:, 14:15], -1.0, None, ALU.mult), vS, 1)
                if _P0CUT == "xn":
                    done_sx[i] = sS.n
                    done_t[i] = tS.n
                    done_evv[i] = vS.n
                    done_evs[i] = sS.n
                    continue
                s.wait(vS, vS.n)
                s.inc(s.activation(H1B[:], XN[:], AF.Identity,
                                   scale=STC[:, 13:14], bias=STC[:, 14:15]), sS, 1)
                ready = sS.n
                if _P0CUT == "h1":
                    done_sx[i] = sS.n
                    done_t[i] = tS.n
                    done_evv[i] = vS.n
                    done_evs[i] = sS.n
                    continue
                t.wait(sS, ready)
                if i >= 1:
                    t.wait(vS, done_evv[i - 1])
                    t.wait(sS, done_evs[i - 1])
                for half, bank in [(0, PS[0]), (1, PS[1])]:
                    for k in range(4):
                        ch = 4 * half + k
                        t.inc(t.matmul(bank[:, 128 * k:128 * k + 128],
                                       XN[:, 128 * ch:128 * ch + 128], IDF[:],
                                       start=(k == 0), stop=(k == 3)), tS, 1)
                for half, bank in [(0, PS[2]), (1, PS[3])]:
                    for k in range(4):
                        ch = 4 * half + k
                        t.inc(t.matmul(bank[:, 128 * k:128 * k + 128],
                                       H1B[:, 128 * ch:128 * ch + 128], IDB[:],
                                       start=(k == 0), stop=(k == 3)), tS, 1)
                done_t[i] = tS.n
                if _P0CUT == "tp":
                    done_evv[i] = vS.n
                    done_evs[i] = sS.n
                    continue
                cw = slice(128 * i, 128 * i + 128)
                v.wait(tS, done_t[i])
                for half, bank in [(0, PS[0]), (1, PS[1])]:
                    for k in range(4):
                        ch = 4 * half + k
                        v.inc(v.tensor_copy(fm(ACC, ch, cw),
                                            bank[:, 128 * k:128 * k + 128]), vS, 1)
                        v.inc(v.tensor_copy(fm(XNBF, ch, cw),
                                            bank[:, 128 * k:128 * k + 128]), vS, 1)
                for half, bank in [(0, PS[2]), (1, PS[3])]:
                    for k in range(4):
                        ch = 4 * half + k
                        v.inc(v.tensor_scalar(h1fm(ch, cw), bank[:, 128 * k:128 * k + 128],
                                              LN1WC[:, ch:ch + 1], LN1BC[:, ch:ch + 1],
                                              ALU.mult, ALU.add), vS, 1)
                done_evv[i] = vS.n
                done_evs[i] = sS.n

            if "p0" in _DEBUG:
                y.wait(vS, vS.n)
                y.wait(sS, sS.n)
                ydma(dbg["xn_fm"].rearrange("(c p) t -> p c t", p=128),
                     ACC[:].rearrange("p (c t) -> p c t", c=NCH))
                ydma(dbg["h1_fm"].rearrange("(c p) t -> p c t", p=128),
                     AR[:, H1FM_BASE:H1FM_BASE + NCH * 2048].rearrange(
                         "p (c t) -> p c t", c=NCH))

            if "p1" in _PHASES:
                v.wait(vS, vS.n)
                for ch in range(NCH):
                    w0 = CONVW[:, 3 * ch + 0:3 * ch + 1]
                    w1 = CONVW[:, 3 * ch + 1:3 * ch + 2]
                    w2 = CONVW[:, 3 * ch + 2:3 * ch + 3]
                    h1c = lambda a, b_: AR[:, H1FM_BASE + 2048 * ch + a:
                                           H1FM_BASE + 2048 * ch + b_]
                    hcc = lambda a, b_: HC[:, 2048 * ch + a:2048 * ch + b_]
                    vbar(v.tensor_scalar(hcc(0, T), h1c(0, T), w2, None, ALU.mult))
                    vbar(v.scalar_tensor_tensor(hcc(1, T), h1c(0, T - 1), w1,
                                                hcc(1, T), ALU.mult, ALU.add))
                    vbar(v.scalar_tensor_tensor(hcc(2, T), h1c(0, T - 2), w0,
                                                hcc(2, T), ALU.mult, ALU.add))
                conv_done = vS.n
                if "p1" in _DEBUG:
                    y.wait(vS, conv_done)
                    ydma(dbg["hc_fm"].rearrange("(c p) t -> p c t", p=128),
                         HC[:].rearrange("p (c t) -> p c t", c=NCH))

                A_SLB = [0, 1024]
                A_QS, A_KS, A_VS = 2048, 4096, 6144
                A_COS, A_SIN = 8192, 10240
                A_RT, A_QR, A_KR, A_VT = 12288, 14336, 16384, 18432
                A_PB = [[20512 + 512 * k for k in range(4)],
                        [12288 + 512 * k for k in range(4)]]
                A_RC = 2048
                A_OUT = [3072, 3584]
                y.wait(vS, conv_done)
                ld_cos = ydma(AR[:, A_COS:A_COS + T], cst["cosb"])
                ld_cos = ydma(AR[:, A_SIN:A_SIN + T], cst["sinb"])

                slab_free[0] = slab_free[1] = tS.n
                spill = [0, 0]
                pending = [None]

                def qkv_chunk(oc, dst_base, slot, scale=1.0):
                    src = io["wattn"].rearrange("(kc p) n -> p kc n", p=128)
                    wsem = gW[0] if slot == 0 else gW[2]
                    g.wait(tS, slab_free[slot])
                    g.inc(g.dma_start(AR[:, A_SLB[slot]:A_SLB[slot] + 1024]
                                      .rearrange("p (kc n) -> p kc n", n=128),
                                      src[:, :, 128 * oc:128 * oc + 128]), wsem, 16)
                    ldw = wsem.n
                    for qt in range(QT):
                        bank = qt % 4
                        qtw = slice(512 * qt, 512 * qt + 512)
                        t.wait(wsem, ldw)
                        t.wait(vS, conv_done)
                        t.wait(sS, bank_free_s[bank])
                        t.wait(vS, bank_free_v[bank])
                        for kc in range(NCH):
                            t.inc(t.matmul(
                                PS[bank][:],
                                AR[:, A_SLB[slot] + 128 * kc:A_SLB[slot] + 128 * kc + 128],
                                fm(HC, kc, qtw),
                                start=(kc == 0), stop=(kc == NCH - 1)), tS, 1)
                        mm = tS.n
                        s.wait(tS, mm)
                        if scale == 1.0:
                            s.inc(s.copy(AR[:, dst_base + 512 * qt:dst_base + 512 * qt + 512],
                                         PS[bank][:]), sS, 1)
                        else:
                            s.inc(s.activation(
                                AR[:, dst_base + 512 * qt:dst_base + 512 * qt + 512],
                                PS[bank][:], AF.Copy, scale=scale), sS, 1)
                        bank_free_s[bank] = sS.n
                    slab_free[slot] = tS.n

                def rope(src_base, dst_base):
                    g.wait(sS, sS.n)
                    g.wait(tS, tS.n)
                    for (dp, sp_) in [(0, 32), (32, 0), (64, 96), (96, 64)]:
                        g.inc(g.dma_start(AR[dp:dp + 32, A_RT:A_RT + T],
                                          AR[sp_:sp_ + 32, src_base:src_base + T]),
                              gW[1], 16)
                    rot_ld = gW[1].n
                    v.wait(gW[1], rot_ld)
                    v.wait(sS, sS.n)
                    vbar(v.tensor_mul(AR[:, dst_base:dst_base + T],
                                      AR[:, src_base:src_base + T],
                                      AR[:, A_COS:A_COS + T]))
                    vbar(v.tensor_mul(AR[:, A_RT:A_RT + T],
                                      AR[:, A_RT:A_RT + T],
                                      AR[:, A_SIN:A_SIN + T]))
                    v.inc(v.tensor_add(AR[:, dst_base:dst_base + T],
                                       AR[:, dst_base:dst_base + T],
                                       AR[:, A_RT:A_RT + T]), vS, 1)

                npairs = 0 if _P1CUT == "conv" else (1 if _P1CUT in (
                    "qkv1", "rope1", "vt1", "attn1") else 8)
                for p_ in range(npairs):
                    qkv_chunk(p_, A_QS, 0, scale=0.125)
                    qkv_chunk(8 + p_, A_KS, 1)
                    qkv_chunk(16 + p_, A_VS, 0)
                    if _P1CUT == "qkv1":
                        continue
                    v.wait(yD, ld_cos)
                    rope(A_QS, A_QR)
                    rope(A_KS, A_KR)
                    rope_done = vS.n
                    if _P1CUT == "rope1":
                        continue
                    v.wait(sS, sS.n)
                    v.inc(v.memset(AR[:, A_VT + 64:A_VT + 64 + 1]
                                   .rearrange("p (a b) -> p a b", a=1) if False else
                                   AR[:, A_VT:A_VT + 2080]
                                   .rearrange("p (kt c) -> p kt c", c=130)[:, :, 64:65],
                                   1.0), vS, 1)
                    v.inc(v.memset(AR[:, A_VT:A_VT + 2080]
                                   .rearrange("p (kt c) -> p kt c", c=130)[:, :, 129:130],
                                   1.0), vS, 1)
                    ones_done = vS.n
                    for half in range(2):
                        bank = 4 + half
                        t.wait(sS, sS.n)
                        t.wait(vS, bank_free_v[bank])
                        for k in range(8):
                            kt = 8 * half + k
                            if k % 4 == 0:
                                t.wait(sS, bank_free_s[bank])
                            t.inc(t.matmul(PS[bank][:, 128 * (k % 4):128 * (k % 4) + 128],
                                           AR[:, A_VS + 128 * kt:A_VS + 128 * kt + 128],
                                           IDB[:],
                                           start=(k % 4 == 0), stop=(k % 4 == 3)), tS, 1)
                            if k % 4 == 3:
                                mm = tS.n
                                s.wait(tS, mm)
                                kt0 = kt - 3
                                for hh in range(2):
                                    s.inc(s.copy(
                                        AR[:, A_VT + 130 * kt0:A_VT + 130 * (kt0 + 4)]
                                        .rearrange("p (kt c) -> p kt c", c=130)
                                        [:, :, 65 * hh:65 * hh + 64],
                                        PS[bank][:].rearrange("p (kt c) -> p kt c", c=128)
                                        [:, :, 64 * hh:64 * hh + 64]), sS, 1)
                                bank_free_s[bank] = sS.n
                    vt_done = sS.n
                    if _P1CUT == "vt1":
                        continue
                    SCB = [[0, 1], [6, 7]]
                    for qt in range(QT):
                        qtw = slice(512 * qt, 512 * qt + 512)
                        nkt = 4 * qt + 4
                        pb_free = [[0] * 4, [0] * 4]
                        pvb = [[2, 4][qt % 2], [3, 5][qt % 2]]
                        exp_done_at = [[0] * 4, [0] * 4]
                        tri_done_at = [[None] * 4, [None] * 4]

                        def p1_pv_wave(kt):
                            off = 128 * (kt - 4 * qt) if kt >= 4 * qt else 0
                            for hh in range(2):
                                pv_bank = pvb[hh]
                                pb = A_PB[hh][kt % 4]
                                t.wait(sS, exp_done_at[hh][kt % 4])
                                tri = tri_done_at[hh][kt % 4]
                                if tri:
                                    t.wait(vS, tri)
                                if kt == 0:
                                    t.wait(sS, bank_free_s[pv_bank])
                                    t.wait(vS, bank_free_v[pv_bank])
                                t.inc(t.matmul(
                                    PS[pv_bank][0:65, slice(off, 512)],
                                    AR[:, A_VT + 130 * kt + 65 * hh:
                                       A_VT + 130 * kt + 65 * hh + 65],
                                    AR[:, pb + off:pb + 512],
                                    start=(kt == 0), stop=(kt == nkt - 1)), tS, 1)
                                pb_free[hh][kt % 4] = tS.n

                        for kt in range(nkt):
                            off = 128 * (kt - 4 * qt) if kt >= 4 * qt else 0
                            w_ = slice(off, 512)
                            for hh in range(2):
                                sc_bank = SCB[hh][kt % 2]
                                t.wait(sS, bank_free_s[sc_bank])
                                t.wait(vS, bank_free_v[sc_bank])
                                t.wait(sS, vt_done)
                                t.wait(vS, max(rope_done, ones_done))
                                t.inc(t.matmul(
                                    PS[sc_bank][:, w_],
                                    AR[64 * hh:64 * hh + 64,
                                       A_KR + 128 * kt:A_KR + 128 * kt + 128],
                                    AR[64 * hh:64 * hh + 64,
                                       A_QR + 512 * qt + off:A_QR + 512 * qt + 512],
                                    start=True, stop=True), tS, 1)
                                sc_mm = tS.n
                                s.wait(tS, sc_mm)
                                s.wait(tS, pb_free[hh][kt % 4])
                                pb = A_PB[hh][kt % 4]
                                s.inc(s.activation(AR[:, pb + off:pb + 512],
                                                   PS[sc_bank][:, w_], AF.Exp), sS, 1)
                                exp_done_at[hh][kt % 4] = sS.n
                                bank_free_s[sc_bank] = sS.n
                                if kt >= 4 * qt:
                                    v.wait(sS, sS.n)
                                    v.inc(v.tensor_mul(AR[:, pb + off:pb + off + 128],
                                                       AR[:, pb + off:pb + off + 128],
                                                       TRI[:]), vS, 1)
                                    tri_done_at[hh][kt % 4] = vS.n
                                else:
                                    tri_done_at[hh][kt % 4] = None
                            if kt >= 1:
                                p1_pv_wave(kt - 1)
                        p1_pv_wave(nkt - 1)
                        pv_all = tS.n

                        def mk_chain1(p_c, qt_c, pv_all_c, bA, bB):
                            def chain():
                                v.wait(tS, pv_all_c)
                                v.inc(v.reciprocal(ARF[0:1, 512:1024],
                                                   PS[bA][64:65, :]), vS, 1)
                                vbar(v.reciprocal(ARF[32:33, 512:1024],
                                                  PS[bB][64:65, :]))
                                v.inc(v.tensor_copy(AR[0:1, A_RC:A_RC + 512],
                                                    ARF[0:1, 512:1024]), vS, 1)
                                v.inc(v.tensor_copy(AR[0:1, A_RC + 512:A_RC + 1024],
                                                    ARF[32:33, 512:1024]), vS, 1)
                                rc_vv = vS.n
                                outb = A_OUT[qt_c % 2]
                                v.wait(yS[qt_c % 2], spill[qt_c % 2])
                                v.inc(v.tensor_copy(AR[0:64, outb:outb + 512],
                                                    PS[bA][0:64, :]), vS, 1)
                                v.inc(v.tensor_copy(AR[64:128, outb:outb + 512],
                                                    PS[bB][0:64, :]), vS, 1)
                                ev_done = vS.n
                                bank_free_s[bB] = sS.n
                                bank_free_v[bB] = ev_done
                                t.wait(vS, max(ev_done, rc_vv))
                                t.inc(t.matmul(PS[bA][:], SELA[:],
                                               AR[0:1, A_RC:A_RC + 512],
                                               start=True, stop=False), tS, 1)
                                t.inc(t.matmul(PS[bA][:], SELB[:],
                                               AR[0:1, A_RC + 512:A_RC + 1024],
                                               start=False, stop=True), tS, 1)
                                bc_mm = tS.n
                                v.wait(tS, bc_mm)
                                v.inc(v.tensor_mul(AR[:, outb:outb + 512],
                                                   AR[:, outb:outb + 512],
                                                   PS[bA][:]), vS, 1)
                                div_done = vS.n
                                bank_free_s[bA] = sS.n
                                bank_free_v[bA] = div_done
                                y.wait(vS, div_done)
                                spill[qt_c % 2] = ydma_on(
                                    yS[qt_c % 2],
                                    dr["bpair"][p_c, :, 512 * qt_c:512 * qt_c + 512],
                                    AR[:, outb:outb + 512])
                            return chain

                        new_chain = mk_chain1(p_, qt, pv_all, pvb[0], pvb[1])
                        if pending[0] is not None:
                            prev = pending[0]
                            pending[0] = new_chain
                            prev()
                        else:
                            pending[0] = new_chain
                    if pending[0] is not None:
                        pending[0]()
                        pending[0] = None
                    s.wait(vS, vS.n)
                    s.wait(tS, tS.n)
                    s.wait(yS[0], spill[0])
                    s.wait(yS[1], spill[1])

                if pending[0] is not None:
                    pending[0]()
                    pending[0] = None
                if _P1CUT != "full":
                    return
                A_PJ, A_BP = 0, 8192
                g.wait(tS, tS.n)
                g.wait(vS, vS.n)
                ld_pj = gdma(AR[:, A_PJ:A_PJ + 8192]
                             .rearrange("p (r n) -> p r n", n=1024),
                             io["wproj"].rearrange("(r p) n -> p r n", p=128))
                for qt in range(QT):
                    qtw = slice(512 * qt, 512 * qt + 512)
                    y.wait(tS, tS.n)
                    ld_bp = 0
                    for p_ in range(8):
                        ld_bp = ydma(AR[:, A_BP + 512 * p_:A_BP + 512 * p_ + 512],
                                     dr["bpair"][p_, :, 512 * qt:512 * qt + 512])
                    for cg in range(2):
                        for cc in range(4):
                            c_ = 4 * cg + cc
                            bank = cc
                            t.wait(gD, ld_pj)
                            t.wait(yD, ld_bp)
                            t.wait(vS, bank_free_v[bank])
                            t.wait(sS, bank_free_s[bank])
                            for p_ in range(8):
                                t.inc(t.matmul(
                                    PS[bank][:],
                                    AR[:, A_PJ + 1024 * p_ + 128 * c_:
                                       A_PJ + 1024 * p_ + 128 * c_ + 128],
                                    AR[:, A_BP + 512 * p_:A_BP + 512 * p_ + 512],
                                    start=(p_ == 0), stop=(p_ == 7)), tS, 1)
                            mm = tS.n
                            v.wait(tS, mm)
                            v.inc(v.tensor_add(fm(ACC, c_, qtw), fm(ACC, c_, qtw),
                                               PS[bank][:]), vS, 1)
                            bank_free_v[bank] = vS.n

                if "p1" in _DEBUG:
                    y.wait(vS, vS.n)
                    ydma(dbg["acc1"].rearrange("(c p) t -> p c t", p=128),
                         ACC[:].rearrange("p (c t) -> p c t", c=NCH))


            if "p2" in _PHASES:
                A_H2F = [0, 512]
                A_XSQ = [1024, 1536]
                A_RW = 2048
                A_LG = 2048
                h2st_free = [0, 0]
                p2_t0 = tS.n
                v.wait(vS, vS.n)
                for qt in range(QT):
                    qtw = slice(512 * qt, 512 * qt + 512)
                    for bk in (0, 1, 2, 3, 4):
                        t.wait(sS, bank_free_s[bk])
                        t.wait(vS, bank_free_v[bk])
                    for c_ in range(NCH):
                        t.inc(t.matmul(PS[0][0:1, :], ONESF[:], fm(ACC, c_, qtw),
                                       start=(c_ == 0), stop=(c_ == NCH - 1)), tS, 1)
                    for c_ in range(NCH):
                        xq = ARF[:, A_XSQ[c_ % 2]:A_XSQ[c_ % 2] + 512]
                        if c_ >= 2:
                            v.wait(tS, tS.n)
                        v.inc(v.tensor_mul(xq, fm(ACC, c_, qtw), fm(ACC, c_, qtw)), vS, 1)
                        t.wait(vS, vS.n)
                        t.inc(t.matmul(PS[1][0:1, :], ONESF[:], xq,
                                       start=(c_ == 0), stop=(c_ == NCH - 1)), tS, 1)
                    stat_mm = tS.n
                    R = lambda j_: ARF[0:1, 512 * j_:512 * j_ + 512]
                    s.wait(tS, stat_mm)
                    s.inc(s.copy(R(0), PS[0][0:1, :]), sS, 1)
                    sbar(s.copy(R(1), PS[1][0:1, :]))
                    bank_free_s[0] = sS.n
                    bank_free_s[1] = sS.n
                    v.wait(sS, sS.n)
                    vbar(v.tensor_scalar(R(2), R(0), 1.0 / C, None, ALU.mult))
                    vbar(v.tensor_scalar(R(3), R(1), 1.0 / C, None, ALU.mult))
                    vbar(v.tensor_mul(R(4), R(2), R(2)))
                    v.inc(v.tensor_sub(R(3), R(3), R(4)), vS, 1)
                    s.wait(vS, vS.n)
                    sbar(s.activation(R(4), R(3), AF.Sqrt, bias=EPSC[0:1, :]))
                    v.wait(sS, sS.n)
                    vbar(v.reciprocal(R(5), R(4)))
                    vbar(v.tensor_mul(R(6), R(2), R(5)))
                    v.inc(v.tensor_scalar(R(6), R(6), -1.0, None, ALU.mult), vS, 1)
                    rb_done = vS.n
                    t.wait(vS, rb_done)
                    t.inc(t.matmul(PS[2][:], ONE1F[:], R(5),
                                   start=True, stop=True), tS, 1)
                    t.inc(t.matmul(PS[3][:], ONE1F[:], R(6),
                                   start=True, stop=True), tS, 1)
                    bc_mm = tS.n
                    for c_ in range(NCH):
                        hb = ARF[:, A_H2F[c_ % 2]:A_H2F[c_ % 2] + 512]
                        v.wait(tS, tS.n)
                        v.wait(tS, bc_mm)
                        v.inc(v.tensor_mul(hb, fm(ACC, c_, qtw), PS[2][:]), vS, 1)
                        vbar(v.tensor_add(hb, hb, PS[3][:]))
                        vbar(v.tensor_scalar(hb, hb, LN2WC[:, c_:c_ + 1],
                                             LN2BC[:, c_:c_ + 1], ALU.mult, ALU.add))
                        h2f_done = vS.n
                        t.wait(vS, h2f_done)
                        t.inc(t.matmul(PS[4][0:4, :], ROUTW[:, 4 * c_:4 * c_ + 4], hb,
                                       start=(c_ == 0), stop=(c_ == NCH - 1)), tS, 1)
                        st_slot = c_ % 2
                        s.wait(vS, h2f_done)
                        s.wait(yS[st_slot], h2st_free[st_slot])
                        s.inc(s.copy(AR[:, 512 * st_slot:512 * st_slot + 512], hb), sS, 1)
                        h2ev = sS.n
                        y.wait(sS, h2ev)
                        h2st_free[st_slot] = ydma_on(
                            yS[st_slot], dr["h2"][128 * c_:128 * c_ + 128, qtw],
                            AR[:, 512 * st_slot:512 * st_slot + 512])
                        if "p2" in _DEBUG:
                            h2st_free[st_slot] = ydma_on(
                                yS[st_slot], dbg["h2"][128 * c_:128 * c_ + 128, qtw],
                                AR[:, 512 * st_slot:512 * st_slot + 512])
                    rt_mm = tS.n
                    s.wait(tS, rt_mm)
                    s.wait(vS, vS.n)
                    sbar(s.copy(ARF[0:4, 0:512], PS[4][0:4, :]))
                    lg_ev = sS.n
                    bank_free_s[4] = sS.n
                    g.wait(sS, lg_ev)
                    for e_ in range(1, 4):
                        g.inc(g.dma_start(R(4 + e_ - 1), ARF[e_:e_ + 1, 0:512]),
                              gW[1], 16)
                    lg_dma = gW[1].n
                    L_ = [R(0), R(4), R(5), R(6)]
                    v.wait(gW[1], lg_dma)
                    vbar(v.tensor_max(R(1), L_[0], L_[1]))
                    vbar(v.tensor_max(R(2), L_[2], L_[3]))
                    vbar(v.tensor_max(R(1), R(1), R(2)))
                    for Le in L_:
                        vbar(v.tensor_tensor(Le, Le, R(1), ALU.is_ge))
                    vbar(v.tensor_scalar(R(2), L_[0], -1.0, 1.0, ALU.mult, ALU.add))
                    vbar(v.tensor_scalar(R(3), L_[1], -1.0, 1.0, ALU.mult, ALU.add))
                    vbar(v.tensor_mul(L_[1], L_[1], R(2)))
                    vbar(v.tensor_mul(R(2), R(2), R(3)))
                    vbar(v.tensor_scalar(R(3), L_[2], -1.0, 1.0, ALU.mult, ALU.add))
                    vbar(v.tensor_mul(L_[2], L_[2], R(2)))
                    vbar(v.tensor_mul(R(2), R(2), R(3)))
                    vbar(v.tensor_mul(L_[3], L_[3], R(2)))
                    mrow_sp = 0
                    for e_, src_ in [(0, L_[0]), (1, L_[1]), (2, L_[2]), (3, L_[3])]:
                        v.wait(yS[0], mrow_sp)
                        v.inc(v.tensor_copy(MROW[0:1, 0:512], src_), vS, 1)
                        mv = vS.n
                        y.wait(vS, mv)
                        mrow_sp = ydma_on(yS[0], dr["masks"][e_:e_ + 1, 512 * qt:512 * qt + 512],
                                          MROW[0:1, 0:512])
                        if "p2" in _DEBUG:
                            mrow_sp = ydma_on(yS[0], dbg["masks"][e_:e_ + 1, 512 * qt:512 * qt + 512],
                                              MROW[0:1, 0:512])
                    v.wait(yS[0], mrow_sp)
                    bank_free_v[2] = vS.n
                    bank_free_v[3] = vS.n

            if "p3" in _PHASES:
                qext = lambda j_: 2048 * j_
                kext = lambda j_: 8192 + 2048 * j_
                A3_VS, A3_VT = 2048, 4096
                A3_PB = [6176 + 512 * k for k in range(8)]
                A3_QER, A3_KER = 10272, 12320
                A3_RC = 3072
                A3_OUT = [2048, 2560]
                slab_free[0] = slab_free[1] = tS.n
                ld_er = ydma(AR[:, A3_QER:A3_QER + T], cst["qextr"])
                ld_er = ydma(AR[:, A3_KER:A3_KER + T], cst["kextr"])
                p3_sp = [0, 0]
                for g_ in range(8):
                    s.wait(yS[0], p3_sp[0])
                    s.wait(yS[1], p3_sp[1])
                    wsrc = io["wqkv"].rearrange("(kc p) n -> p kc n", p=128)
                    for idx, oc in enumerate([2 * g_, 2 * g_ + 1, 16 + g_, 24 + g_]):
                        slot = idx % 2
                        wsem = gW[0] if slot == 0 else gW[2]
                        g.wait(tS, slab_free[slot])
                        g.inc(g.dma_start(AR[:, A_SLB[slot]:A_SLB[slot] + 1024]
                                          .rearrange("p (kc n) -> p kc n", n=128),
                                          wsrc[:, :, 128 * oc:128 * oc + 128]), wsem, 16)
                        ldw = wsem.n
                        for qt in range(QT):
                            bank = qt % 2
                            qtw = slice(512 * qt, 512 * qt + 512)
                            t.wait(wsem, ldw)
                            t.wait(sS, bank_free_s[bank])
                            t.wait(vS, bank_free_v[bank])
                            for kc in range(NCH):
                                t.inc(t.matmul(
                                    PS[bank][:],
                                    AR[:, A_SLB[slot] + 128 * kc:A_SLB[slot] + 128 * kc + 128],
                                    fm(XNBF, kc, qtw),
                                    start=(kc == 0), stop=(kc == NCH - 1)), tS, 1)
                            mm = tS.n
                            s.wait(tS, mm)
                            if idx < 2:
                                for hh in range(2):
                                    j_ = 2 * idx + hh
                                    s.inc(s.activation(
                                        HC[0:64, qext(j_) + 512 * qt:qext(j_) + 512 * qt + 512],
                                        PS[bank][64 * hh:64 * hh + 64, :],
                                        AF.Copy, scale=0.125), sS, 1)
                            elif idx == 2:
                                for j_ in range(4):
                                    hh = j_ // 2
                                    s.inc(s.copy(
                                        HC[0:64, kext(j_) + 512 * qt:kext(j_) + 512 * qt + 512],
                                        PS[bank][64 * hh:64 * hh + 64, :]), sS, 1)
                            else:
                                s.inc(s.copy(AR[:, A3_VS + 512 * qt:A3_VS + 512 * qt + 512],
                                             PS[bank][:]), sS, 1)
                            bank_free_s[bank] = sS.n
                        slab_free[slot] = tS.n
                    qkv_ev = sS.n
                    g.wait(sS, qkv_ev)
                    g.wait(yD, ld_er)
                    for j_ in range(4):
                        h_ = 4 * g_ + j_
                        g.inc(g.dma_start(HC[64:68, qext(j_):qext(j_) + T],
                                          AR[4 * h_:4 * h_ + 4, A3_QER:A3_QER + T]),
                              gW[1], 16)
                        g.inc(g.dma_start(HC[64:68, kext(j_):kext(j_) + T],
                                          AR[4 * h_:4 * h_ + 4, A3_KER:A3_KER + T]),
                              gW[1], 16)
                    ext_dma = gW[1].n
                    v.wait(sS, sS.n)
                    v.inc(v.memset(AR[:, A3_VT:A3_VT + 2080]
                                   .rearrange("p (kt c) -> p kt c", c=130)[:, :, 64:65],
                                   1.0), vS, 1)
                    v.inc(v.memset(AR[:, A3_VT:A3_VT + 2080]
                                   .rearrange("p (kt c) -> p kt c", c=130)[:, :, 129:130],
                                   1.0), vS, 1)
                    ones_done = vS.n
                    for half in range(2):
                        bank = 2 + half
                        t.wait(sS, qkv_ev)
                        t.wait(vS, bank_free_v[bank])
                        for k in range(8):
                            kt = 8 * half + k
                            if k % 4 == 0:
                                t.wait(sS, bank_free_s[bank])
                            t.inc(t.matmul(PS[bank][:, 128 * (k % 4):128 * (k % 4) + 128],
                                           AR[:, A3_VS + 128 * kt:A3_VS + 128 * kt + 128],
                                           IDB[:],
                                           start=(k % 4 == 0), stop=(k % 4 == 3)), tS, 1)
                            if k % 4 == 3:
                                mm = tS.n
                                s.wait(tS, mm)
                                kt0 = kt - 3
                                for hh in range(2):
                                    s.inc(s.copy(
                                        AR[:, A3_VT + 130 * kt0:A3_VT + 130 * (kt0 + 4)]
                                        .rearrange("p (kt c) -> p kt c", c=130)
                                        [:, :, 65 * hh:65 * hh + 64],
                                        PS[bank][:].rearrange("p (kt c) -> p kt c", c=128)
                                        [:, :, 64 * hh:64 * hh + 64]), sS, 1)
                                bank_free_s[bank] = sS.n
                    vt_done = sS.n
                    SCBH = [0, 1, 4, 5]
                    PVH = [2, 3, 6, 7]
                    for qt in range(QT):
                        nkt = 4 * qt + 4
                        exp_free = [0] * 4
                        pb_free = [[0, 0] for _ in range(4)]
                        pv_last = [0] * 4
                        def p3_pv_wave(kt):
                            off = 128 * (kt - 4 * qt) if kt >= 4 * qt else 0
                            for j_ in range(4):
                                pv_bank = PVH[j_]
                                pb = A3_PB[2 * j_ + (kt % 2)]
                                t.wait(sS, exp_done_at[j_][kt % 2])
                                tri = tri_done_at[j_][kt % 2]
                                if tri:
                                    t.wait(vS, tri)
                                if kt == 0:
                                    t.wait(sS, bank_free_s[pv_bank])
                                    t.wait(vS, bank_free_v[pv_bank])
                                t.inc(t.matmul(
                                    PS[pv_bank][0:65, slice(off, 512)],
                                    AR[:, A3_VT + 130 * kt + 65 * (j_ // 2):
                                       A3_VT + 130 * kt + 65 * (j_ // 2) + 65],
                                    AR[:, pb + off:pb + 512],
                                    start=(kt == 0), stop=(kt == nkt - 1)), tS, 1)
                                pb_free[j_][kt % 2] = tS.n
                                pv_last[j_] = tS.n

                        exp_done_at = [[0, 0] for _ in range(4)]
                        tri_done_at = [[None, None] for _ in range(4)]
                        for kt in range(nkt):
                            off = 128 * (kt - 4 * qt) if kt >= 4 * qt else 0
                            w_ = slice(off, 512)
                            diag = kt >= 4 * qt
                            for j_ in range(4):
                                h_ = 4 * g_ + j_
                                sc_bank = SCBH[j_]
                                t.wait(sS, bank_free_s[sc_bank])
                                t.wait(vS, bank_free_v[sc_bank])
                                t.wait(gW[1], ext_dma)
                                t.wait(sS, vt_done)
                                t.wait(vS, ones_done)
                                t.inc(t.matmul(
                                    PS[sc_bank][:, w_],
                                    HC[0:68, kext(j_) + 128 * kt:kext(j_) + 128 * kt + 128],
                                    HC[0:68, qext(j_) + 512 * qt + off:qext(j_) + 512 * qt + 512],
                                    start=True, stop=True), tS, 1)
                                sc_mm = tS.n
                                if diag and h_ < 3:
                                    v.wait(tS, sc_mm)
                                    v.inc(v.tensor_add(PS[sc_bank][:, off:off + 128],
                                                       PS[sc_bank][:, off:off + 128],
                                                       MTRI[:]), vS, 1)
                                    mask_done = vS.n
                                else:
                                    mask_done = None
                                pb = A3_PB[2 * j_ + (kt % 2)]
                                s.wait(tS, sc_mm)
                                if mask_done is not None:
                                    s.wait(vS, mask_done)
                                s.wait(tS, pb_free[j_][kt % 2])
                                s.inc(s.activation(AR[:, pb + off:pb + 512],
                                                   PS[sc_bank][:, w_], AF.Exp), sS, 1)
                                exp_done_at[j_][kt % 2] = sS.n
                                bank_free_s[sc_bank] = sS.n
                                if diag and h_ >= 3:
                                    v.wait(sS, sS.n)
                                    v.inc(v.tensor_mul(AR[:, pb + off:pb + off + 128],
                                                       AR[:, pb + off:pb + off + 128],
                                                       TRI[:]), vS, 1)
                                    tri_done_at[j_][kt % 2] = vS.n
                                else:
                                    tri_done_at[j_][kt % 2] = None
                            if kt >= 1:
                                p3_pv_wave(kt - 1)
                        p3_pv_wave(nkt - 1)
                        for j_ in range(4):
                            v.wait(tS, pv_last[j_])
                            v.inc(v.tensor_copy(ARF[32 * j_:32 * j_ + 1, 0:512],
                                                PS[PVH[j_]][64:65, :]), vS, 1)
                        vbar(v.reciprocal(ARF[0:97, 512:1024], ARF[0:97, 0:512]))
                        for j_ in range(4):
                            rc = A3_RC + 512 * (j_ % 2)
                            v.inc(v.tensor_copy(AR[0:1, rc:rc + 512],
                                                ARF[32 * j_:32 * j_ + 1, 512:1024]),
                                  vS, 1)
                            rc_vv = vS.n
                            outb = A3_OUT[j_ // 2]
                            hrow = slice(64 * (j_ % 2), 64 * (j_ % 2) + 64)
                            if j_ % 2 == 0:
                                v.wait(yS[j_ // 2], p3_sp[j_ // 2])
                            v.wait(tS, pv_last[j_])
                            v.inc(v.tensor_copy(AR[hrow, outb:outb + 512],
                                                PS[PVH[j_]][0:64, :]), vS, 1)
                            ev_done = vS.n
                            t.wait(vS, max(ev_done, rc_vv))
                            t.inc(t.matmul(PS[PVH[j_]][:],
                                           SELA[:] if j_ % 2 == 0 else SELB[:],
                                           AR[0:1, rc:rc + 512],
                                           start=True, stop=True), tS, 1)
                            bc_mm = tS.n
                            v.wait(tS, bc_mm)
                            v.inc(v.tensor_mul(AR[hrow, outb:outb + 512],
                                               AR[hrow, outb:outb + 512],
                                               PS[PVH[j_]][hrow, :]), vS, 1)
                            bank_free_s[PVH[j_]] = sS.n
                            bank_free_v[PVH[j_]] = vS.n
                            if j_ % 2 == 1:
                                div_done = vS.n
                                y.wait(vS, div_done)
                                p3_sp[j_ // 2] = ydma_on(
                                    yS[j_ // 2],
                                    dr["apair"][2 * g_ + j_ // 2, :,
                                                512 * qt:512 * qt + 512],
                                    AR[:, outb:outb + 512])
                A3_WO, A3_BP = 0, 16384
                g.wait(tS, tS.n)
                g.wait(sS, sS.n)
                g.wait(vS, vS.n)
                ld_wo = gdma(AR[:, A3_WO:A3_WO + 16384]
                             .rearrange("p (r n) -> p r n", n=1024),
                             io["wout"].rearrange("(r p) n -> p r n", p=128))
                for qt in range(QT):
                    qtw = slice(512 * qt, 512 * qt + 512)
                    y.wait(tS, tS.n)
                    ld_bp = 0
                    for p_ in range(16):
                        ld_bp = ydma(AR[:, A3_BP + 512 * p_:A3_BP + 512 * p_ + 512],
                                     dr["apair"][p_, :, qtw])
                    for c_ in range(NCH):
                        bank = c_ % 4
                        t.wait(gD, ld_wo)
                        t.wait(yD, ld_bp)
                        t.wait(vS, bank_free_v[bank])
                        t.wait(sS, bank_free_s[bank])
                        for p_ in range(16):
                            t.inc(t.matmul(
                                PS[bank][:],
                                AR[:, A3_WO + 1024 * p_ + 128 * c_:
                                   A3_WO + 1024 * p_ + 128 * c_ + 128],
                                AR[:, A3_BP + 512 * p_:A3_BP + 512 * p_ + 512],
                                start=(p_ == 0), stop=(p_ == 15)), tS, 1)
                        mm = tS.n
                        v.wait(tS, mm)
                        v.inc(v.tensor_add(fm(ACC, c_, qtw), fm(ACC, c_, qtw),
                                           PS[bank][:]), vS, 1)
                        bank_free_v[bank] = vS.n
                if "p3" in _DEBUG:
                    y.wait(vS, vS.n)
                    ydma(dbg["acc3"].rearrange("(c p) t -> p c t", p=128),
                         ACC[:].rearrange("p (c t) -> p c t", c=NCH))

            if "p4" in _PHASES:
                A4_H2, A4_WV, A4_PJ, A4_MB = 0, 16384, 20480, 22528

                def abuf(jp16, t_):
                    base = 2048 * (jp16 % 8) + 512 * t_
                    tile = XNBF if jp16 < 8 else HC
                    return tile[:, base:base + 512]

                y.wait(tS, tS.n)
                y.wait(vS, vS.n)
                y.wait(sS, sS.n)
                slab_free[0] = slab_free[1] = tS.n
                ld_h2 = 0
                for c_ in range(NCH):
                    ld_h2 = ydma(AR[:, A4_H2 + 2048 * c_:A4_H2 + 2048 * (c_ + 1)],
                                 dr["h2"][128 * c_:128 * c_ + 128, :])
                wv_src = io["wv"].rearrange("e (kc p) n -> e p kc n", p=128)
                pj_src = io["wp"].rearrange("e (r p) n -> e p r n", p=128)
                pj_free = [tS.n, tS.n]
                mrow_ld = 0
                for e_ in range(_P4LIM[0]):
                    y.wait(tS, tS.n)
                    mrow_ld = ydma(MROW[0:1, :], dr["masks"][e_:e_ + 1, :])
                    for t_ in range(4):
                        t.wait(yD, mrow_ld)
                        t.wait(sS, bank_free_s[6])
                        t.wait(vS, bank_free_v[6])
                        t.inc(t.matmul(PS[6][:], ONE1B[:],
                                       MROW[0:1, 512 * t_:512 * t_ + 512],
                                       start=True, stop=True), tS, 1)
                        mm = tS.n
                        s.wait(tS, mm)
                        s.inc(s.copy(AR[:, A4_MB + 512 * t_:A4_MB + 512 * t_ + 512],
                                     PS[6][:]), sS, 1)
                        bank_free_s[6] = sS.n
                    mb_done = sS.n
                    for half in range(_P4LIM[1]):
                        for jp16 in range(_P4LIM[2]):
                            jp = 16 * half + jp16
                            slot = jp16 % 2
                            wsem = gW[0] if slot == 0 else gW[2]
                            g.wait(tS, slab_free[slot])
                            g.inc(g.dma_start(
                                AR[:, A4_WV + 2048 * slot:A4_WV + 2048 * slot + 1024]
                                .rearrange("p (kc n) -> p kc n", n=128),
                                wv_src[e_, :, :, 128 * jp:128 * jp + 128]), wsem, 16)
                            g.inc(g.dma_start(
                                AR[:, A4_WV + 2048 * slot + 1024:A4_WV + 2048 * slot + 2048]
                                .rearrange("p (kc n) -> p kc n", n=128),
                                wv_src[e_, :, :, 4096 + 128 * jp:4096 + 128 * jp + 128]),
                                wsem, 16)
                            ldw = wsem.n
                            for t_ in range(4):
                                gb, vb = t_ % 2, 2 + t_ % 2
                                t.wait(wsem, ldw)
                                t.wait(yD, ld_h2)
                                t.wait(sS, bank_free_s[gb])
                                t.wait(vS, bank_free_v[vb])
                                for kc in range(NCH):
                                    t.inc(t.matmul(
                                        PS[gb][:],
                                        AR[:, A4_WV + 2048 * slot + 128 * kc:
                                           A4_WV + 2048 * slot + 128 * kc + 128],
                                        AR[:, A4_H2 + 2048 * kc + 512 * t_:
                                           A4_H2 + 2048 * kc + 512 * t_ + 512],
                                        start=(kc == 0), stop=(kc == NCH - 1)), tS, 1)
                                for kc in range(NCH):
                                    t.inc(t.matmul(
                                        PS[vb][:],
                                        AR[:, A4_WV + 2048 * slot + 1024 + 128 * kc:
                                           A4_WV + 2048 * slot + 1024 + 128 * kc + 128],
                                        AR[:, A4_H2 + 2048 * kc + 512 * t_:
                                           A4_H2 + 2048 * kc + 512 * t_ + 512],
                                        start=(kc == 0), stop=(kc == NCH - 1)), tS, 1)
                                gv_mm = tS.n
                                s.wait(tS, gv_mm)
                                s.wait(vS, bank_free_v[gb])
                                s.inc(s.activation(abuf(jp16, t_), PS[gb][:],
                                                   AF.Silu), sS, 1)
                                silu_done = sS.n
                                bank_free_s[gb] = sS.n
                                v.wait(sS, silu_done)
                                v.wait(sS, mb_done)
                                v.inc(v.tensor_mul(abuf(jp16, t_), abuf(jp16, t_),
                                                   PS[vb][:]), vS, 1)
                                vbar(v.tensor_mul(abuf(jp16, t_), abuf(jp16, t_),
                                                  AR[:, A4_MB + 512 * t_:
                                                     A4_MB + 512 * t_ + 512]))
                                bank_free_v[vb] = vS.n
                            slab_free[slot] = tS.n
                        a_done = vS.n
                        pj_free[1] = max(pj_free[1], tS.n)
                        pj_base = [A4_PJ, A4_WV]
                        pjsem = [gW[1], gD]
                        for c_ in range(NCH):
                            sl = c_ % 2
                            g.wait(tS, pj_free[sl])
                            g.inc(g.dma_start(
                                AR[:, pj_base[sl]:pj_base[sl] + 2048]
                                .rearrange("p (r n) -> p r n", n=128),
                                pj_src[e_, :, 16 * half:16 * half + 16,
                                       128 * c_:128 * c_ + 128]), pjsem[sl], 16)
                            ld_pj = pjsem[sl].n
                            for t_ in range(4):
                                bank = 4 + (t_ % 4)
                                t.wait(pjsem[sl], ld_pj)
                                t.wait(vS, a_done)
                                t.wait(vS, bank_free_v[bank])
                                for jp16 in range(16):
                                    t.inc(t.matmul(
                                        PS[bank][:],
                                        AR[:, pj_base[sl] + 128 * jp16:
                                           pj_base[sl] + 128 * jp16 + 128],
                                        abuf(jp16, t_),
                                        start=(jp16 == 0), stop=(jp16 == 15)), tS, 1)
                                mm = tS.n
                                v.wait(tS, mm)
                                v.inc(v.tensor_add(
                                    fm(ACC, c_, slice(512 * t_, 512 * t_ + 512)),
                                    fm(ACC, c_, slice(512 * t_, 512 * t_ + 512)),
                                    PS[bank][:]), vS, 1)
                                bank_free_v[bank] = vS.n
                            pj_free[sl] = tS.n
                        slab_free[0] = slab_free[1] = tS.n
                if "p4" in _DEBUG:
                    y.wait(vS, vS.n)
                    ydma(dbg["acc4"].rearrange("(c p) t -> p c t", p=128),
                         ACC[:].rearrange("p (c t) -> p c t", c=NCH))

            if "p5" in _SKIP:
                y.wait(yD, yD.n)
                y.wait(yX[0], yX[0].n)
                y.wait(yX[1], yX[1].n)
                y.wait(vS, vS.n)
                y.wait(sS, sS.n)
                y.wait(tS, tS.n)
                return
            OUTS = [ARF[:, 2048:3072], ARF[:, 3072:4096]]
            p5_s0 = sS.n
            p5_v0 = vS.n
            st5 = [0] * NTT
            ev5 = [0] * NTT
            for i in range(NTT):
                xt = XT[i % 2]
                if i < 2:
                    y.wait(sS, p5_s0)
                    y.wait(vS, p5_v0)
                else:
                    y.wait(vS, ev5[i - 2])
                ld = ydma_on(yX[i % 2], xt[:], io["x"][128 * i:128 * i + 128, :])
                if i == 0:
                    t.wait(vS, p5_v0)
                    t.wait(sS, p5_s0)
                else:
                    t.wait(vS, ev5[i - 1])
                for half, bank in [(0, PS[2]), (1, PS[3])]:
                    for k in range(4):
                        ch = 4 * half + k
                        t.inc(t.matmul(bank[:, 128 * k:128 * k + 128],
                                       fm(ACC, ch, slice(128 * i, 128 * i + 128)),
                                       IDF[:], start=(k == 0), stop=(k == 3)), tS, 1)
                tp = tS.n
                v.wait(tS, tp)
                v.wait(yX[i % 2], ld)
                if i >= 2:
                    v.wait(yS[i % 2], st5[i - 2])
                outt = OUTS[i % 2]
                v.inc(v.tensor_add(outt[:, 0:512], PS[2][:], xt[:, 0:512]), vS, 1)
                v.inc(v.tensor_add(outt[:, 512:1024], PS[3][:], xt[:, 512:1024]), vS, 1)
                ev5[i] = vS.n
                y.wait(vS, ev5[i])
                st5[i] = ydma_on(yS[i % 2], io["out"][128 * i:128 * i + 128, :], outt[:])
            y.wait(yD, yD.n)
            for s_ in (yS[0], yS[1], yX[0], yX[1], gW[0], gW[1], gW[2], gD):
                y.wait(s_, s_.n)

        with nc.Block() as block:
            @block.tensor
            def _(eng):
                plan(Plan("tensor", eng))

            @block.vector
            def _(eng):
                plan(Plan("vector", eng))

            @block.scalar
            def _(eng):
                plan(Plan("scalar", eng))

            @block.gpsimd
            def _(eng):
                plan(Plan("gpsimd", eng))

            @block.sync
            def _(eng):
                plan(Plan("sync", eng))

    return nc


def _in_maps(inputs):
    w = {
        "wqkv": np.ascontiguousarray(inputs["L0_attn_W_qkv"]),
        "wout": np.ascontiguousarray(inputs["L0_attn_W_out"]),
        "ln1w": np.ascontiguousarray(inputs["L0_ffn_ln1_w"]).reshape(C),
        "ln1b": np.ascontiguousarray(inputs["L0_ffn_ln1_b"]).reshape(C),
        "ln2w": np.ascontiguousarray(inputs["L0_ffn_ln2_w"]).reshape(C),
        "ln2b": np.ascontiguousarray(inputs["L0_ffn_ln2_b"]).reshape(C),
        "convw": np.ascontiguousarray(inputs["L0_ffn_conv_w"]).reshape(C, 3),
        "wattn": np.ascontiguousarray(inputs["L0_ffn_attn_w"]),
        "wproj": np.ascontiguousarray(inputs["L0_ffn_proj_w"]),
        "wrout": np.ascontiguousarray(inputs["L0_ffn_router_w"]),
        "wv": np.ascontiguousarray(inputs["L0_ffn_expert_wv"]),
        "wp": np.ascontiguousarray(inputs["L0_ffn_expert_proj"]),
    }
    x = np.ascontiguousarray(inputs["L0_attn_inputs"])
    return [dict(w, x=x[b]) for b in range(B)]


def kernel(**inputs):
    nc = build_nc()
    maps = _in_maps(inputs)
    res = run_bass_kernel_spmd(nc, maps, core_ids=list(range(B)))
    return np.stack([np.asarray(r["out"], dtype=np.float32) for r in res.results])


# version 10
# speedup vs baseline: 1.2024x; 1.1779x over previous
import math
from contextlib import ExitStack

import numpy as np
import ml_dtypes

import concourse.bass as bass
import concourse.mybir as mybir
from concourse.bass_utils import run_bass_kernel_spmd

F32 = mybir.dt.float32
BF16 = mybir.dt.bfloat16
AF = mybir.ActivationFunctionType
ALU = mybir.AluOpType
AX = mybir.AxisListType

B, T, C = 8, 2048, 1024
NCH = C // 128
QT = 4
TW = 512
NTT = T // 128
EPS = 1e-5

_DEBUG = []
_NTILES = NTT
_SKIP = set()
_P0CUT = "full"
_PHASES = {"p1", "p2", "p3", "p4"}
_P1CUT = "full"
_P4LIM = (4, 2, 16)



def _alibi_slopes(num_heads):
    closest_pow2 = 2 ** int(math.floor(math.log2(num_heads)))
    base = 2.0 ** (-(2.0 ** (-(math.log2(closest_pow2) - 3))))
    powers = np.arange(1, closest_pow2 + 1, dtype=np.float32)
    slopes = base ** powers
    if closest_pow2 != num_heads:
        eb = 2.0 ** (-(2.0 ** (-(math.log2(2 * closest_pow2) - 3))))
        ep = np.arange(1, 2 * (num_heads - closest_pow2) + 1, 2, dtype=np.float32)
        slopes = np.concatenate([slopes, eb ** ep])
    return slopes[:num_heads].astype(np.float32)


def _consts():
    c = {}
    c["identb"] = np.eye(128, dtype=ml_dtypes.bfloat16)
    c["identf"] = np.eye(128, dtype=np.float32)
    D = 64
    inv_freq = 1.0 / (10000.0 ** (np.arange(0, D, 2, dtype=np.float64) / D))
    freqs = np.outer(np.arange(T, dtype=np.float64), inv_freq)
    emb = np.concatenate([freqs, freqs], 1)
    cos = np.cos(emb).astype(np.float32)
    sin = np.sin(emb).astype(np.float32)
    ssin = sin.copy()
    ssin[:, :32] *= -1.0
    c["cosb"] = cos.T[np.tile(np.arange(64), 2)].astype(ml_dtypes.bfloat16)
    c["sinb"] = ssin.T[np.tile(np.arange(64), 2)].astype(ml_dtypes.bfloat16)
    tri = (np.arange(128)[:, None] <= np.arange(128)[None, :])
    c["tri01"] = tri.astype(ml_dtypes.bfloat16)
    c["masktri"] = np.where(tri, 0.0, np.float32(-1e30)).astype(np.float32)
    kidx = np.arange(T)
    c["kbias"] = np.stack([kidx // 64, kidx % 64]).astype(ml_dtypes.bfloat16)
    c["qbias"] = np.stack([-(kidx // 64), -(kidx % 64)]).astype(ml_dtypes.bfloat16)
    sel = np.zeros((2, 128), dtype=ml_dtypes.bfloat16)
    sel[0, 0:64] = 1.0
    sel[1, 64:128] = 1.0
    c["sel01"] = sel
    sl = _alibi_slopes(32)
    slb = sl.astype(ml_dtypes.bfloat16).astype(np.float64)
    qr_ = np.zeros((128, T), np.float64)
    kr_ = np.zeros((128, T), np.float64)
    for h in range(32):
        qr_[4 * h + 0] = 64.0 * slb[h]
        qr_[4 * h + 1] = slb[h]
        qr_[4 * h + 2] = -(kidx // 64)
        qr_[4 * h + 3] = -(kidx % 64)
        kr_[4 * h + 0] = kidx // 64
        kr_[4 * h + 1] = kidx % 64
        kr_[4 * h + 2] = 64.0 * slb[h]
        kr_[4 * h + 3] = slb[h]
    c["qextr"] = qr_.astype(ml_dtypes.bfloat16)
    c["kextr"] = kr_.astype(ml_dtypes.bfloat16)
    selp3 = np.zeros((32, 16 * 128), np.float32)
    for p in range(16):
        for m in range(128):
            selp3[2 * p + m // 64, 128 * p + m] = 1.0
    c["selp3"] = selp3.astype(ml_dtypes.bfloat16)
    selp1 = np.zeros((16, 8 * 128), np.float32)
    for p in range(8):
        for m in range(128):
            selp1[2 * p + m // 64, 128 * p + m] = 1.0
    c["selp1"] = selp1.astype(ml_dtypes.bfloat16)
    return c



class _Sem:
    def __init__(self, h):
        self.h = h
        self.n = 0


class _FakeInstr:
    def then_inc(self, *a, **k):
        return self


_FAKE = _FakeInstr()


class _Eng:
    def __init__(self, plan, name):
        self._p = plan
        self._n = name
        self._last_wait = {}

    def wait(self, sem, val=None):
        v = sem.n if val is None else val
        if v <= 0:
            return
        if self._last_wait.get(id(sem), -1) >= v:
            return
        self._last_wait[id(sem)] = v
        r = self._p.real if self._p.active == self._n else None
        if r is not None:
            r.wait_ge(sem.h, v)

    def inc(self, instr, sem, k=1):
        sem.n += k
        if instr is not _FAKE:
            instr.then_inc(sem.h, k)
        return instr

    def __getattr__(self, op):
        if self._p.active == self._n:
            return getattr(self._p.real, op)
        return lambda *a, **k: _FAKE


class Plan:
    def __init__(self, active, real):
        self.active = active
        self.real = real
        self.t = _Eng(self, "tensor")
        self.v = _Eng(self, "vector")
        self.s = _Eng(self, "scalar")
        self.g = _Eng(self, "gpsimd")
        self.y = _Eng(self, "sync")



def build_nc():
    nc = bass.Bass("TRN2", target_bir_lowering=False, detect_race_conditions=False)

    io = {}
    io["x"] = nc.dram_tensor("x", [T, C], F32, kind="ExternalInput").ap()
    io["wqkv"] = nc.dram_tensor("wqkv", [C, 4096], F32, kind="ExternalInput").ap()
    io["wout"] = nc.dram_tensor("wout", [2048, C], F32, kind="ExternalInput").ap()
    io["ln1w"] = nc.dram_tensor("ln1w", [C], F32, kind="ExternalInput").ap()
    io["ln1b"] = nc.dram_tensor("ln1b", [C], F32, kind="ExternalInput").ap()
    io["ln2w"] = nc.dram_tensor("ln2w", [C], F32, kind="ExternalInput").ap()
    io["ln2b"] = nc.dram_tensor("ln2b", [C], F32, kind="ExternalInput").ap()
    io["convw"] = nc.dram_tensor("convw", [C, 3], F32, kind="ExternalInput").ap()
    io["wattn"] = nc.dram_tensor("wattn", [C, 3072], F32, kind="ExternalInput").ap()
    io["wproj"] = nc.dram_tensor("wproj", [C, C], F32, kind="ExternalInput").ap()
    io["wrout"] = nc.dram_tensor("wrout", [C, 4], F32, kind="ExternalInput").ap()
    io["wv"] = nc.dram_tensor("wv", [4, C, 8192], F32, kind="ExternalInput").ap()
    io["wp"] = nc.dram_tensor("wp", [4, 4096, C], F32, kind="ExternalInput").ap()
    io["out"] = nc.dram_tensor("out", [T, C], F32, kind="ExternalOutput").ap()

    cst = {k: nc.inline_tensor(np.ascontiguousarray(v), "c_" + k).ap()
           for k, v in _consts().items()}

    dr = {
        "xnbf": nc.dram_tensor("d_xnbf", [C, T], BF16).ap(),
        "h2": nc.dram_tensor("d_h2", [C, T], BF16).ap(),
        "apair": nc.dram_tensor("d_apair", [16, 128, T], BF16).ap(),
        "bpair": nc.dram_tensor("d_bpair", [8, 128, T], BF16).ap(),
        "masks": nc.dram_tensor("d_masks", [4, T], BF16).ap(),
        "dn1": nc.dram_tensor("d_dn1", [16, T], F32).ap(),
        "dn3": nc.dram_tensor("d_dn3", [32, T], F32).ap(),
    }

    dbg = {}
    def _dbg(name, shape, dtype=F32):
        dbg[name] = nc.dram_tensor("dbg_" + name, shape, dtype, kind="ExternalOutput").ap()
    if "p0" in _DEBUG:
        _dbg("xn_fm", [C, T])
        _dbg("h1_fm", [C, T], BF16)
    if "p1" in _DEBUG:
        _dbg("hc_fm", [C, T], BF16)
        _dbg("acc1", [C, T])
    if "p2" in _DEBUG:
        _dbg("h2", [C, T], BF16)
        _dbg("masks", [4, T], BF16)
    if "p3" in _DEBUG:
        _dbg("acc3", [C, T])
    if "p4" in _DEBUG:
        _dbg("acc4", [C, T])

    slopes = _alibi_slopes(32)

    with ExitStack() as ctx:
        def sb(name, shape, dtype):
            return ctx.enter_context(nc.sbuf_tensor(name, shape, dtype))

        def psum(name, shape, dtype=F32):
            return ctx.enter_context(nc.psum_tensor(name, shape, dtype))

        sems = {n: _Sem(ctx.enter_context(nc.semaphore(n)))
                for n in ["tS", "vS", "sS", "gD", "yD", "g2",
                          "yX0", "yX1", "yS0", "yS1", "gW0", "gW1", "gW2"]}

        ACC = sb("ACC", [128, NCH * T], F32)
        XNBF = sb("XNBF", [128, NCH * T], BF16)
        HC = sb("HC", [128, NCH * T], BF16)
        AR = sb("AR", [128, 24 * 1024], BF16)
        ARF = sb("ARF", [128, 4096], F32)
        IDB = sb("IDB", [128, 128], BF16)
        IDF = sb("IDF", [128, 128], F32)
        TRI = sb("TRI", [128, 128], BF16)
        MTRI = sb("MTRI", [128, 128], F32)
        SELA = sb("SELA", [1, 128], BF16)
        SELB = sb("SELB", [1, 128], BF16)
        SELP3 = sb("SELP3", [32, 16 * 128], BF16)
        SELP1 = sb("SELP1", [16, 8 * 128], BF16)
        RC3 = sb("RC3", [32, T], BF16)
        ONESB = sb("ONESB", [128, 1], BF16)
        ONESF = sb("ONESF", [128, 1], F32)
        ONE1F = sb("ONE1F", [1, 128], F32)
        ONE1B = sb("ONE1B", [1, 128], BF16)
        LN1WC = sb("LN1WC", [128, NCH], F32)
        LN1BC = sb("LN1BC", [128, NCH], F32)
        LN2WC = sb("LN2WC", [128, NCH], F32)
        LN2BC = sb("LN2BC", [128, NCH], F32)
        CONVW = sb("CONVW", [128, 3 * NCH], F32)
        ROUTW = sb("ROUTW", [128, 4 * NCH], F32)
        EPSC = sb("EPSC", [128, 1], F32)
        STC = sb("STC", [128, 16], F32)

        PS = [psum(f"PS{i}", [128, TW], F32) for i in range(8)]
        PD = PS[7]

        SQD = ARF[:, 3072:4096]

        def fm(tile, ch, cols):
            return tile[:, 2048 * ch + cols.start: 2048 * ch + cols.stop]

        def plan(p: Plan):
            for s_ in sems.values():
                s_.n = 0
            t, v, s, g, y = p.t, p.v, p.s, p.g, p.y
            tS, vS, sS, gD, yD, g2 = (sems[n] for n in ["tS", "vS", "sS", "gD", "yD", "g2"])
            yX = [sems["yX0"], sems["yX1"]]
            yS = [sems["yS0"], sems["yS1"]]
            gW = [sems["gW0"], sems["gW1"], sems["gW2"]]

            def ydma_on(sem, dst, src):
                y.inc(y.dma_start(dst, src), sem, 16)
                return sem.n

            def ydma(dst, src):
                y.inc(y.dma_start(dst, src), yD, 16)
                return yD.n

            def gdma(dst, src):
                g.inc(g.dma_start(dst, src), gD, 16)
                return gD.n

            def ydma_nc(dst, src):
                y.inc(y.dma_start(dst, src, allow_slow_non_contiguous=True), yD, 16)
                return yD.n

            def vbar(instr):
                v.inc(instr, vS, 1)
                v.wait(vS, vS.n)

            def sbar(instr):
                s.inc(instr, sS, 1)
                s.wait(sS, sS.n)

            for dst, key in [(IDB, "identb"), (IDF, "identf"), (TRI, "tri01"),
                             (MTRI, "masktri")]:
                ydma(dst[:], cst[key])
            ydma(SELA[:], cst["sel01"][0:1, :])
            ydma(SELB[:], cst["sel01"][1:2, :])
            ydma(SELP3[:], cst["selp3"])
            ydma(SELP1[:], cst["selp1"])
            ydma_nc(LN1WC[:], io["ln1w"].rearrange("(c p) -> p c", p=128))
            ydma_nc(LN1BC[:], io["ln1b"].rearrange("(c p) -> p c", p=128))
            ydma_nc(LN2WC[:], io["ln2w"].rearrange("(c p) -> p c", p=128))
            ydma_nc(LN2BC[:], io["ln2b"].rearrange("(c p) -> p c", p=128))
            ydma_nc(CONVW[:].rearrange("p (c k) -> p c k", k=3),
                 io["convw"].rearrange("(c p) k -> p c k", p=128))
            ydma_nc(ROUTW[:].rearrange("p (c e) -> p c e", e=4),
                 io["wrout"].rearrange("(c p) e -> p c e", p=128))
            ld_consts = yD.n
            slab_free = [0, 0]
            bank_free_s = [0] * 8
            bank_free_v = [0] * 8
            v.memset(ONESB[:], 1.0)
            v.memset(ONESF[:], 1.0)
            v.memset(ONE1F[:], 1.0)
            v.memset(ONE1B[:], 1.0)
            v.inc(v.memset(EPSC[:], EPS), vS, 1)
            v_consts = vS.n
            t.wait(yD, ld_consts)
            t.wait(vS, v_consts)
            v.wait(yD, ld_consts)
            s.wait(yD, ld_consts)
            XT = [ARF[:, 0:1024], ARF[:, 1024:2048]]
            XN = ARF[:, 2048:3072]
            H1B = AR[:, 0:1024]
            H1FM_BASE = 8192

            def h1fm(ch, cols):
                o = H1FM_BASE + 2048 * ch
                return AR[:, o + cols.start: o + cols.stop]

            ld_x = [0] * (NTT + 1)
            ld_x[0] = ydma_on(yX[0], XT[0][:], io["x"][0:128, :])
            done_sx = [0] * NTT
            done_vx = [0] * NTT
            done_t = [0] * NTT
            done_evv = [0] * NTT
            done_evs = [0] * NTT

            for i in range(_NTILES):
                xt = XT[i % 2]
                if i + 1 < _NTILES:
                    if i >= 1:
                        y.wait(sS, done_sx[i - 1])
                        y.wait(vS, done_vx[i - 1])
                    ld_x[i + 1] = ydma_on(yX[(i + 1) % 2], XT[(i + 1) % 2][:],
                                          io["x"][128 * (i + 1):128 * (i + 2), :])
                v.wait(yX[i % 2], ld_x[i])
                vbar(v.reduce_sum(STC[:, 0:1], xt[:], axis=AX.X))
                done_vx[i] = vS.n
                s.wait(yX[i % 2], ld_x[i])
                if i >= 1:
                    s.wait(tS, done_t[i - 1])
                s.inc(s.activation(SQD, xt[:], AF.Square), sS, 1)
                v.wait(sS, sS.n)
                vbar(v.reduce_sum(STC[:, 1:2], SQD, axis=AX.X))
                vbar(v.tensor_scalar(STC[:, 2:3], STC[:, 0:1], 1.0 / C, None, ALU.mult))
                vbar(v.tensor_scalar(STC[:, 3:4], STC[:, 1:2], 1.0 / C, None, ALU.mult))
                vbar(v.tensor_mul(STC[:, 4:5], STC[:, 2:3], STC[:, 2:3]))
                v.inc(v.tensor_sub(STC[:, 3:4], STC[:, 3:4], STC[:, 4:5]), vS, 1)
                s.wait(vS, vS.n)
                s.inc(s.activation(STC[:, 4:5], STC[:, 3:4], AF.Sqrt, bias=EPSC[:]), sS, 1)
                v.wait(sS, sS.n)
                vbar(v.reciprocal(STC[:, 5:6], STC[:, 4:5]))
                vbar(v.tensor_mul(STC[:, 6:7], STC[:, 2:3], STC[:, 5:6]))
                v.inc(v.tensor_scalar(STC[:, 6:7], STC[:, 6:7], -1.0, None, ALU.mult), vS, 1)
                s.wait(vS, vS.n)
                if _P0CUT == "stats":
                    done_sx[i] = sS.n
                    done_t[i] = tS.n
                    done_evv[i] = vS.n
                    done_evs[i] = sS.n
                    continue
                sbar(s.activation(XN[:], xt[:], AF.Identity,
                                  scale=STC[:, 5:6], bias=STC[:, 6:7]))
                v.wait(sS, sS.n)
                vbar(v.reduce_sum(STC[:, 8:9], XN[:], axis=AX.X))
                sbar(s.activation(SQD, XN[:], AF.Square))
                done_sx[i] = sS.n
                v.wait(sS, sS.n)
                vbar(v.reduce_sum(STC[:, 9:10], SQD, axis=AX.X))
                vbar(v.tensor_scalar(STC[:, 10:11], STC[:, 8:9], 1.0 / C, None, ALU.mult))
                vbar(v.tensor_scalar(STC[:, 11:12], STC[:, 9:10], 1.0 / C, None, ALU.mult))
                vbar(v.tensor_mul(STC[:, 12:13], STC[:, 10:11], STC[:, 10:11]))
                v.inc(v.tensor_sub(STC[:, 11:12], STC[:, 11:12], STC[:, 12:13]), vS, 1)
                s.wait(vS, vS.n)
                s.inc(s.activation(STC[:, 12:13], STC[:, 11:12], AF.Sqrt, bias=EPSC[:]), sS, 1)
                v.wait(sS, sS.n)
                vbar(v.reciprocal(STC[:, 13:14], STC[:, 12:13]))
                vbar(v.tensor_mul(STC[:, 14:15], STC[:, 10:11], STC[:, 13:14]))
                v.inc(v.tensor_scalar(STC[:, 14:15], STC[:, 14:15], -1.0, None, ALU.mult), vS, 1)
                if _P0CUT == "xn":
                    done_sx[i] = sS.n
                    done_t[i] = tS.n
                    done_evv[i] = vS.n
                    done_evs[i] = sS.n
                    continue
                s.wait(vS, vS.n)
                s.inc(s.activation(H1B[:], XN[:], AF.Identity,
                                   scale=STC[:, 13:14], bias=STC[:, 14:15]), sS, 1)
                ready = sS.n
                if _P0CUT == "h1":
                    done_sx[i] = sS.n
                    done_t[i] = tS.n
                    done_evv[i] = vS.n
                    done_evs[i] = sS.n
                    continue
                t.wait(sS, ready)
                if i >= 1:
                    t.wait(vS, done_evv[i - 1])
                    t.wait(sS, done_evs[i - 1])
                for half, bank in [(0, PS[0]), (1, PS[1])]:
                    for k in range(4):
                        ch = 4 * half + k
                        t.inc(t.matmul(bank[:, 128 * k:128 * k + 128],
                                       XN[:, 128 * ch:128 * ch + 128], IDF[:],
                                       start=(k == 0), stop=(k == 3)), tS, 1)
                for half, bank in [(0, PS[2]), (1, PS[3])]:
                    for k in range(4):
                        ch = 4 * half + k
                        t.inc(t.matmul(bank[:, 128 * k:128 * k + 128],
                                       H1B[:, 128 * ch:128 * ch + 128], IDB[:],
                                       start=(k == 0), stop=(k == 3)), tS, 1)
                done_t[i] = tS.n
                if _P0CUT == "tp":
                    done_evv[i] = vS.n
                    done_evs[i] = sS.n
                    continue
                cw = slice(128 * i, 128 * i + 128)
                v.wait(tS, done_t[i])
                for half, bank in [(0, PS[0]), (1, PS[1])]:
                    for k in range(4):
                        ch = 4 * half + k
                        v.inc(v.tensor_copy(fm(ACC, ch, cw),
                                            bank[:, 128 * k:128 * k + 128]), vS, 1)
                        v.inc(v.tensor_copy(fm(XNBF, ch, cw),
                                            bank[:, 128 * k:128 * k + 128]), vS, 1)
                for half, bank in [(0, PS[2]), (1, PS[3])]:
                    for k in range(4):
                        ch = 4 * half + k
                        v.inc(v.tensor_scalar(h1fm(ch, cw), bank[:, 128 * k:128 * k + 128],
                                              LN1WC[:, ch:ch + 1], LN1BC[:, ch:ch + 1],
                                              ALU.mult, ALU.add), vS, 1)
                done_evv[i] = vS.n
                done_evs[i] = sS.n

            if "p0" in _DEBUG:
                y.wait(vS, vS.n)
                y.wait(sS, sS.n)
                ydma(dbg["xn_fm"].rearrange("(c p) t -> p c t", p=128),
                     ACC[:].rearrange("p (c t) -> p c t", c=NCH))
                ydma(dbg["h1_fm"].rearrange("(c p) t -> p c t", p=128),
                     AR[:, H1FM_BASE:H1FM_BASE + NCH * 2048].rearrange(
                         "p (c t) -> p c t", c=NCH))

            if "p1" in _PHASES:
                v.wait(vS, vS.n)
                for ch in range(NCH):
                    w0 = CONVW[:, 3 * ch + 0:3 * ch + 1]
                    w1 = CONVW[:, 3 * ch + 1:3 * ch + 2]
                    w2 = CONVW[:, 3 * ch + 2:3 * ch + 3]
                    h1c = lambda a, b_: AR[:, H1FM_BASE + 2048 * ch + a:
                                           H1FM_BASE + 2048 * ch + b_]
                    hcc = lambda a, b_: HC[:, 2048 * ch + a:2048 * ch + b_]
                    vbar(v.tensor_scalar(hcc(0, T), h1c(0, T), w2, None, ALU.mult))
                    vbar(v.scalar_tensor_tensor(hcc(1, T), h1c(0, T - 1), w1,
                                                hcc(1, T), ALU.mult, ALU.add))
                    vbar(v.scalar_tensor_tensor(hcc(2, T), h1c(0, T - 2), w0,
                                                hcc(2, T), ALU.mult, ALU.add))
                conv_done = vS.n
                if "p1" in _DEBUG:
                    y.wait(vS, conv_done)
                    ydma(dbg["hc_fm"].rearrange("(c p) t -> p c t", p=128),
                         HC[:].rearrange("p (c t) -> p c t", c=NCH))

                A_SLB = [0, 1024]
                A_QS, A_KS, A_VS = 2048, 4096, 6144
                A_COS, A_SIN = 8192, 10240
                A_RT, A_QR, A_KR, A_VT = 12288, 14336, 16384, 18432
                A_PB = [[20512 + 512 * k for k in range(4)],
                        [12288 + 512 * k for k in range(4)]]
                A_RC = 2048
                A_OUT = [3072, 3584]
                y.wait(vS, conv_done)
                ld_cos = ydma(AR[:, A_COS:A_COS + T], cst["cosb"])
                ld_cos = ydma(AR[:, A_SIN:A_SIN + T], cst["sinb"])

                slab_free[0] = slab_free[1] = tS.n
                spill = [0, 0]
                dnsp = [0, 0]

                def qkv_chunk(oc, dst_base, slot, scale=1.0):
                    src = io["wattn"].rearrange("(kc p) n -> p kc n", p=128)
                    wsem = gW[0] if slot == 0 else gW[2]
                    g.wait(tS, slab_free[slot])
                    g.inc(g.dma_start(AR[:, A_SLB[slot]:A_SLB[slot] + 1024]
                                      .rearrange("p (kc n) -> p kc n", n=128),
                                      src[:, :, 128 * oc:128 * oc + 128]), wsem, 16)
                    ldw = wsem.n
                    for qt in range(QT):
                        bank = qt % 4
                        qtw = slice(512 * qt, 512 * qt + 512)
                        t.wait(wsem, ldw)
                        t.wait(vS, conv_done)
                        t.wait(sS, bank_free_s[bank])
                        t.wait(vS, bank_free_v[bank])
                        for kc in range(NCH):
                            t.inc(t.matmul(
                                PS[bank][:],
                                AR[:, A_SLB[slot] + 128 * kc:A_SLB[slot] + 128 * kc + 128],
                                fm(HC, kc, qtw),
                                start=(kc == 0), stop=(kc == NCH - 1)), tS, 1)
                        mm = tS.n
                        s.wait(tS, mm)
                        if scale == 1.0:
                            s.inc(s.copy(AR[:, dst_base + 512 * qt:dst_base + 512 * qt + 512],
                                         PS[bank][:]), sS, 1)
                        else:
                            s.inc(s.activation(
                                AR[:, dst_base + 512 * qt:dst_base + 512 * qt + 512],
                                PS[bank][:], AF.Copy, scale=scale), sS, 1)
                        bank_free_s[bank] = sS.n
                    slab_free[slot] = tS.n

                def rope(src_base, dst_base):
                    g.wait(sS, sS.n)
                    g.wait(tS, tS.n)
                    for (dp, sp_) in [(0, 32), (32, 0), (64, 96), (96, 64)]:
                        g.inc(g.dma_start(AR[dp:dp + 32, A_RT:A_RT + T],
                                          AR[sp_:sp_ + 32, src_base:src_base + T]),
                              gW[1], 16)
                    rot_ld = gW[1].n
                    v.wait(gW[1], rot_ld)
                    v.wait(sS, sS.n)
                    vbar(v.tensor_mul(AR[:, dst_base:dst_base + T],
                                      AR[:, src_base:src_base + T],
                                      AR[:, A_COS:A_COS + T]))
                    vbar(v.tensor_mul(AR[:, A_RT:A_RT + T],
                                      AR[:, A_RT:A_RT + T],
                                      AR[:, A_SIN:A_SIN + T]))
                    v.inc(v.tensor_add(AR[:, dst_base:dst_base + T],
                                       AR[:, dst_base:dst_base + T],
                                       AR[:, A_RT:A_RT + T]), vS, 1)

                npairs = 0 if _P1CUT == "conv" else (1 if _P1CUT in (
                    "qkv1", "rope1", "vt1", "attn1") else 8)
                for p_ in range(npairs):
                    qkv_chunk(p_, A_QS, 0, scale=0.125)
                    qkv_chunk(8 + p_, A_KS, 1)
                    qkv_chunk(16 + p_, A_VS, 0)
                    if _P1CUT == "qkv1":
                        continue
                    v.wait(yD, ld_cos)
                    rope(A_QS, A_QR)
                    rope(A_KS, A_KR)
                    rope_done = vS.n
                    if _P1CUT == "rope1":
                        continue
                    v.wait(sS, sS.n)
                    v.inc(v.memset(AR[:, A_VT + 64:A_VT + 64 + 1]
                                   .rearrange("p (a b) -> p a b", a=1) if False else
                                   AR[:, A_VT:A_VT + 2080]
                                   .rearrange("p (kt c) -> p kt c", c=130)[:, :, 64:65],
                                   1.0), vS, 1)
                    v.inc(v.memset(AR[:, A_VT:A_VT + 2080]
                                   .rearrange("p (kt c) -> p kt c", c=130)[:, :, 129:130],
                                   1.0), vS, 1)
                    ones_done = vS.n
                    for half in range(2):
                        bank = 4 + half
                        t.wait(sS, sS.n)
                        t.wait(vS, bank_free_v[bank])
                        for k in range(8):
                            kt = 8 * half + k
                            if k % 4 == 0:
                                t.wait(sS, bank_free_s[bank])
                            t.inc(t.matmul(PS[bank][:, 128 * (k % 4):128 * (k % 4) + 128],
                                           AR[:, A_VS + 128 * kt:A_VS + 128 * kt + 128],
                                           IDB[:],
                                           start=(k % 4 == 0), stop=(k % 4 == 3)), tS, 1)
                            if k % 4 == 3:
                                mm = tS.n
                                s.wait(tS, mm)
                                kt0 = kt - 3
                                for hh in range(2):
                                    s.inc(s.copy(
                                        AR[:, A_VT + 130 * kt0:A_VT + 130 * (kt0 + 4)]
                                        .rearrange("p (kt c) -> p kt c", c=130)
                                        [:, :, 65 * hh:65 * hh + 64],
                                        PS[bank][:].rearrange("p (kt c) -> p kt c", c=128)
                                        [:, :, 64 * hh:64 * hh + 64]), sS, 1)
                                bank_free_s[bank] = sS.n
                    vt_done = sS.n
                    if _P1CUT == "vt1":
                        continue
                    SCB = [[0, 1], [6, 7]]
                    for qt in range(QT):
                        qtw = slice(512 * qt, 512 * qt + 512)
                        nkt = 4 * qt + 4
                        pb_free = [[0] * 4, [0] * 4]
                        pvb = [[2, 4][qt % 2], [3, 5][qt % 2]]
                        exp_done_at = [[0] * 4, [0] * 4]
                        tri_done_at = [[None] * 4, [None] * 4]

                        def p1_pv_wave(kt):
                            off = 128 * (kt - 4 * qt) if kt >= 4 * qt else 0
                            for hh in range(2):
                                pv_bank = pvb[hh]
                                pb = A_PB[hh][kt % 4]
                                t.wait(sS, exp_done_at[hh][kt % 4])
                                tri = tri_done_at[hh][kt % 4]
                                if tri:
                                    t.wait(vS, tri)
                                if kt == 0:
                                    t.wait(sS, bank_free_s[pv_bank])
                                    t.wait(vS, bank_free_v[pv_bank])
                                t.inc(t.matmul(
                                    PS[pv_bank][0:65, slice(off, 512)],
                                    AR[:, A_VT + 130 * kt + 65 * hh:
                                       A_VT + 130 * kt + 65 * hh + 65],
                                    AR[:, pb + off:pb + 512],
                                    start=(kt == 0), stop=(kt == nkt - 1)), tS, 1)
                                pb_free[hh][kt % 4] = tS.n

                        for kt in range(nkt):
                            off = 128 * (kt - 4 * qt) if kt >= 4 * qt else 0
                            w_ = slice(off, 512)
                            for hh in range(2):
                                sc_bank = SCB[hh][kt % 2]
                                t.wait(sS, bank_free_s[sc_bank])
                                t.wait(vS, bank_free_v[sc_bank])
                                t.wait(sS, vt_done)
                                t.wait(vS, max(rope_done, ones_done))
                                t.inc(t.matmul(
                                    PS[sc_bank][:, w_],
                                    AR[64 * hh:64 * hh + 64,
                                       A_KR + 128 * kt:A_KR + 128 * kt + 128],
                                    AR[64 * hh:64 * hh + 64,
                                       A_QR + 512 * qt + off:A_QR + 512 * qt + 512],
                                    start=True, stop=True), tS, 1)
                                sc_mm = tS.n
                                s.wait(tS, sc_mm)
                                s.wait(tS, pb_free[hh][kt % 4])
                                pb = A_PB[hh][kt % 4]
                                s.inc(s.activation(AR[:, pb + off:pb + 512],
                                                   PS[sc_bank][:, w_], AF.Exp), sS, 1)
                                exp_done_at[hh][kt % 4] = sS.n
                                bank_free_s[sc_bank] = sS.n
                                if kt >= 4 * qt:
                                    v.wait(sS, sS.n)
                                    v.inc(v.tensor_mul(AR[:, pb + off:pb + off + 128],
                                                       AR[:, pb + off:pb + off + 128],
                                                       TRI[:]), vS, 1)
                                    tri_done_at[hh][kt % 4] = vS.n
                                else:
                                    tri_done_at[hh][kt % 4] = None
                            if kt >= 1:
                                p1_pv_wave(kt - 1)
                        p1_pv_wave(nkt - 1)
                        pv_all = tS.n
                        dcol = 512 * (qt % 2)
                        v.wait(tS, pv_all)
                        v.wait(yX[qt % 2], dnsp[qt % 2])
                        v.inc(v.tensor_copy(ARF[0:1, dcol:dcol + 512],
                                            PS[pvb[0]][64:65, :]), vS, 1)
                        v.inc(v.tensor_copy(ARF[32:33, dcol:dcol + 512],
                                            PS[pvb[1]][64:65, :]), vS, 1)
                        outb = A_OUT[qt % 2]
                        v.wait(yS[qt % 2], spill[qt % 2])
                        v.inc(v.tensor_copy(AR[0:64, outb:outb + 512],
                                            PS[pvb[0]][0:64, :]), vS, 1)
                        v.inc(v.tensor_copy(AR[64:128, outb:outb + 512],
                                            PS[pvb[1]][0:64, :]), vS, 1)
                        ev_done = vS.n
                        for hh in range(2):
                            bank_free_s[pvb[hh]] = sS.n
                            bank_free_v[pvb[hh]] = ev_done
                        y.wait(vS, ev_done)
                        spill[qt % 2] = ydma_on(yS[qt % 2],
                                                dr["bpair"][p_, :, 512 * qt:512 * qt + 512],
                                                AR[:, outb:outb + 512])
                        ydma_on(yX[qt % 2],
                                dr["dn1"][2 * p_:2 * p_ + 1, 512 * qt:512 * qt + 512],
                                ARF[0:1, dcol:dcol + 512])
                        dnsp[qt % 2] = ydma_on(
                            yX[qt % 2],
                            dr["dn1"][2 * p_ + 1:2 * p_ + 2, 512 * qt:512 * qt + 512],
                            ARF[32:33, dcol:dcol + 512])

                    s.wait(vS, vS.n)
                    s.wait(tS, tS.n)
                    s.wait(yS[0], spill[0])
                    s.wait(yS[1], spill[1])

                if _P1CUT != "full":
                    return
                A_PJ, A_BP = 0, 8192
                g.wait(tS, tS.n)
                g.wait(vS, vS.n)
                ld_pj = gdma(AR[:, A_PJ:A_PJ + 8192]
                             .rearrange("p (r n) -> p r n", n=1024),
                             io["wproj"].rearrange("(r p) n -> p r n", p=128))
                y.wait(yX[0], yX[0].n)
                y.wait(yX[1], yX[1].n)
                y.wait(vS, vS.n)
                ld_dn = ydma(ARF[0:16, 0:2048], dr["dn1"])
                v.wait(yD, ld_dn)
                vbar(v.reciprocal(ARF[0:16, 2048:4096], ARF[0:16, 0:2048]))
                v.inc(v.tensor_copy(RC3[0:16, :], ARF[0:16, 2048:4096]), vS, 1)
                rc1_done = vS.n
                for qt in range(QT):
                    qtw = slice(512 * qt, 512 * qt + 512)
                    y.wait(tS, tS.n)
                    ld_bp = 0
                    for p_ in range(8):
                        ld_bp = ydma(AR[:, A_BP + 512 * p_:A_BP + 512 * p_ + 512],
                                     dr["bpair"][p_, :, 512 * qt:512 * qt + 512])
                    for p_ in range(8):
                        bank = 4 + (p_ % 2)
                        t.wait(yD, ld_bp)
                        t.wait(vS, rc1_done)
                        t.wait(vS, bank_free_v[bank])
                        t.inc(t.matmul(PS[bank][:],
                                       SELP1[:, 128 * p_:128 * p_ + 128],
                                       RC3[0:16, 512 * qt:512 * qt + 512],
                                       start=True, stop=True), tS, 1)
                        bmm = tS.n
                        v.wait(tS, bmm)
                        v.inc(v.tensor_mul(AR[:, A_BP + 512 * p_:A_BP + 512 * p_ + 512],
                                           AR[:, A_BP + 512 * p_:A_BP + 512 * p_ + 512],
                                           PS[bank][:]), vS, 1)
                        bank_free_v[bank] = vS.n
                    norm_done = vS.n
                    for cg in range(2):
                        for cc in range(4):
                            c_ = 4 * cg + cc
                            bank = cc
                            t.wait(gD, ld_pj)
                            t.wait(vS, norm_done)
                            t.wait(vS, bank_free_v[bank])
                            t.wait(sS, bank_free_s[bank])
                            for p_ in range(8):
                                t.inc(t.matmul(
                                    PS[bank][:],
                                    AR[:, A_PJ + 1024 * p_ + 128 * c_:
                                       A_PJ + 1024 * p_ + 128 * c_ + 128],
                                    AR[:, A_BP + 512 * p_:A_BP + 512 * p_ + 512],
                                    start=(p_ == 0), stop=(p_ == 7)), tS, 1)
                            mm = tS.n
                            v.wait(tS, mm)
                            v.inc(v.tensor_add(fm(ACC, c_, qtw), fm(ACC, c_, qtw),
                                               PS[bank][:]), vS, 1)
                            bank_free_v[bank] = vS.n

                if "p1" in _DEBUG:
                    y.wait(vS, vS.n)
                    ydma(dbg["acc1"].rearrange("(c p) t -> p c t", p=128),
                         ACC[:].rearrange("p (c t) -> p c t", c=NCH))


            if "p2" in _PHASES:
                A_H2F = [0, 512]
                A_XSQ = [1024, 1536]
                A_RW = 2048
                A_LG = 2048
                h2st_free = [0, 0]
                p2_t0 = tS.n
                v.wait(vS, vS.n)
                for qt in range(QT):
                    qtw = slice(512 * qt, 512 * qt + 512)
                    for bk in (0, 1, 2, 3, 4):
                        t.wait(sS, bank_free_s[bk])
                        t.wait(vS, bank_free_v[bk])
                    for c_ in range(NCH):
                        t.inc(t.matmul(PS[0][0:1, :], ONESF[:], fm(ACC, c_, qtw),
                                       start=(c_ == 0), stop=(c_ == NCH - 1)), tS, 1)
                    for c_ in range(NCH):
                        xq = ARF[:, A_XSQ[c_ % 2]:A_XSQ[c_ % 2] + 512]
                        if c_ >= 2:
                            v.wait(tS, tS.n)
                        v.inc(v.tensor_mul(xq, fm(ACC, c_, qtw), fm(ACC, c_, qtw)), vS, 1)
                        t.wait(vS, vS.n)
                        t.inc(t.matmul(PS[1][0:1, :], ONESF[:], xq,
                                       start=(c_ == 0), stop=(c_ == NCH - 1)), tS, 1)
                    stat_mm = tS.n
                    R = lambda j_: ARF[0:1, 512 * j_:512 * j_ + 512]
                    s.wait(tS, stat_mm)
                    s.inc(s.copy(R(0), PS[0][0:1, :]), sS, 1)
                    sbar(s.copy(R(1), PS[1][0:1, :]))
                    bank_free_s[0] = sS.n
                    bank_free_s[1] = sS.n
                    v.wait(sS, sS.n)
                    vbar(v.tensor_scalar(R(2), R(0), 1.0 / C, None, ALU.mult))
                    vbar(v.tensor_scalar(R(3), R(1), 1.0 / C, None, ALU.mult))
                    vbar(v.tensor_mul(R(4), R(2), R(2)))
                    v.inc(v.tensor_sub(R(3), R(3), R(4)), vS, 1)
                    s.wait(vS, vS.n)
                    sbar(s.activation(R(4), R(3), AF.Sqrt, bias=EPSC[0:1, :]))
                    v.wait(sS, sS.n)
                    vbar(v.reciprocal(R(5), R(4)))
                    vbar(v.tensor_mul(R(6), R(2), R(5)))
                    v.inc(v.tensor_scalar(R(6), R(6), -1.0, None, ALU.mult), vS, 1)
                    rb_done = vS.n
                    t.wait(vS, rb_done)
                    t.inc(t.matmul(PS[2][:], ONE1F[:], R(5),
                                   start=True, stop=True), tS, 1)
                    t.inc(t.matmul(PS[3][:], ONE1F[:], R(6),
                                   start=True, stop=True), tS, 1)
                    bc_mm = tS.n
                    for c_ in range(NCH):
                        hb = ARF[:, A_H2F[c_ % 2]:A_H2F[c_ % 2] + 512]
                        v.wait(tS, tS.n)
                        v.wait(tS, bc_mm)
                        v.inc(v.tensor_mul(hb, fm(ACC, c_, qtw), PS[2][:]), vS, 1)
                        vbar(v.tensor_add(hb, hb, PS[3][:]))
                        vbar(v.tensor_scalar(hb, hb, LN2WC[:, c_:c_ + 1],
                                             LN2BC[:, c_:c_ + 1], ALU.mult, ALU.add))
                        h2f_done = vS.n
                        t.wait(vS, h2f_done)
                        t.inc(t.matmul(PS[4][0:4, :], ROUTW[:, 4 * c_:4 * c_ + 4], hb,
                                       start=(c_ == 0), stop=(c_ == NCH - 1)), tS, 1)
                        st_slot = c_ % 2
                        s.wait(vS, h2f_done)
                        s.wait(yS[st_slot], h2st_free[st_slot])
                        s.inc(s.copy(AR[:, 512 * st_slot:512 * st_slot + 512], hb), sS, 1)
                        h2ev = sS.n
                        y.wait(sS, h2ev)
                        h2st_free[st_slot] = ydma_on(
                            yS[st_slot], dr["h2"][128 * c_:128 * c_ + 128, qtw],
                            AR[:, 512 * st_slot:512 * st_slot + 512])
                        if "p2" in _DEBUG:
                            h2st_free[st_slot] = ydma_on(
                                yS[st_slot], dbg["h2"][128 * c_:128 * c_ + 128, qtw],
                                AR[:, 512 * st_slot:512 * st_slot + 512])
                    rt_mm = tS.n
                    s.wait(tS, rt_mm)
                    s.wait(vS, vS.n)
                    sbar(s.copy(ARF[0:4, 0:512], PS[4][0:4, :]))
                    lg_ev = sS.n
                    bank_free_s[4] = sS.n
                    g.wait(sS, lg_ev)
                    for e_ in range(1, 4):
                        g.inc(g.dma_start(R(4 + e_ - 1), ARF[e_:e_ + 1, 0:512]),
                              gW[1], 16)
                    lg_dma = gW[1].n
                    L_ = [R(0), R(4), R(5), R(6)]
                    v.wait(gW[1], lg_dma)
                    vbar(v.tensor_max(R(1), L_[0], L_[1]))
                    vbar(v.tensor_max(R(2), L_[2], L_[3]))
                    vbar(v.tensor_max(R(1), R(1), R(2)))
                    for Le in L_:
                        vbar(v.tensor_tensor(Le, Le, R(1), ALU.is_ge))
                    vbar(v.tensor_scalar(R(2), L_[0], -1.0, 1.0, ALU.mult, ALU.add))
                    vbar(v.tensor_scalar(R(3), L_[1], -1.0, 1.0, ALU.mult, ALU.add))
                    vbar(v.tensor_mul(L_[1], L_[1], R(2)))
                    vbar(v.tensor_mul(R(2), R(2), R(3)))
                    vbar(v.tensor_scalar(R(3), L_[2], -1.0, 1.0, ALU.mult, ALU.add))
                    vbar(v.tensor_mul(L_[2], L_[2], R(2)))
                    vbar(v.tensor_mul(R(2), R(2), R(3)))
                    vbar(v.tensor_mul(L_[3], L_[3], R(2)))
                    mrow_sp = 0
                    for e_, src_ in [(0, L_[0]), (1, L_[1]), (2, L_[2]), (3, L_[3])]:
                        v.wait(yS[0], mrow_sp)
                        v.inc(v.tensor_copy(RC3[0:1, 0:512], src_), vS, 1)
                        mv = vS.n
                        y.wait(vS, mv)
                        mrow_sp = ydma_on(yS[0], dr["masks"][e_:e_ + 1, 512 * qt:512 * qt + 512],
                                          RC3[0:1, 0:512])
                        if "p2" in _DEBUG:
                            mrow_sp = ydma_on(yS[0], dbg["masks"][e_:e_ + 1, 512 * qt:512 * qt + 512],
                                              RC3[0:1, 0:512])
                    v.wait(yS[0], mrow_sp)
                    bank_free_v[2] = vS.n
                    bank_free_v[3] = vS.n

            if "p3" in _PHASES:
                qext = lambda j_: 2048 * j_
                kext = lambda j_: 8192 + 2048 * j_
                A3_VS, A3_VT = 2048, 4096
                A3_PB = [6176 + 512 * k for k in range(8)]
                A3_QER, A3_KER = 10272, 12320
                A3_RC = 3072
                A3_OUT = [2048, 2560]
                slab_free[0] = slab_free[1] = tS.n
                dnsp = [yX[0].n, yX[1].n]
                ld_er = ydma(AR[:, A3_QER:A3_QER + T], cst["qextr"])
                ld_er = ydma(AR[:, A3_KER:A3_KER + T], cst["kextr"])
                p3_sp = [0, 0]
                for g_ in range(8):
                    s.wait(yS[0], p3_sp[0])
                    s.wait(yS[1], p3_sp[1])
                    wsrc = io["wqkv"].rearrange("(kc p) n -> p kc n", p=128)
                    for idx, oc in enumerate([2 * g_, 2 * g_ + 1, 16 + g_, 24 + g_]):
                        slot = idx % 2
                        wsem = gW[0] if slot == 0 else gW[2]
                        g.wait(tS, slab_free[slot])
                        g.inc(g.dma_start(AR[:, A_SLB[slot]:A_SLB[slot] + 1024]
                                          .rearrange("p (kc n) -> p kc n", n=128),
                                          wsrc[:, :, 128 * oc:128 * oc + 128]), wsem, 16)
                        ldw = wsem.n
                        for qt in range(QT):
                            bank = qt % 2
                            qtw = slice(512 * qt, 512 * qt + 512)
                            t.wait(wsem, ldw)
                            t.wait(sS, bank_free_s[bank])
                            t.wait(vS, bank_free_v[bank])
                            for kc in range(NCH):
                                t.inc(t.matmul(
                                    PS[bank][:],
                                    AR[:, A_SLB[slot] + 128 * kc:A_SLB[slot] + 128 * kc + 128],
                                    fm(XNBF, kc, qtw),
                                    start=(kc == 0), stop=(kc == NCH - 1)), tS, 1)
                            mm = tS.n
                            s.wait(tS, mm)
                            if idx < 2:
                                for hh in range(2):
                                    j_ = 2 * idx + hh
                                    s.inc(s.activation(
                                        HC[0:64, qext(j_) + 512 * qt:qext(j_) + 512 * qt + 512],
                                        PS[bank][64 * hh:64 * hh + 64, :],
                                        AF.Copy, scale=0.125), sS, 1)
                            elif idx == 2:
                                for j_ in range(4):
                                    hh = j_ // 2
                                    s.inc(s.copy(
                                        HC[0:64, kext(j_) + 512 * qt:kext(j_) + 512 * qt + 512],
                                        PS[bank][64 * hh:64 * hh + 64, :]), sS, 1)
                            else:
                                s.inc(s.copy(AR[:, A3_VS + 512 * qt:A3_VS + 512 * qt + 512],
                                             PS[bank][:]), sS, 1)
                            bank_free_s[bank] = sS.n
                        slab_free[slot] = tS.n
                    qkv_ev = sS.n
                    g.wait(sS, qkv_ev)
                    g.wait(yD, ld_er)
                    for j_ in range(4):
                        h_ = 4 * g_ + j_
                        g.inc(g.dma_start(HC[64:68, qext(j_):qext(j_) + T],
                                          AR[4 * h_:4 * h_ + 4, A3_QER:A3_QER + T]),
                              gW[1], 16)
                        g.inc(g.dma_start(HC[64:68, kext(j_):kext(j_) + T],
                                          AR[4 * h_:4 * h_ + 4, A3_KER:A3_KER + T]),
                              gW[1], 16)
                    ext_dma = gW[1].n
                    v.wait(sS, sS.n)
                    v.inc(v.memset(AR[:, A3_VT:A3_VT + 2080]
                                   .rearrange("p (kt c) -> p kt c", c=130)[:, :, 64:65],
                                   1.0), vS, 1)
                    v.inc(v.memset(AR[:, A3_VT:A3_VT + 2080]
                                   .rearrange("p (kt c) -> p kt c", c=130)[:, :, 129:130],
                                   1.0), vS, 1)
                    ones_done = vS.n
                    for half in range(2):
                        bank = 2 + half
                        t.wait(sS, qkv_ev)
                        t.wait(vS, bank_free_v[bank])
                        for k in range(8):
                            kt = 8 * half + k
                            if k % 4 == 0:
                                t.wait(sS, bank_free_s[bank])
                            t.inc(t.matmul(PS[bank][:, 128 * (k % 4):128 * (k % 4) + 128],
                                           AR[:, A3_VS + 128 * kt:A3_VS + 128 * kt + 128],
                                           IDB[:],
                                           start=(k % 4 == 0), stop=(k % 4 == 3)), tS, 1)
                            if k % 4 == 3:
                                mm = tS.n
                                s.wait(tS, mm)
                                kt0 = kt - 3
                                for hh in range(2):
                                    s.inc(s.copy(
                                        AR[:, A3_VT + 130 * kt0:A3_VT + 130 * (kt0 + 4)]
                                        .rearrange("p (kt c) -> p kt c", c=130)
                                        [:, :, 65 * hh:65 * hh + 64],
                                        PS[bank][:].rearrange("p (kt c) -> p kt c", c=128)
                                        [:, :, 64 * hh:64 * hh + 64]), sS, 1)
                                bank_free_s[bank] = sS.n
                    vt_done = sS.n
                    SCBH = [0, 1, 4, 5]
                    PVH = [2, 3, 6, 7]
                    for qt in range(QT):
                        nkt = 4 * qt + 4
                        exp_free = [0] * 4
                        pb_free = [[0, 0] for _ in range(4)]
                        pv_last = [0] * 4
                        def p3_pv_wave(kt):
                            off = 128 * (kt - 4 * qt) if kt >= 4 * qt else 0
                            for j_ in range(4):
                                pv_bank = PVH[j_]
                                pb = A3_PB[2 * j_ + (kt % 2)]
                                t.wait(sS, exp_done_at[j_][kt % 2])
                                tri = tri_done_at[j_][kt % 2]
                                if tri:
                                    t.wait(vS, tri)
                                if kt == 0:
                                    t.wait(sS, bank_free_s[pv_bank])
                                    t.wait(vS, bank_free_v[pv_bank])
                                t.inc(t.matmul(
                                    PS[pv_bank][0:65, slice(off, 512)],
                                    AR[:, A3_VT + 130 * kt + 65 * (j_ // 2):
                                       A3_VT + 130 * kt + 65 * (j_ // 2) + 65],
                                    AR[:, pb + off:pb + 512],
                                    start=(kt == 0), stop=(kt == nkt - 1)), tS, 1)
                                pb_free[j_][kt % 2] = tS.n
                                pv_last[j_] = tS.n

                        exp_done_at = [[0, 0] for _ in range(4)]
                        tri_done_at = [[None, None] for _ in range(4)]
                        for kt in range(nkt):
                            off = 128 * (kt - 4 * qt) if kt >= 4 * qt else 0
                            w_ = slice(off, 512)
                            diag = kt >= 4 * qt
                            for j_ in range(4):
                                h_ = 4 * g_ + j_
                                sc_bank = SCBH[j_]
                                t.wait(sS, bank_free_s[sc_bank])
                                t.wait(vS, bank_free_v[sc_bank])
                                t.wait(gW[1], ext_dma)
                                t.wait(sS, vt_done)
                                t.wait(vS, ones_done)
                                t.inc(t.matmul(
                                    PS[sc_bank][:, w_],
                                    HC[0:68, kext(j_) + 128 * kt:kext(j_) + 128 * kt + 128],
                                    HC[0:68, qext(j_) + 512 * qt + off:qext(j_) + 512 * qt + 512],
                                    start=True, stop=True), tS, 1)
                                sc_mm = tS.n
                                if diag and h_ < 3:
                                    v.wait(tS, sc_mm)
                                    v.inc(v.tensor_add(PS[sc_bank][:, off:off + 128],
                                                       PS[sc_bank][:, off:off + 128],
                                                       MTRI[:]), vS, 1)
                                    mask_done = vS.n
                                else:
                                    mask_done = None
                                pb = A3_PB[2 * j_ + (kt % 2)]
                                s.wait(tS, sc_mm)
                                if mask_done is not None:
                                    s.wait(vS, mask_done)
                                s.wait(tS, pb_free[j_][kt % 2])
                                s.inc(s.activation(AR[:, pb + off:pb + 512],
                                                   PS[sc_bank][:, w_], AF.Exp), sS, 1)
                                exp_done_at[j_][kt % 2] = sS.n
                                bank_free_s[sc_bank] = sS.n
                                if diag and h_ >= 3:
                                    v.wait(sS, sS.n)
                                    v.inc(v.tensor_mul(AR[:, pb + off:pb + off + 128],
                                                       AR[:, pb + off:pb + off + 128],
                                                       TRI[:]), vS, 1)
                                    tri_done_at[j_][kt % 2] = vS.n
                                else:
                                    tri_done_at[j_][kt % 2] = None
                            if kt >= 1:
                                p3_pv_wave(kt - 1)
                        p3_pv_wave(nkt - 1)
                        dcol = 512 * (qt % 2)
                        v.wait(yX[qt % 2], dnsp[qt % 2])
                        for j_ in range(4):
                            v.wait(tS, pv_last[j_])
                            v.inc(v.tensor_copy(ARF[32 * j_:32 * j_ + 1, dcol:dcol + 512],
                                                PS[PVH[j_]][64:65, :]), vS, 1)
                            outb = A3_OUT[j_ // 2]
                            hrow = slice(64 * (j_ % 2), 64 * (j_ % 2) + 64)
                            if j_ % 2 == 0:
                                v.wait(yS[j_ // 2], p3_sp[j_ // 2])
                            v.inc(v.tensor_copy(AR[hrow, outb:outb + 512],
                                                PS[PVH[j_]][0:64, :]), vS, 1)
                            bank_free_s[PVH[j_]] = sS.n
                            bank_free_v[PVH[j_]] = vS.n
                            if j_ % 2 == 1:
                                div_done = vS.n
                                y.wait(vS, div_done)
                                p3_sp[j_ // 2] = ydma_on(
                                    yS[j_ // 2],
                                    dr["apair"][2 * g_ + j_ // 2, :,
                                                512 * qt:512 * qt + 512],
                                    AR[:, outb:outb + 512])
                        dn_ev = vS.n
                        y.wait(vS, dn_ev)
                        for j_ in range(4):
                            dnsp[qt % 2] = ydma_on(
                                yX[qt % 2],
                                dr["dn3"][4 * g_ + j_:4 * g_ + j_ + 1,
                                          512 * qt:512 * qt + 512],
                                ARF[32 * j_:32 * j_ + 1, dcol:dcol + 512])
                A3_WO, A3_BP = 0, 16384
                g.wait(tS, tS.n)
                g.wait(sS, sS.n)
                g.wait(vS, vS.n)
                ld_wo = gdma(AR[:, A3_WO:A3_WO + 16384]
                             .rearrange("p (r n) -> p r n", n=1024),
                             io["wout"].rearrange("(r p) n -> p r n", p=128))
                y.wait(yX[0], yX[0].n)
                y.wait(yX[1], yX[1].n)
                y.wait(vS, vS.n)
                ld_dn = ydma(ARF[0:32, 0:2048], dr["dn3"])
                v.wait(yD, ld_dn)
                vbar(v.reciprocal(ARF[0:32, 2048:4096], ARF[0:32, 0:2048]))
                v.inc(v.tensor_copy(RC3[0:32, :], ARF[0:32, 2048:4096]), vS, 1)
                rc3_done = vS.n
                for qt in range(QT):
                    qtw = slice(512 * qt, 512 * qt + 512)
                    y.wait(tS, tS.n)
                    ld_bp = 0
                    for p_ in range(16):
                        ld_bp = ydma(AR[:, A3_BP + 512 * p_:A3_BP + 512 * p_ + 512],
                                     dr["apair"][p_, :, qtw])
                    for p_ in range(16):
                        bank = 4 + (p_ % 2)
                        t.wait(yD, ld_bp)
                        t.wait(vS, bank_free_v[bank])
                        t.wait(vS, rc3_done)
                        t.inc(t.matmul(PS[bank][:],
                                       SELP3[:, 128 * p_:128 * p_ + 128],
                                       RC3[0:32, 512 * qt:512 * qt + 512],
                                       start=True, stop=True), tS, 1)
                        bmm = tS.n
                        v.wait(tS, bmm)
                        v.inc(v.tensor_mul(AR[:, A3_BP + 512 * p_:A3_BP + 512 * p_ + 512],
                                           AR[:, A3_BP + 512 * p_:A3_BP + 512 * p_ + 512],
                                           PS[bank][:]), vS, 1)
                        bank_free_v[bank] = vS.n
                    norm_done = vS.n
                    for c_ in range(NCH):
                        bank = c_ % 4
                        t.wait(gD, ld_wo)
                        t.wait(vS, norm_done)
                        t.wait(vS, bank_free_v[bank])
                        t.wait(sS, bank_free_s[bank])
                        for p_ in range(16):
                            t.inc(t.matmul(
                                PS[bank][:],
                                AR[:, A3_WO + 1024 * p_ + 128 * c_:
                                   A3_WO + 1024 * p_ + 128 * c_ + 128],
                                AR[:, A3_BP + 512 * p_:A3_BP + 512 * p_ + 512],
                                start=(p_ == 0), stop=(p_ == 15)), tS, 1)
                        mm = tS.n
                        v.wait(tS, mm)
                        v.inc(v.tensor_add(fm(ACC, c_, qtw), fm(ACC, c_, qtw),
                                           PS[bank][:]), vS, 1)
                        bank_free_v[bank] = vS.n
                if "p3" in _DEBUG:
                    y.wait(vS, vS.n)
                    ydma(dbg["acc3"].rearrange("(c p) t -> p c t", p=128),
                         ACC[:].rearrange("p (c t) -> p c t", c=NCH))

            if "p4" in _PHASES:
                A4_H2, A4_WV, A4_PJ, A4_MB = 0, 16384, 20480, 22528

                def abuf(jp16, t_):
                    base = 2048 * (jp16 % 8) + 512 * t_
                    tile = XNBF if jp16 < 8 else HC
                    return tile[:, base:base + 512]

                y.wait(tS, tS.n)
                y.wait(vS, vS.n)
                y.wait(sS, sS.n)
                slab_free[0] = slab_free[1] = tS.n
                ld_h2 = 0
                for c_ in range(NCH):
                    ld_h2 = ydma(AR[:, A4_H2 + 2048 * c_:A4_H2 + 2048 * (c_ + 1)],
                                 dr["h2"][128 * c_:128 * c_ + 128, :])
                wv_src = io["wv"].rearrange("e (kc p) n -> e p kc n", p=128)
                pj_src = io["wp"].rearrange("e (r p) n -> e p r n", p=128)
                pj_free = [tS.n, tS.n]
                mrow_ld = 0
                for e_ in range(_P4LIM[0]):
                    y.wait(tS, tS.n)
                    mrow_ld = ydma(RC3[0:1, 0:T], dr["masks"][e_:e_ + 1, :])
                    for t_ in range(4):
                        t.wait(yD, mrow_ld)
                        t.wait(sS, bank_free_s[6])
                        t.wait(vS, bank_free_v[6])
                        t.inc(t.matmul(PS[6][:], ONE1B[:],
                                       RC3[0:1, 512 * t_:512 * t_ + 512],
                                       start=True, stop=True), tS, 1)
                        mm = tS.n
                        s.wait(tS, mm)
                        s.inc(s.copy(AR[:, A4_MB + 512 * t_:A4_MB + 512 * t_ + 512],
                                     PS[6][:]), sS, 1)
                        bank_free_s[6] = sS.n
                    mb_done = sS.n
                    for half in range(_P4LIM[1]):
                        for jp16 in range(_P4LIM[2]):
                            jp = 16 * half + jp16
                            slot = jp16 % 2
                            wsem = gW[0] if slot == 0 else gW[2]
                            g.wait(tS, slab_free[slot])
                            g.inc(g.dma_start(
                                AR[:, A4_WV + 2048 * slot:A4_WV + 2048 * slot + 1024]
                                .rearrange("p (kc n) -> p kc n", n=128),
                                wv_src[e_, :, :, 128 * jp:128 * jp + 128]), wsem, 16)
                            g.inc(g.dma_start(
                                AR[:, A4_WV + 2048 * slot + 1024:A4_WV + 2048 * slot + 2048]
                                .rearrange("p (kc n) -> p kc n", n=128),
                                wv_src[e_, :, :, 4096 + 128 * jp:4096 + 128 * jp + 128]),
                                wsem, 16)
                            ldw = wsem.n
                            for t_ in range(4):
                                gb, vb = t_ % 2, 2 + t_ % 2
                                t.wait(wsem, ldw)
                                t.wait(yD, ld_h2)
                                t.wait(sS, bank_free_s[gb])
                                t.wait(vS, bank_free_v[vb])
                                for kc in range(NCH):
                                    t.inc(t.matmul(
                                        PS[gb][:],
                                        AR[:, A4_WV + 2048 * slot + 128 * kc:
                                           A4_WV + 2048 * slot + 128 * kc + 128],
                                        AR[:, A4_H2 + 2048 * kc + 512 * t_:
                                           A4_H2 + 2048 * kc + 512 * t_ + 512],
                                        start=(kc == 0), stop=(kc == NCH - 1)), tS, 1)
                                for kc in range(NCH):
                                    t.inc(t.matmul(
                                        PS[vb][:],
                                        AR[:, A4_WV + 2048 * slot + 1024 + 128 * kc:
                                           A4_WV + 2048 * slot + 1024 + 128 * kc + 128],
                                        AR[:, A4_H2 + 2048 * kc + 512 * t_:
                                           A4_H2 + 2048 * kc + 512 * t_ + 512],
                                        start=(kc == 0), stop=(kc == NCH - 1)), tS, 1)
                                gv_mm = tS.n
                                s.wait(tS, gv_mm)
                                s.wait(vS, bank_free_v[gb])
                                s.inc(s.activation(abuf(jp16, t_), PS[gb][:],
                                                   AF.Silu), sS, 1)
                                silu_done = sS.n
                                bank_free_s[gb] = sS.n
                                v.wait(sS, silu_done)
                                v.wait(sS, mb_done)
                                v.inc(v.tensor_mul(abuf(jp16, t_), abuf(jp16, t_),
                                                   PS[vb][:]), vS, 1)
                                vbar(v.tensor_mul(abuf(jp16, t_), abuf(jp16, t_),
                                                  AR[:, A4_MB + 512 * t_:
                                                     A4_MB + 512 * t_ + 512]))
                                bank_free_v[vb] = vS.n
                            slab_free[slot] = tS.n
                        a_done = vS.n
                        pj_free[1] = max(pj_free[1], tS.n)
                        pj_base = [A4_PJ, A4_WV]
                        pjsem = [gW[1], gD]
                        for c_ in range(NCH):
                            sl = c_ % 2
                            g.wait(tS, pj_free[sl])
                            g.inc(g.dma_start(
                                AR[:, pj_base[sl]:pj_base[sl] + 2048]
                                .rearrange("p (r n) -> p r n", n=128),
                                pj_src[e_, :, 16 * half:16 * half + 16,
                                       128 * c_:128 * c_ + 128]), pjsem[sl], 16)
                            ld_pj = pjsem[sl].n
                            for t_ in range(4):
                                bank = 4 + (t_ % 4)
                                t.wait(pjsem[sl], ld_pj)
                                t.wait(vS, a_done)
                                t.wait(vS, bank_free_v[bank])
                                for jp16 in range(16):
                                    t.inc(t.matmul(
                                        PS[bank][:],
                                        AR[:, pj_base[sl] + 128 * jp16:
                                           pj_base[sl] + 128 * jp16 + 128],
                                        abuf(jp16, t_),
                                        start=(jp16 == 0), stop=(jp16 == 15)), tS, 1)
                                mm = tS.n
                                v.wait(tS, mm)
                                v.inc(v.tensor_add(
                                    fm(ACC, c_, slice(512 * t_, 512 * t_ + 512)),
                                    fm(ACC, c_, slice(512 * t_, 512 * t_ + 512)),
                                    PS[bank][:]), vS, 1)
                                bank_free_v[bank] = vS.n
                            pj_free[sl] = tS.n
                        slab_free[0] = slab_free[1] = tS.n
                if "p4" in _DEBUG:
                    y.wait(vS, vS.n)
                    ydma(dbg["acc4"].rearrange("(c p) t -> p c t", p=128),
                         ACC[:].rearrange("p (c t) -> p c t", c=NCH))

            if "p5" in _SKIP:
                y.wait(yD, yD.n)
                y.wait(yX[0], yX[0].n)
                y.wait(yX[1], yX[1].n)
                y.wait(vS, vS.n)
                y.wait(sS, sS.n)
                y.wait(tS, tS.n)
                return
            OUTS = [ARF[:, 2048:3072], ARF[:, 3072:4096]]
            p5_s0 = sS.n
            p5_v0 = vS.n
            st5 = [0] * NTT
            ev5 = [0] * NTT
            for i in range(NTT):
                xt = XT[i % 2]
                if i < 2:
                    y.wait(sS, p5_s0)
                    y.wait(vS, p5_v0)
                else:
                    y.wait(vS, ev5[i - 2])
                ld = ydma_on(yX[i % 2], xt[:], io["x"][128 * i:128 * i + 128, :])
                if i == 0:
                    t.wait(vS, p5_v0)
                    t.wait(sS, p5_s0)
                else:
                    t.wait(vS, ev5[i - 1])
                for half, bank in [(0, PS[2]), (1, PS[3])]:
                    for k in range(4):
                        ch = 4 * half + k
                        t.inc(t.matmul(bank[:, 128 * k:128 * k + 128],
                                       fm(ACC, ch, slice(128 * i, 128 * i + 128)),
                                       IDF[:], start=(k == 0), stop=(k == 3)), tS, 1)
                tp = tS.n
                v.wait(tS, tp)
                v.wait(yX[i % 2], ld)
                if i >= 2:
                    v.wait(yS[i % 2], st5[i - 2])
                outt = OUTS[i % 2]
                v.inc(v.tensor_add(outt[:, 0:512], PS[2][:], xt[:, 0:512]), vS, 1)
                v.inc(v.tensor_add(outt[:, 512:1024], PS[3][:], xt[:, 512:1024]), vS, 1)
                ev5[i] = vS.n
                y.wait(vS, ev5[i])
                st5[i] = ydma_on(yS[i % 2], io["out"][128 * i:128 * i + 128, :], outt[:])
            y.wait(yD, yD.n)
            for s_ in (yS[0], yS[1], yX[0], yX[1], gW[0], gW[1], gW[2], gD):
                y.wait(s_, s_.n)

        with nc.Block() as block:
            @block.tensor
            def _(eng):
                plan(Plan("tensor", eng))

            @block.vector
            def _(eng):
                plan(Plan("vector", eng))

            @block.scalar
            def _(eng):
                plan(Plan("scalar", eng))

            @block.gpsimd
            def _(eng):
                plan(Plan("gpsimd", eng))

            @block.sync
            def _(eng):
                plan(Plan("sync", eng))

    return nc


def _in_maps(inputs):
    w = {
        "wqkv": np.ascontiguousarray(inputs["L0_attn_W_qkv"]),
        "wout": np.ascontiguousarray(inputs["L0_attn_W_out"]),
        "ln1w": np.ascontiguousarray(inputs["L0_ffn_ln1_w"]).reshape(C),
        "ln1b": np.ascontiguousarray(inputs["L0_ffn_ln1_b"]).reshape(C),
        "ln2w": np.ascontiguousarray(inputs["L0_ffn_ln2_w"]).reshape(C),
        "ln2b": np.ascontiguousarray(inputs["L0_ffn_ln2_b"]).reshape(C),
        "convw": np.ascontiguousarray(inputs["L0_ffn_conv_w"]).reshape(C, 3),
        "wattn": np.ascontiguousarray(inputs["L0_ffn_attn_w"]),
        "wproj": np.ascontiguousarray(inputs["L0_ffn_proj_w"]),
        "wrout": np.ascontiguousarray(inputs["L0_ffn_router_w"]),
        "wv": np.ascontiguousarray(inputs["L0_ffn_expert_wv"]),
        "wp": np.ascontiguousarray(inputs["L0_ffn_expert_proj"]),
    }
    x = np.ascontiguousarray(inputs["L0_attn_inputs"])
    return [dict(w, x=x[b]) for b in range(B)]


def kernel(**inputs):
    nc = build_nc()
    maps = _in_maps(inputs)
    res = run_bass_kernel_spmd(nc, maps, core_ids=list(range(B)))
    return np.stack([np.asarray(r["out"], dtype=np.float32) for r in res.results])


# version 13
# speedup vs baseline: 1.2845x; 1.0683x over previous
import math
from contextlib import ExitStack

import numpy as np
import ml_dtypes

import concourse.bass as bass
import concourse.mybir as mybir
from concourse.bass_utils import run_bass_kernel_spmd

F32 = mybir.dt.float32
BF16 = mybir.dt.bfloat16
AF = mybir.ActivationFunctionType
ALU = mybir.AluOpType
AX = mybir.AxisListType

B, T, C = 8, 2048, 1024
NCH = C // 128
QT = 4
TW = 512
NTT = T // 128
EPS = 1e-5

_DEBUG = []
_NTILES = NTT
_SKIP = set()
_P0CUT = "full"
_PHASES = {"p1", "p2", "p3", "p4"}
_P1CUT = "full"
_P4LIM = (4, 2, 16)



def _alibi_slopes(num_heads):
    closest_pow2 = 2 ** int(math.floor(math.log2(num_heads)))
    base = 2.0 ** (-(2.0 ** (-(math.log2(closest_pow2) - 3))))
    powers = np.arange(1, closest_pow2 + 1, dtype=np.float32)
    slopes = base ** powers
    if closest_pow2 != num_heads:
        eb = 2.0 ** (-(2.0 ** (-(math.log2(2 * closest_pow2) - 3))))
        ep = np.arange(1, 2 * (num_heads - closest_pow2) + 1, 2, dtype=np.float32)
        slopes = np.concatenate([slopes, eb ** ep])
    return slopes[:num_heads].astype(np.float32)


def _consts():
    c = {}
    c["identb"] = np.eye(128, dtype=ml_dtypes.bfloat16)
    c["identf"] = np.eye(128, dtype=np.float32)
    D = 64
    inv_freq = 1.0 / (10000.0 ** (np.arange(0, D, 2, dtype=np.float64) / D))
    freqs = np.outer(np.arange(T, dtype=np.float64), inv_freq)
    emb = np.concatenate([freqs, freqs], 1)
    cos = np.cos(emb).astype(np.float32)
    sin = np.sin(emb).astype(np.float32)
    ssin = sin.copy()
    ssin[:, :32] *= -1.0
    c["cosb"] = cos.T[np.tile(np.arange(64), 2)].astype(ml_dtypes.bfloat16)
    c["sinb"] = ssin.T[np.tile(np.arange(64), 2)].astype(ml_dtypes.bfloat16)
    tri = (np.arange(128)[:, None] <= np.arange(128)[None, :])
    c["tri01"] = tri.astype(ml_dtypes.bfloat16)
    c["masktri"] = np.where(tri, 0.0, np.float32(-1e30)).astype(np.float32)
    kidx = np.arange(T)
    c["kbias"] = np.stack([kidx // 64, kidx % 64]).astype(ml_dtypes.bfloat16)
    c["qbias"] = np.stack([-(kidx // 64), -(kidx % 64)]).astype(ml_dtypes.bfloat16)
    sel = np.zeros((2, 128), dtype=ml_dtypes.bfloat16)
    sel[0, 0:64] = 1.0
    sel[1, 64:128] = 1.0
    c["sel01"] = sel
    sl = _alibi_slopes(32)
    slb = sl.astype(ml_dtypes.bfloat16).astype(np.float64)
    qr_ = np.zeros((128, T), np.float64)
    kr_ = np.zeros((128, T), np.float64)
    for h in range(32):
        qr_[4 * h + 0] = 64.0 * slb[h]
        qr_[4 * h + 1] = slb[h]
        qr_[4 * h + 2] = -(kidx // 64)
        qr_[4 * h + 3] = -(kidx % 64)
        kr_[4 * h + 0] = kidx // 64
        kr_[4 * h + 1] = kidx % 64
        kr_[4 * h + 2] = 64.0 * slb[h]
        kr_[4 * h + 3] = slb[h]
    c["qextr"] = qr_.astype(ml_dtypes.bfloat16)
    c["kextr"] = kr_.astype(ml_dtypes.bfloat16)
    selp3 = np.zeros((32, 16 * 128), np.float32)
    for p in range(16):
        for m in range(128):
            selp3[2 * p + m // 64, 128 * p + m] = 1.0
    c["selp3"] = selp3.astype(ml_dtypes.bfloat16)
    selp1 = np.zeros((16, 8 * 128), np.float32)
    for p in range(8):
        for m in range(128):
            selp1[2 * p + m // 64, 128 * p + m] = 1.0
    c["selp1"] = selp1.astype(ml_dtypes.bfloat16)
    return c



class _Sem:
    def __init__(self, h):
        self.h = h
        self.n = 0


class _FakeInstr:
    def then_inc(self, *a, **k):
        return self


_FAKE = _FakeInstr()


class _Eng:
    def __init__(self, plan, name):
        self._p = plan
        self._n = name
        self._last_wait = {}

    def wait(self, sem, val=None):
        v = sem.n if val is None else val
        if v <= 0:
            return
        if self._last_wait.get(id(sem), -1) >= v:
            return
        self._last_wait[id(sem)] = v
        r = self._p.real if self._p.active == self._n else None
        if r is not None:
            r.wait_ge(sem.h, v)

    def inc(self, instr, sem, k=1):
        sem.n += k
        if instr is not _FAKE:
            instr.then_inc(sem.h, k)
        return instr

    def __getattr__(self, op):
        if self._p.active == self._n:
            return getattr(self._p.real, op)
        return lambda *a, **k: _FAKE


class Plan:
    def __init__(self, active, real):
        self.active = active
        self.real = real
        self.t = _Eng(self, "tensor")
        self.v = _Eng(self, "vector")
        self.s = _Eng(self, "scalar")
        self.g = _Eng(self, "gpsimd")
        self.y = _Eng(self, "sync")



def build_nc():
    nc = bass.Bass("TRN2", target_bir_lowering=False, detect_race_conditions=False)

    io = {}
    io["x"] = nc.dram_tensor("x", [T, C], F32, kind="ExternalInput").ap()
    io["wqkv"] = nc.dram_tensor("wqkv", [C, 4096], F32, kind="ExternalInput").ap()
    io["wout"] = nc.dram_tensor("wout", [2048, C], F32, kind="ExternalInput").ap()
    io["ln1w"] = nc.dram_tensor("ln1w", [C], F32, kind="ExternalInput").ap()
    io["ln1b"] = nc.dram_tensor("ln1b", [C], F32, kind="ExternalInput").ap()
    io["ln2w"] = nc.dram_tensor("ln2w", [C], F32, kind="ExternalInput").ap()
    io["ln2b"] = nc.dram_tensor("ln2b", [C], F32, kind="ExternalInput").ap()
    io["convw"] = nc.dram_tensor("convw", [C, 3], F32, kind="ExternalInput").ap()
    io["wattn"] = nc.dram_tensor("wattn", [C, 3072], F32, kind="ExternalInput").ap()
    io["wproj"] = nc.dram_tensor("wproj", [C, C], F32, kind="ExternalInput").ap()
    io["wrout"] = nc.dram_tensor("wrout", [C, 4], F32, kind="ExternalInput").ap()
    io["wv"] = nc.dram_tensor("wv", [4, C, 8192], F32, kind="ExternalInput").ap()
    io["wp"] = nc.dram_tensor("wp", [4, 4096, C], F32, kind="ExternalInput").ap()
    io["out"] = nc.dram_tensor("out", [T, C], F32, kind="ExternalOutput").ap()

    cst = {k: nc.inline_tensor(np.ascontiguousarray(v), "c_" + k).ap()
           for k, v in _consts().items()}

    dr = {
        "xnbf": nc.dram_tensor("d_xnbf", [C, T], BF16).ap(),
        "h2": nc.dram_tensor("d_h2", [C, T], BF16).ap(),
        "apair": nc.dram_tensor("d_apair", [16, 128, T], BF16).ap(),
        "bpair": nc.dram_tensor("d_bpair", [8, 128, T], BF16).ap(),
        "masks": nc.dram_tensor("d_masks", [4, T], BF16).ap(),
        "dn1": nc.dram_tensor("d_dn1", [16, T], F32).ap(),
        "dn3": nc.dram_tensor("d_dn3", [32, T], F32).ap(),
    }

    dbg = {}
    def _dbg(name, shape, dtype=F32):
        dbg[name] = nc.dram_tensor("dbg_" + name, shape, dtype, kind="ExternalOutput").ap()
    if "p0" in _DEBUG:
        _dbg("xn_fm", [C, T])
        _dbg("h1_fm", [C, T], BF16)
    if "p1" in _DEBUG:
        _dbg("hc_fm", [C, T], BF16)
        _dbg("acc1", [C, T])
    if "p2" in _DEBUG:
        _dbg("h2", [C, T], BF16)
        _dbg("masks", [4, T], BF16)
    if "p3" in _DEBUG:
        _dbg("acc3", [C, T])
    if "p4" in _DEBUG:
        _dbg("acc4", [C, T])

    slopes = _alibi_slopes(32)

    with ExitStack() as ctx:
        def sb(name, shape, dtype):
            return ctx.enter_context(nc.sbuf_tensor(name, shape, dtype))

        def psum(name, shape, dtype=F32):
            return ctx.enter_context(nc.psum_tensor(name, shape, dtype))

        sems = {n: _Sem(ctx.enter_context(nc.semaphore(n)))
                for n in ["tS", "vS", "sS", "gD", "yD", "g2",
                          "yX0", "yX1", "yS0", "yS1", "gW0", "gW1", "gW2"]}

        ACC = sb("ACC", [128, NCH * T], F32)
        XNBF = sb("XNBF", [128, NCH * T], BF16)
        HC = sb("HC", [128, NCH * T], BF16)
        AR = sb("AR", [128, 24 * 1024], BF16)
        ARF = sb("ARF", [128, 4096], F32)
        IDB = sb("IDB", [128, 128], BF16)
        IDF = sb("IDF", [128, 128], F32)
        TRI = sb("TRI", [128, 128], BF16)
        MTRI = sb("MTRI", [128, 128], F32)
        SELA = sb("SELA", [1, 128], BF16)
        SELB = sb("SELB", [1, 128], BF16)
        SELP3 = sb("SELP3", [32, 16 * 128], BF16)
        SELP1 = sb("SELP1", [16, 8 * 128], BF16)
        RC3 = sb("RC3", [32, T], BF16)
        ONESB = sb("ONESB", [128, 1], BF16)
        ONESF = sb("ONESF", [128, 1], F32)
        ONE1F = sb("ONE1F", [1, 128], F32)
        ONE1B = sb("ONE1B", [1, 128], BF16)
        LN1WC = sb("LN1WC", [128, NCH], F32)
        LN1BC = sb("LN1BC", [128, NCH], F32)
        LN2WC = sb("LN2WC", [128, NCH], F32)
        LN2BC = sb("LN2BC", [128, NCH], F32)
        CONVW = sb("CONVW", [128, 3 * NCH], F32)
        ROUTW = sb("ROUTW", [128, 5 * NCH], F32)
        ONESM = sb("ONESM", [128, 128], F32)
        EPSC = sb("EPSC", [128, 1], F32)
        STC = sb("STC", [128, 64], F32)

        PS = [psum(f"PS{i}", [128, TW], F32) for i in range(8)]
        PD = PS[7]

        def fm(tile, ch, cols):
            return tile[:, 2048 * ch + cols.start: 2048 * ch + cols.stop]

        def plan(p: Plan):
            for s_ in sems.values():
                s_.n = 0
            t, v, s, g, y = p.t, p.v, p.s, p.g, p.y
            tS, vS, sS, gD, yD, g2 = (sems[n] for n in ["tS", "vS", "sS", "gD", "yD", "g2"])
            yX = [sems["yX0"], sems["yX1"]]
            yS = [sems["yS0"], sems["yS1"]]
            gW = [sems["gW0"], sems["gW1"], sems["gW2"]]

            def ydma_on(sem, dst, src):
                y.inc(y.dma_start(dst, src), sem, 16)
                return sem.n

            def ydma(dst, src):
                y.inc(y.dma_start(dst, src), yD, 16)
                return yD.n

            def gdma(dst, src):
                g.inc(g.dma_start(dst, src), gD, 16)
                return gD.n

            def ydma_nc(dst, src):
                y.inc(y.dma_start(dst, src, allow_slow_non_contiguous=True), yD, 16)
                return yD.n

            def vbar(instr):
                v.inc(instr, vS, 1)
                v.wait(vS, vS.n)

            def sbar(instr):
                s.inc(instr, sS, 1)
                s.wait(sS, sS.n)

            for dst, key in [(IDB, "identb"), (IDF, "identf"), (TRI, "tri01"),
                             (MTRI, "masktri")]:
                ydma(dst[:], cst[key])
            ydma(SELA[:], cst["sel01"][0:1, :])
            ydma(SELB[:], cst["sel01"][1:2, :])
            ydma(SELP3[:], cst["selp3"])
            ydma(SELP1[:], cst["selp1"])
            ydma_nc(LN1WC[:], io["ln1w"].rearrange("(c p) -> p c", p=128))
            ydma_nc(LN1BC[:], io["ln1b"].rearrange("(c p) -> p c", p=128))
            ydma_nc(LN2WC[:], io["ln2w"].rearrange("(c p) -> p c", p=128))
            ydma_nc(LN2BC[:], io["ln2b"].rearrange("(c p) -> p c", p=128))
            ydma_nc(CONVW[:].rearrange("p (c k) -> p c k", k=3),
                 io["convw"].rearrange("(c p) k -> p c k", p=128))
            ydma_nc(ROUTW[:].rearrange("p (c e) -> p c e", e=5)[:, :, 0:4],
                 io["wrout"].rearrange("(c p) e -> p c e", p=128))
            ld_consts = yD.n
            slab_free = [0, 0]
            bank_free_s = [0] * 8
            bank_free_v = [0] * 8
            v.memset(ONESB[:], 1.0)
            v.memset(ONESF[:], 1.0)
            v.memset(ONE1F[:], 1.0)
            v.memset(ONESM[:], 1.0)
            v.memset(ROUTW[:].rearrange("p (c e) -> p c e", e=5)[:, :, 4:5], 1.0)
            v.memset(ONE1B[:], 1.0)
            v.inc(v.memset(EPSC[:], EPS), vS, 1)
            v_consts = vS.n
            t.wait(yD, ld_consts)
            t.wait(vS, v_consts)
            v.wait(yD, ld_consts)
            s.wait(yD, ld_consts)
            XT = [ARF[:, 0:1024], ARF[:, 1024:2048]]
            DG0 = [ARF[:, 2048 + 256 * p: 2048 + 256 * p + 128] for p in range(2)]
            DG1 = [ARF[:, 2176 + 256 * p: 2176 + 256 * p + 128] for p in range(2)]
            H1FM_BASE = 8192

            def h1fm(ch, cols):
                o = H1FM_BASE + 2048 * ch
                return AR[:, o + cols.start: o + cols.stop]

            def fm3(tile, cols):
                return tile[:].rearrange("p (c t) -> p c t", c=NCH)[:, :, cols]

            xnbanks = [[PS[0], PS[1]], [PS[4], PS[5]]]
            h1banks = [[PS[2], PS[3]], [PS[6], PS[7]]]
            ld_x = [0] * (NTT + 1)
            ld_x[0] = ydma_on(yX[0], XT[0][:], io["x"][0:128, :])
            done_t = [0] * NTT
            ev_xn = [0] * NTT
            ev_h1 = [0] * NTT

            def p0_evac(i):
                cw = slice(128 * i, 128 * i + 128)
                xb_, hb_ = xnbanks[i % 2], h1banks[i % 2]
                v.wait(tS, done_t[i])
                for half in range(2):
                    for k in range(4):
                        ch = 4 * half + k
                        v.inc(v.tensor_copy(fm(ACC, ch, cw),
                                            xb_[half][:, 128 * k:128 * k + 128]),
                              vS, 1)
                ev_xn[i] = vS.n
                g.wait(vS, ev_xn[i])
                gdma(fm3(XNBF, cw), fm3(ACC, cw))
                s.wait(tS, done_t[i])
                for half in range(2):
                    for k in range(4):
                        ch = 4 * half + k
                        s.inc(s.activation(h1fm(ch, cw),
                                           hb_[half][:, 128 * k:128 * k + 128],
                                           AF.Identity, scale=LN1WC[:, ch:ch + 1],
                                           bias=LN1BC[:, ch:ch + 1]), sS, 1)
                ev_h1[i] = sS.n

            for i in range(_NTILES):
                xt = XT[i % 2]
                o = 32 * (i % 2)
                if i + 1 < _NTILES:
                    if i >= 1:
                        y.wait(tS, done_t[i - 1])
                    ld_x[i + 1] = ydma_on(yX[(i + 1) % 2], XT[(i + 1) % 2][:],
                                          io["x"][128 * (i + 1):128 * (i + 2), :])
                v.wait(yX[i % 2], ld_x[i])
                v.bn_stats(STC[:, o + 0:o + 6], xt[:, 0:512])
                v.inc(v.bn_stats(STC[:, o + 6:o + 12], xt[:, 512:1024]), vS, 1)
                v.wait(vS, vS.n)
                v.inc(v.bn_aggr(STC[:, o + 12:o + 14], STC[:, o + 0:o + 12]), vS, 1)
                v.wait(vS, vS.n)
                aggr_v = vS.n
                v.inc(v.tensor_scalar(xt[:], xt[:], STC[:, o + 12:o + 13], None,
                                      ALU.subtract), vS, 1)
                s.wait(vS, aggr_v)
                s.inc(s.activation(STC[:, o + 14:o + 15], STC[:, o + 13:o + 14],
                                   AF.Sqrt, bias=EPSC[:]), sS, 1)
                v.wait(sS, sS.n)
                vbar(v.reciprocal(STC[:, o + 15:o + 16], STC[:, o + 14:o + 15]))
                v.inc(v.tensor_scalar(STC[:, o + 16:o + 17], STC[:, o + 13:o + 14],
                                      STC[:, o + 15:o + 16], STC[:, o + 15:o + 16],
                                      ALU.mult, ALU.mult), vS, 1)
                s.wait(vS, vS.n)
                s.inc(s.activation(STC[:, o + 17:o + 18], STC[:, o + 16:o + 17],
                                   AF.Sqrt, bias=EPSC[:]), sS, 1)
                v.wait(sS, sS.n)
                vbar(v.reciprocal(STC[:, o + 18:o + 19], STC[:, o + 17:o + 18]))
                vbar(v.tensor_mul(STC[:, o + 19:o + 20], STC[:, o + 15:o + 16],
                                  STC[:, o + 18:o + 19]))
                if i >= 2:
                    v.wait(tS, done_t[i - 2])
                v.tensor_scalar(DG0[i % 2], IDF[:], STC[:, o + 15:o + 16], None,
                                ALU.mult)
                v.inc(v.tensor_scalar(DG1[i % 2], IDF[:], STC[:, o + 19:o + 20],
                                      None, ALU.mult), vS, 1)
                diag_v = vS.n
                if _P0CUT == "stats":
                    done_t[i] = tS.n
                    ev_xn[i] = vS.n
                    ev_h1[i] = sS.n
                    continue
                t.wait(vS, diag_v)
                if i >= 2:
                    t.wait(vS, ev_xn[i - 2])
                    t.wait(sS, ev_h1[i - 2])
                xb_, hb_ = xnbanks[i % 2], h1banks[i % 2]
                for half in range(2):
                    for k in range(4):
                        ch = 4 * half + k
                        t.matmul(xb_[half][:, 128 * k:128 * k + 128],
                                 xt[:, 128 * ch:128 * ch + 128], DG0[i % 2],
                                 start=(k == 0), stop=(k == 3))
                for half in range(2):
                    for k in range(4):
                        ch = 4 * half + k
                        t.inc(t.matmul(hb_[half][:, 128 * k:128 * k + 128],
                                       xt[:, 128 * ch:128 * ch + 128], DG1[i % 2],
                                       start=(k == 0), stop=(k == 3)), tS, 1)
                done_t[i] = tS.n
                if i >= 1:
                    p0_evac(i - 1)
            if _P0CUT != "stats":
                p0_evac(_NTILES - 1)
            xnbf_gd = gD.n
            for b in range(8):
                bank_free_v[b] = vS.n
                bank_free_s[b] = sS.n

            if "p0" in _DEBUG:
                y.wait(vS, vS.n)
                y.wait(sS, sS.n)
                ydma(dbg["xn_fm"].rearrange("(c p) t -> p c t", p=128),
                     ACC[:].rearrange("p (c t) -> p c t", c=NCH))
                ydma(dbg["h1_fm"].rearrange("(c p) t -> p c t", p=128),
                     AR[:, H1FM_BASE:H1FM_BASE + NCH * 2048].rearrange(
                         "p (c t) -> p c t", c=NCH))

            if "p1" in _PHASES:
                v.wait(sS, sS.n)
                for ch in range(NCH):
                    w0 = CONVW[:, 3 * ch + 0:3 * ch + 1]
                    w1 = CONVW[:, 3 * ch + 1:3 * ch + 2]
                    w2 = CONVW[:, 3 * ch + 2:3 * ch + 3]
                    h1c = lambda a, b_: AR[:, H1FM_BASE + 2048 * ch + a:
                                           H1FM_BASE + 2048 * ch + b_]
                    hcc = lambda a, b_: HC[:, 2048 * ch + a:2048 * ch + b_]
                    vbar(v.tensor_scalar(hcc(0, T), h1c(0, T), w2, None, ALU.mult))
                    vbar(v.scalar_tensor_tensor(hcc(1, T), h1c(0, T - 1), w1,
                                                hcc(1, T), ALU.mult, ALU.add))
                    vbar(v.scalar_tensor_tensor(hcc(2, T), h1c(0, T - 2), w0,
                                                hcc(2, T), ALU.mult, ALU.add))
                conv_done = vS.n
                if "p1" in _DEBUG:
                    y.wait(vS, conv_done)
                    ydma(dbg["hc_fm"].rearrange("(c p) t -> p c t", p=128),
                         HC[:].rearrange("p (c t) -> p c t", c=NCH))

                A_SLB = [0, 1024]
                A_QS, A_KS, A_VS = 2048, 4096, 6144
                A_COS, A_SIN = 8192, 10240
                A_RT, A_QR, A_KR, A_VT = 12288, 14336, 16384, 18432
                A_PB = [[20512 + 512 * k for k in range(4)],
                        [12288 + 512 * k for k in range(4)]]
                A_RC = 2048
                A_OUT = [3072, 3584]
                y.wait(vS, conv_done)
                ld_cos = ydma(AR[:, A_COS:A_COS + T], cst["cosb"])
                ld_cos = ydma(AR[:, A_SIN:A_SIN + T], cst["sinb"])

                slab_free[0] = slab_free[1] = tS.n
                spill = [0, 0]
                dnsp = [0, 0]

                def qkv_chunk(oc, dst_base, slot, scale=1.0):
                    src = io["wattn"].rearrange("(kc p) n -> p kc n", p=128)
                    wsem = gW[0] if slot == 0 else gW[2]
                    g.wait(tS, slab_free[slot])
                    g.inc(g.dma_start(AR[:, A_SLB[slot]:A_SLB[slot] + 1024]
                                      .rearrange("p (kc n) -> p kc n", n=128),
                                      src[:, :, 128 * oc:128 * oc + 128]), wsem, 16)
                    ldw = wsem.n
                    for qt in range(QT):
                        bank = qt % 4
                        qtw = slice(512 * qt, 512 * qt + 512)
                        t.wait(wsem, ldw)
                        t.wait(vS, conv_done)
                        t.wait(sS, bank_free_s[bank])
                        t.wait(vS, bank_free_v[bank])
                        for kc in range(NCH):
                            t.inc(t.matmul(
                                PS[bank][:],
                                AR[:, A_SLB[slot] + 128 * kc:A_SLB[slot] + 128 * kc + 128],
                                fm(HC, kc, qtw),
                                start=(kc == 0), stop=(kc == NCH - 1)), tS, 1)
                        mm = tS.n
                        s.wait(tS, mm)
                        if scale == 1.0:
                            s.inc(s.copy(AR[:, dst_base + 512 * qt:dst_base + 512 * qt + 512],
                                         PS[bank][:]), sS, 1)
                        else:
                            s.inc(s.activation(
                                AR[:, dst_base + 512 * qt:dst_base + 512 * qt + 512],
                                PS[bank][:], AF.Copy, scale=scale), sS, 1)
                        bank_free_s[bank] = sS.n
                    slab_free[slot] = tS.n

                def rope(src_base, dst_base):
                    g.wait(sS, sS.n)
                    g.wait(tS, tS.n)
                    for (dp, sp_) in [(0, 32), (32, 0), (64, 96), (96, 64)]:
                        g.inc(g.dma_start(AR[dp:dp + 32, A_RT:A_RT + T],
                                          AR[sp_:sp_ + 32, src_base:src_base + T]),
                              gW[1], 16)
                    rot_ld = gW[1].n
                    v.wait(gW[1], rot_ld)
                    v.wait(sS, sS.n)
                    vbar(v.tensor_mul(AR[:, dst_base:dst_base + T],
                                      AR[:, src_base:src_base + T],
                                      AR[:, A_COS:A_COS + T]))
                    vbar(v.tensor_mul(AR[:, A_RT:A_RT + T],
                                      AR[:, A_RT:A_RT + T],
                                      AR[:, A_SIN:A_SIN + T]))
                    v.inc(v.tensor_add(AR[:, dst_base:dst_base + T],
                                       AR[:, dst_base:dst_base + T],
                                       AR[:, A_RT:A_RT + T]), vS, 1)

                npairs = 0 if _P1CUT == "conv" else (1 if _P1CUT in (
                    "qkv1", "rope1", "vt1", "attn1") else 8)
                for p_ in range(npairs):
                    qkv_chunk(p_, A_QS, 0, scale=0.125)
                    qkv_chunk(8 + p_, A_KS, 1)
                    qkv_chunk(16 + p_, A_VS, 0)
                    if _P1CUT == "qkv1":
                        continue
                    v.wait(yD, ld_cos)
                    rope(A_QS, A_QR)
                    rope(A_KS, A_KR)
                    rope_done = vS.n
                    if _P1CUT == "rope1":
                        continue
                    v.wait(sS, sS.n)
                    v.inc(v.memset(AR[:, A_VT + 64:A_VT + 64 + 1]
                                   .rearrange("p (a b) -> p a b", a=1) if False else
                                   AR[:, A_VT:A_VT + 2080]
                                   .rearrange("p (kt c) -> p kt c", c=130)[:, :, 64:65],
                                   1.0), vS, 1)
                    v.inc(v.memset(AR[:, A_VT:A_VT + 2080]
                                   .rearrange("p (kt c) -> p kt c", c=130)[:, :, 129:130],
                                   1.0), vS, 1)
                    ones_done = vS.n
                    for half in range(2):
                        bank = 4 + half
                        t.wait(sS, sS.n)
                        t.wait(vS, bank_free_v[bank])
                        for k in range(8):
                            kt = 8 * half + k
                            if k % 4 == 0:
                                t.wait(sS, bank_free_s[bank])
                            t.inc(t.matmul(PS[bank][:, 128 * (k % 4):128 * (k % 4) + 128],
                                           AR[:, A_VS + 128 * kt:A_VS + 128 * kt + 128],
                                           IDB[:],
                                           start=(k % 4 == 0), stop=(k % 4 == 3)), tS, 1)
                            if k % 4 == 3:
                                mm = tS.n
                                s.wait(tS, mm)
                                kt0 = kt - 3
                                for hh in range(2):
                                    s.inc(s.copy(
                                        AR[:, A_VT + 130 * kt0:A_VT + 130 * (kt0 + 4)]
                                        .rearrange("p (kt c) -> p kt c", c=130)
                                        [:, :, 65 * hh:65 * hh + 64],
                                        PS[bank][:].rearrange("p (kt c) -> p kt c", c=128)
                                        [:, :, 64 * hh:64 * hh + 64]), sS, 1)
                                bank_free_s[bank] = sS.n
                    vt_done = sS.n
                    if _P1CUT == "vt1":
                        continue
                    SCB = [[0, 1], [6, 7]]
                    for qt in range(QT):
                        qtw = slice(512 * qt, 512 * qt + 512)
                        nkt = 4 * qt + 4
                        pb_free = [[0] * 4, [0] * 4]
                        pvb = [[2, 4][qt % 2], [3, 5][qt % 2]]
                        exp_done_at = [[0] * 4, [0] * 4]
                        tri_done_at = [[None] * 4, [None] * 4]

                        def p1_pv_wave(kt):
                            off = 128 * (kt - 4 * qt) if kt >= 4 * qt else 0
                            for hh in range(2):
                                pv_bank = pvb[hh]
                                pb = A_PB[hh][kt % 4]
                                t.wait(sS, exp_done_at[hh][kt % 4])
                                tri = tri_done_at[hh][kt % 4]
                                if tri:
                                    t.wait(vS, tri)
                                if kt == 0:
                                    t.wait(sS, bank_free_s[pv_bank])
                                    t.wait(vS, bank_free_v[pv_bank])
                                t.inc(t.matmul(
                                    PS[pv_bank][0:65, slice(off, 512)],
                                    AR[:, A_VT + 130 * kt + 65 * hh:
                                       A_VT + 130 * kt + 65 * hh + 65],
                                    AR[:, pb + off:pb + 512],
                                    start=(kt == 0), stop=(kt == nkt - 1)), tS, 1)
                                pb_free[hh][kt % 4] = tS.n

                        for kt in range(nkt):
                            off = 128 * (kt - 4 * qt) if kt >= 4 * qt else 0
                            w_ = slice(off, 512)
                            for hh in range(2):
                                sc_bank = SCB[hh][kt % 2]
                                t.wait(sS, bank_free_s[sc_bank])
                                t.wait(vS, bank_free_v[sc_bank])
                                t.wait(sS, vt_done)
                                t.wait(vS, max(rope_done, ones_done))
                                t.inc(t.matmul(
                                    PS[sc_bank][:, w_],
                                    AR[64 * hh:64 * hh + 64,
                                       A_KR + 128 * kt:A_KR + 128 * kt + 128],
                                    AR[64 * hh:64 * hh + 64,
                                       A_QR + 512 * qt + off:A_QR + 512 * qt + 512],
                                    start=True, stop=True), tS, 1)
                                sc_mm = tS.n
                                s.wait(tS, sc_mm)
                                s.wait(tS, pb_free[hh][kt % 4])
                                pb = A_PB[hh][kt % 4]
                                s.inc(s.activation(AR[:, pb + off:pb + 512],
                                                   PS[sc_bank][:, w_], AF.Exp), sS, 1)
                                exp_done_at[hh][kt % 4] = sS.n
                                bank_free_s[sc_bank] = sS.n
                                if kt >= 4 * qt:
                                    v.wait(sS, sS.n)
                                    v.inc(v.tensor_mul(AR[:, pb + off:pb + off + 128],
                                                       AR[:, pb + off:pb + off + 128],
                                                       TRI[:]), vS, 1)
                                    tri_done_at[hh][kt % 4] = vS.n
                                else:
                                    tri_done_at[hh][kt % 4] = None
                            if kt >= 1:
                                p1_pv_wave(kt - 1)
                        p1_pv_wave(nkt - 1)
                        pv_all = tS.n
                        dcol = 512 * (qt % 2)
                        v.wait(tS, pv_all)
                        v.wait(yX[qt % 2], dnsp[qt % 2])
                        v.inc(v.tensor_copy(ARF[0:1, dcol:dcol + 512],
                                            PS[pvb[0]][64:65, :]), vS, 1)
                        v.inc(v.tensor_copy(ARF[32:33, dcol:dcol + 512],
                                            PS[pvb[1]][64:65, :]), vS, 1)
                        outb = A_OUT[qt % 2]
                        v.wait(yS[qt % 2], spill[qt % 2])
                        v.inc(v.tensor_copy(AR[0:64, outb:outb + 512],
                                            PS[pvb[0]][0:64, :]), vS, 1)
                        v.inc(v.tensor_copy(AR[64:128, outb:outb + 512],
                                            PS[pvb[1]][0:64, :]), vS, 1)
                        ev_done = vS.n
                        for hh in range(2):
                            bank_free_s[pvb[hh]] = sS.n
                            bank_free_v[pvb[hh]] = ev_done
                        y.wait(vS, ev_done)
                        spill[qt % 2] = ydma_on(yS[qt % 2],
                                                dr["bpair"][p_, :, 512 * qt:512 * qt + 512],
                                                AR[:, outb:outb + 512])
                        ydma_on(yX[qt % 2],
                                dr["dn1"][2 * p_:2 * p_ + 1, 512 * qt:512 * qt + 512],
                                ARF[0:1, dcol:dcol + 512])
                        dnsp[qt % 2] = ydma_on(
                            yX[qt % 2],
                            dr["dn1"][2 * p_ + 1:2 * p_ + 2, 512 * qt:512 * qt + 512],
                            ARF[32:33, dcol:dcol + 512])

                    s.wait(vS, vS.n)
                    s.wait(tS, tS.n)
                    s.wait(yS[0], spill[0])
                    s.wait(yS[1], spill[1])

                if _P1CUT != "full":
                    return
                A_PJ, A_BP = 0, 8192
                g.wait(tS, tS.n)
                g.wait(vS, vS.n)
                ld_pj = gdma(AR[:, A_PJ:A_PJ + 8192]
                             .rearrange("p (r n) -> p r n", n=1024),
                             io["wproj"].rearrange("(r p) n -> p r n", p=128))
                y.wait(yX[0], yX[0].n)
                y.wait(yX[1], yX[1].n)
                y.wait(vS, vS.n)
                ld_dn = ydma(ARF[0:16, 0:2048], dr["dn1"])
                v.wait(yD, ld_dn)
                vbar(v.reciprocal(ARF[0:16, 2048:4096], ARF[0:16, 0:2048]))
                v.inc(v.tensor_copy(RC3[0:16, :], ARF[0:16, 2048:4096]), vS, 1)
                rc1_done = vS.n
                for qt in range(QT):
                    qtw = slice(512 * qt, 512 * qt + 512)
                    y.wait(tS, tS.n)
                    ld_bp = 0
                    for p_ in range(8):
                        ld_bp = ydma(AR[:, A_BP + 512 * p_:A_BP + 512 * p_ + 512],
                                     dr["bpair"][p_, :, 512 * qt:512 * qt + 512])
                    for p_ in range(8):
                        bank = 4 + (p_ % 2)
                        t.wait(yD, ld_bp)
                        t.wait(vS, rc1_done)
                        t.wait(vS, bank_free_v[bank])
                        t.inc(t.matmul(PS[bank][:],
                                       SELP1[:, 128 * p_:128 * p_ + 128],
                                       RC3[0:16, 512 * qt:512 * qt + 512],
                                       start=True, stop=True), tS, 1)
                        bmm = tS.n
                        v.wait(tS, bmm)
                        v.inc(v.tensor_mul(AR[:, A_BP + 512 * p_:A_BP + 512 * p_ + 512],
                                           AR[:, A_BP + 512 * p_:A_BP + 512 * p_ + 512],
                                           PS[bank][:]), vS, 1)
                        bank_free_v[bank] = vS.n
                    norm_done = vS.n
                    for cg in range(2):
                        for cc in range(4):
                            c_ = 4 * cg + cc
                            bank = cc
                            t.wait(gD, ld_pj)
                            t.wait(vS, norm_done)
                            t.wait(vS, bank_free_v[bank])
                            t.wait(sS, bank_free_s[bank])
                            for p_ in range(8):
                                t.inc(t.matmul(
                                    PS[bank][:],
                                    AR[:, A_PJ + 1024 * p_ + 128 * c_:
                                       A_PJ + 1024 * p_ + 128 * c_ + 128],
                                    AR[:, A_BP + 512 * p_:A_BP + 512 * p_ + 512],
                                    start=(p_ == 0), stop=(p_ == 7)), tS, 1)
                            mm = tS.n
                            v.wait(tS, mm)
                            v.inc(v.tensor_add(fm(ACC, c_, qtw), fm(ACC, c_, qtw),
                                               PS[bank][:]), vS, 1)
                            bank_free_v[bank] = vS.n

                if "p1" in _DEBUG:
                    y.wait(vS, vS.n)
                    ydma(dbg["acc1"].rearrange("(c p) t -> p c t", p=128),
                         ACC[:].rearrange("p (c t) -> p c t", c=NCH))


            if "p2" in _PHASES:
                A_H2F = [0, 512]
                A_XSQ = [1024, 1536]
                M4 = ARF[32:33, 2048:2560]
                M5 = ARF[32:33, 2560:3072]
                M6 = ARF[32:33, 3072:3584]
                M7 = ARF[32:33, 3584:4096]
                LB = ARF[0:4, 2048:2560]
                Lr = [ARF[0:1, 2048 + 512 * e: 2560 + 512 * e] for e in range(4)]
                X0 = ARF[0:1, 1024:1536]
                X1 = ARF[0:1, 1536:2048]
                CW = ARF[64:66, 2048:2052]
                SDMU = ARF[64:66, 2560:3072]
                R5 = lambda c_: ROUTW[:, 5 * c_:5 * c_ + 4]
                h2st_free = [0, 0]
                p2_v0 = vS.n
                t.wait(vS, bank_free_v[4])
                t.wait(sS, bank_free_s[4])
                for c_ in range(NCH):
                    t.inc(t.matmul(PS[4][0:4, 0:1], R5(c_), LN2BC[:, c_:c_ + 1],
                                   start=(c_ == 0), stop=(c_ == NCH - 1)), tS, 1)
                const_mm = tS.n
                s.wait(tS, const_mm)
                s.inc(s.copy(ARF[0:4, 2048:2049], PS[4][0:4, 0:1]), sS, 1)
                cst_s = sS.n
                v.wait(tS, const_mm)
                for c_ in range(NCH):
                    v.inc(v.tensor_scalar(R5(c_), R5(c_), LN2WC[:, c_:c_ + 1],
                                          None, ALU.mult), vS, 1)
                fold_v = vS.n
                t.wait(vS, fold_v)
                for c_ in range(NCH):
                    t.inc(t.matmul(PS[4][0:4, 1:2], R5(c_), ONESF[:],
                                   start=(c_ == 0), stop=(c_ == NCH - 1)), tS, 1)
                sp_mm = tS.n
                s.wait(tS, sp_mm)
                sbar(s.copy(ARF[0:4, 2049:2050], PS[4][0:4, 1:2]))
                v.wait(sS, sS.n)
                v.inc(v.tensor_scalar(ARF[0:4, 2049:2050], ARF[0:4, 2049:2050],
                                      -1.0, None, ALU.mult), vS, 1)
                negs_v = vS.n
                g.wait(sS, cst_s)
                g.wait(vS, negs_v)
                for e_ in range(4):
                    g.inc(g.dma_start(CW[0:1, e_:e_ + 1],
                                      ARF[e_:e_ + 1, 2048:2049]), gW[1], 16)
                    g.inc(g.dma_start(CW[1:2, e_:e_ + 1],
                                      ARF[e_:e_ + 1, 2049:2050]), gW[1], 16)
                cw_gd = gW[1].n
                bank_free_v[7] = vS.n
                sqmm = [0] * NCH
                corr_mm = tS.n
                amax_n = vS.n
                mask_gd = gD.n
                for qt in range(QT):
                    qtw = slice(512 * qt, 512 * qt + 512)
                    t.wait(sS, bank_free_s[6])
                    t.wait(vS, bank_free_v[6])
                    for c_ in range(NCH):
                        t.inc(t.matmul(PS[6][0:4, :], R5(c_),
                                       fm(ACC, c_, qtw),
                                       start=(c_ == 0), stop=(c_ == NCH - 1)), tS, 1)
                    t.wait(sS, bank_free_s[0])
                    t.wait(vS, bank_free_v[0])
                    for c_ in range(NCH):
                        t.inc(t.matmul(PS[0][0:1, :], ONESF[:], fm(ACC, c_, qtw),
                                       start=(c_ == 0), stop=(c_ == NCH - 1)), tS, 1)
                    rt_mm = tS.n
                    for c_ in range(NCH):
                        xq = ARF[:, A_XSQ[c_ % 2]:A_XSQ[c_ % 2] + 512]
                        if c_ >= 2:
                            s.wait(tS, sqmm[c_ - 2])
                        else:
                            s.wait(vS, max(p2_v0, amax_n))
                            s.wait(tS, corr_mm)
                        s.inc(s.activation(xq, fm(ACC, c_, qtw), AF.Square), sS, 1)
                        t.wait(sS, sS.n)
                        if c_ == 0:
                            t.wait(sS, bank_free_s[1])
                            t.wait(vS, bank_free_v[1])
                        t.inc(t.matmul(PS[1][0:1, :], ONESF[:], xq,
                                       start=(c_ == 0), stop=(c_ == NCH - 1)), tS, 1)
                        sqmm[c_] = tS.n
                    sq_mm = tS.n
                    s.wait(tS, rt_mm)
                    s.inc(s.copy(M6, PS[0][0:1, :]), sS, 1)
                    bank_free_s[0] = sS.n
                    s.wait(tS, sq_mm)
                    s.inc(s.copy(M7, PS[1][0:1, :]), sS, 1)
                    cp_s = sS.n
                    bank_free_s[1] = sS.n
                    v.wait(sS, cp_s)
                    v.tensor_scalar(M4, M6, 1.0 / C, None, ALU.mult)
                    v.inc(v.tensor_scalar(M7, M7, 1.0 / C, None, ALU.mult), vS, 1)
                    v.wait(vS, vS.n)
                    vbar(v.tensor_mul(M6, M4, M4))
                    vbar(v.tensor_sub(M6, M7, M6))
                    s.wait(vS, vS.n)
                    s.inc(s.activation(M5, M6, AF.Sqrt, bias=EPSC[32:33, :]), sS, 1)
                    sd_s = sS.n
                    v.wait(sS, sd_s)
                    vbar(v.reciprocal(M7, M5))
                    rb_done = vS.n
                    g.wait(sS, sd_s)
                    g.wait(tS, corr_mm)
                    g.inc(g.dma_start(SDMU[0:1, :], M5), gW[1], 16)
                    g.inc(g.dma_start(SDMU[1:2, :], M4), gW[1], 16)
                    sdmu_gd = gW[1].n
                    bmu, br2 = PS[2 + 2 * (qt % 2)], PS[3 + 2 * (qt % 2)]
                    t.wait(vS, bank_free_v[2 + 2 * (qt % 2)])
                    t.wait(vS, bank_free_v[3 + 2 * (qt % 2)])
                    t.wait(sS, bank_free_s[2 + 2 * (qt % 2)])
                    t.wait(sS, bank_free_s[3 + 2 * (qt % 2)])
                    t.wait(vS, rb_done)
                    t.inc(t.matmul(bmu[:], ONESM[32:33, :], M4,
                                   start=True, stop=True), tS, 1)
                    t.inc(t.matmul(br2[:], ONESM[32:33, :], M7,
                                   start=True, stop=True), tS, 1)
                    bc_mm = tS.n
                    t.wait(gW[1], sdmu_gd)
                    t.wait(vS, bank_free_v[7])
                    t.wait(sS, bank_free_s[7])
                    t.inc(t.matmul(PS[7][0:4, :], CW, SDMU,
                                   start=True, stop=True), tS, 1)
                    corr_mm = tS.n
                    s.wait(tS, rt_mm)
                    s.wait(gW[1], cw_gd)
                    s.wait(vS, amax_n)
                    s.wait(gD, mask_gd)
                    s.inc(s.copy(LB, PS[6][0:4, :]), sS, 1)
                    spread_s = sS.n
                    bank_free_s[6] = sS.n
                    v.wait(sS, spread_s)
                    v.wait(tS, corr_mm)
                    v.inc(v.tensor_add(LB, LB, PS[7][0:4, :]), vS, 1)
                    corr_add = vS.n
                    bank_free_v[7] = vS.n
                    g.wait(vS, corr_add)
                    for e_ in range(1, 4):
                        g.inc(g.dma_start(Lr[e_], ARF[e_:e_ + 1, 2048:2560]),
                              g2, 16)
                    spread_gd = g2.n
                    ap1 = [0] * NCH
                    ap2 = [0] * NCH
                    stage_n = [0, 0]

                    def p2_op2(c_):
                        hb = ARF[:, A_H2F[c_ % 2]:A_H2F[c_ % 2] + 512]
                        v.wait(vS, ap1[c_])
                        v.inc(v.tensor_mul(hb, hb, br2[:]), vS, 1)
                        ap2[c_] = vS.n
                        st_slot = c_ % 2
                        s.wait(vS, ap2[c_])
                        s.wait(yS[st_slot], h2st_free[st_slot])
                        s.inc(s.activation(AR[:, 512 * st_slot:512 * st_slot + 512],
                                           hb, AF.Identity,
                                           scale=LN2WC[:, c_:c_ + 1],
                                           bias=LN2BC[:, c_:c_ + 1]), sS, 1)
                        stage_n[st_slot] = sS.n
                        y.wait(sS, sS.n)
                        h2st_free[st_slot] = ydma_on(
                            yS[st_slot], dr["h2"][128 * c_:128 * c_ + 128, qtw],
                            AR[:, 512 * st_slot:512 * st_slot + 512])
                        if "p2" in _DEBUG:
                            h2st_free[st_slot] = ydma_on(
                                yS[st_slot], dbg["h2"][128 * c_:128 * c_ + 128, qtw],
                                AR[:, 512 * st_slot:512 * st_slot + 512])

                    for c_ in range(NCH):
                        hb = ARF[:, A_H2F[c_ % 2]:A_H2F[c_ % 2] + 512]
                        if c_ < 2:
                            t_dep = bc_mm
                            v.wait(tS, t_dep)
                        if c_ >= 2:
                            v.wait(sS, stage_n[c_ % 2])
                        v.inc(v.tensor_sub(hb, fm(ACC, c_, qtw), bmu[:]), vS, 1)
                        ap1[c_] = vS.n
                        if c_ >= 1:
                            p2_op2(c_ - 1)
                    p2_op2(NCH - 1)
                    bank_free_v[2 + 2 * (qt % 2)] = vS.n
                    bank_free_v[3 + 2 * (qt % 2)] = vS.n
                    v.wait(g2, spread_gd)
                    v.tensor_max(M6, Lr[0], Lr[1])
                    v.inc(v.tensor_max(M7, Lr[2], Lr[3]), vS, 1)
                    v.wait(vS, vS.n)
                    vbar(v.tensor_max(X0, M6, M7))
                    for e_ in range(3):
                        v.tensor_tensor(Lr[e_], Lr[e_], X0, ALU.is_ge)
                    v.inc(v.tensor_tensor(Lr[3], Lr[3], X0, ALU.is_ge), vS, 1)
                    v.wait(vS, vS.n)
                    v.tensor_scalar(X1, Lr[0], -1.0, 1.0, ALU.mult, ALU.add)
                    v.inc(v.tensor_scalar(X0, Lr[1], -1.0, 1.0, ALU.mult, ALU.add),
                          vS, 1)
                    v.wait(vS, vS.n)
                    v.tensor_mul(Lr[1], Lr[1], X1)
                    v.inc(v.tensor_mul(X1, X1, X0), vS, 1)
                    v.wait(vS, vS.n)
                    vbar(v.tensor_scalar(X0, Lr[2], -1.0, 1.0, ALU.mult, ALU.add))
                    v.tensor_mul(Lr[2], Lr[2], X1)
                    v.inc(v.tensor_mul(X1, X1, X0), vS, 1)
                    v.wait(vS, vS.n)
                    v.inc(v.tensor_mul(Lr[3], Lr[3], X1), vS, 1)
                    amax_n = vS.n
                    g.wait(vS, amax_n)
                    for e_ in range(4):
                        g.inc(g.dma_start(dr["masks"][e_:e_ + 1, qtw], Lr[e_]),
                              gD, 16)
                        if "p2" in _DEBUG:
                            g.inc(g.dma_start(dbg["masks"][e_:e_ + 1, qtw], Lr[e_]),
                                  gD, 16)
                    mask_gd = gD.n
                p2_mask_gd = mask_gd
                p2_h2_free = list(h2st_free)

            if "p3" in _PHASES:
                qext = lambda j_: 2048 * j_
                kext = lambda j_: 8192 + 2048 * j_
                A3_VS, A3_VT = 2048, 4096
                A3_PB = [6176 + 512 * k for k in range(8)]
                A3_QER, A3_KER = 10272, 12320
                A3_RC = 3072
                A3_OUT = [2048, 2560]
                slab_free[0] = slab_free[1] = tS.n
                dnsp = [yX[0].n, yX[1].n]
                t.wait(gD, xnbf_gd)
                ld_er = ydma(AR[:, A3_QER:A3_QER + T], cst["qextr"])
                ld_er = ydma(AR[:, A3_KER:A3_KER + T], cst["kextr"])
                p3_sp = [0, 0]
                for g_ in range(8):
                    s.wait(yS[0], p3_sp[0])
                    s.wait(yS[1], p3_sp[1])
                    wsrc = io["wqkv"].rearrange("(kc p) n -> p kc n", p=128)
                    for idx, oc in enumerate([2 * g_, 2 * g_ + 1, 16 + g_, 24 + g_]):
                        slot = idx % 2
                        wsem = gW[0] if slot == 0 else gW[2]
                        g.wait(tS, slab_free[slot])
                        g.inc(g.dma_start(AR[:, A_SLB[slot]:A_SLB[slot] + 1024]
                                          .rearrange("p (kc n) -> p kc n", n=128),
                                          wsrc[:, :, 128 * oc:128 * oc + 128]), wsem, 16)
                        ldw = wsem.n
                        for qt in range(QT):
                            bank = qt % 2
                            qtw = slice(512 * qt, 512 * qt + 512)
                            t.wait(wsem, ldw)
                            t.wait(sS, bank_free_s[bank])
                            t.wait(vS, bank_free_v[bank])
                            for kc in range(NCH):
                                t.inc(t.matmul(
                                    PS[bank][:],
                                    AR[:, A_SLB[slot] + 128 * kc:A_SLB[slot] + 128 * kc + 128],
                                    fm(XNBF, kc, qtw),
                                    start=(kc == 0), stop=(kc == NCH - 1)), tS, 1)
                            mm = tS.n
                            s.wait(tS, mm)
                            if idx < 2:
                                for hh in range(2):
                                    j_ = 2 * idx + hh
                                    s.inc(s.activation(
                                        HC[0:64, qext(j_) + 512 * qt:qext(j_) + 512 * qt + 512],
                                        PS[bank][64 * hh:64 * hh + 64, :],
                                        AF.Copy, scale=0.125), sS, 1)
                            elif idx == 2:
                                for j_ in range(4):
                                    hh = j_ // 2
                                    s.inc(s.copy(
                                        HC[0:64, kext(j_) + 512 * qt:kext(j_) + 512 * qt + 512],
                                        PS[bank][64 * hh:64 * hh + 64, :]), sS, 1)
                            else:
                                s.inc(s.copy(AR[:, A3_VS + 512 * qt:A3_VS + 512 * qt + 512],
                                             PS[bank][:]), sS, 1)
                            bank_free_s[bank] = sS.n
                        slab_free[slot] = tS.n
                    qkv_ev = sS.n
                    g.wait(sS, qkv_ev)
                    g.wait(yD, ld_er)
                    for j_ in range(4):
                        h_ = 4 * g_ + j_
                        g.inc(g.dma_start(HC[64:68, qext(j_):qext(j_) + T],
                                          AR[4 * h_:4 * h_ + 4, A3_QER:A3_QER + T]),
                              gW[1], 16)
                        g.inc(g.dma_start(HC[64:68, kext(j_):kext(j_) + T],
                                          AR[4 * h_:4 * h_ + 4, A3_KER:A3_KER + T]),
                              gW[1], 16)
                    ext_dma = gW[1].n
                    v.wait(sS, sS.n)
                    v.inc(v.memset(AR[:, A3_VT:A3_VT + 2080]
                                   .rearrange("p (kt c) -> p kt c", c=130)[:, :, 64:65],
                                   1.0), vS, 1)
                    v.inc(v.memset(AR[:, A3_VT:A3_VT + 2080]
                                   .rearrange("p (kt c) -> p kt c", c=130)[:, :, 129:130],
                                   1.0), vS, 1)
                    ones_done = vS.n
                    for half in range(2):
                        bank = 2 + half
                        t.wait(sS, qkv_ev)
                        t.wait(vS, bank_free_v[bank])
                        for k in range(8):
                            kt = 8 * half + k
                            if k % 4 == 0:
                                t.wait(sS, bank_free_s[bank])
                            t.inc(t.matmul(PS[bank][:, 128 * (k % 4):128 * (k % 4) + 128],
                                           AR[:, A3_VS + 128 * kt:A3_VS + 128 * kt + 128],
                                           IDB[:],
                                           start=(k % 4 == 0), stop=(k % 4 == 3)), tS, 1)
                            if k % 4 == 3:
                                mm = tS.n
                                s.wait(tS, mm)
                                kt0 = kt - 3
                                for hh in range(2):
                                    s.inc(s.copy(
                                        AR[:, A3_VT + 130 * kt0:A3_VT + 130 * (kt0 + 4)]
                                        .rearrange("p (kt c) -> p kt c", c=130)
                                        [:, :, 65 * hh:65 * hh + 64],
                                        PS[bank][:].rearrange("p (kt c) -> p kt c", c=128)
                                        [:, :, 64 * hh:64 * hh + 64]), sS, 1)
                                bank_free_s[bank] = sS.n
                    vt_done = sS.n
                    SCBH = [0, 1, 4, 5]
                    PVH = [2, 3, 6, 7]
                    for qt in range(QT):
                        nkt = 4 * qt + 4
                        exp_free = [0] * 4
                        pb_free = [[0, 0] for _ in range(4)]
                        pv_last = [0] * 4
                        def p3_pv_wave(kt):
                            off = 128 * (kt - 4 * qt) if kt >= 4 * qt else 0
                            for j_ in range(4):
                                pv_bank = PVH[j_]
                                pb = A3_PB[2 * j_ + (kt % 2)]
                                t.wait(sS, exp_done_at[j_][kt % 2])
                                tri = tri_done_at[j_][kt % 2]
                                if tri:
                                    t.wait(vS, tri)
                                if kt == 0:
                                    t.wait(sS, bank_free_s[pv_bank])
                                    t.wait(vS, bank_free_v[pv_bank])
                                t.inc(t.matmul(
                                    PS[pv_bank][0:65, slice(off, 512)],
                                    AR[:, A3_VT + 130 * kt + 65 * (j_ // 2):
                                       A3_VT + 130 * kt + 65 * (j_ // 2) + 65],
                                    AR[:, pb + off:pb + 512],
                                    start=(kt == 0), stop=(kt == nkt - 1)), tS, 1)
                                pb_free[j_][kt % 2] = tS.n
                                pv_last[j_] = tS.n

                        exp_done_at = [[0, 0] for _ in range(4)]
                        tri_done_at = [[None, None] for _ in range(4)]
                        for kt in range(nkt):
                            off = 128 * (kt - 4 * qt) if kt >= 4 * qt else 0
                            w_ = slice(off, 512)
                            diag = kt >= 4 * qt
                            for j_ in range(4):
                                h_ = 4 * g_ + j_
                                sc_bank = SCBH[j_]
                                t.wait(sS, bank_free_s[sc_bank])
                                t.wait(vS, bank_free_v[sc_bank])
                                t.wait(gW[1], ext_dma)
                                t.wait(sS, vt_done)
                                t.wait(vS, ones_done)
                                t.inc(t.matmul(
                                    PS[sc_bank][:, w_],
                                    HC[0:68, kext(j_) + 128 * kt:kext(j_) + 128 * kt + 128],
                                    HC[0:68, qext(j_) + 512 * qt + off:qext(j_) + 512 * qt + 512],
                                    start=True, stop=True), tS, 1)
                                sc_mm = tS.n
                                if diag and h_ < 3:
                                    v.wait(tS, sc_mm)
                                    v.inc(v.tensor_add(PS[sc_bank][:, off:off + 128],
                                                       PS[sc_bank][:, off:off + 128],
                                                       MTRI[:]), vS, 1)
                                    mask_done = vS.n
                                else:
                                    mask_done = None
                                pb = A3_PB[2 * j_ + (kt % 2)]
                                s.wait(tS, sc_mm)
                                if mask_done is not None:
                                    s.wait(vS, mask_done)
                                s.wait(tS, pb_free[j_][kt % 2])
                                s.inc(s.activation(AR[:, pb + off:pb + 512],
                                                   PS[sc_bank][:, w_], AF.Exp), sS, 1)
                                exp_done_at[j_][kt % 2] = sS.n
                                bank_free_s[sc_bank] = sS.n
                                if diag and h_ >= 3:
                                    v.wait(sS, sS.n)
                                    v.inc(v.tensor_mul(AR[:, pb + off:pb + off + 128],
                                                       AR[:, pb + off:pb + off + 128],
                                                       TRI[:]), vS, 1)
                                    tri_done_at[j_][kt % 2] = vS.n
                                else:
                                    tri_done_at[j_][kt % 2] = None
                            if kt >= 1:
                                p3_pv_wave(kt - 1)
                        p3_pv_wave(nkt - 1)
                        dcol = 512 * (qt % 2)
                        v.wait(yX[qt % 2], dnsp[qt % 2])
                        for j_ in range(4):
                            v.wait(tS, pv_last[j_])
                            v.inc(v.tensor_copy(ARF[32 * j_:32 * j_ + 1, dcol:dcol + 512],
                                                PS[PVH[j_]][64:65, :]), vS, 1)
                            outb = A3_OUT[j_ // 2]
                            hrow = slice(64 * (j_ % 2), 64 * (j_ % 2) + 64)
                            if j_ % 2 == 0:
                                v.wait(yS[j_ // 2], p3_sp[j_ // 2])
                            v.inc(v.tensor_copy(AR[hrow, outb:outb + 512],
                                                PS[PVH[j_]][0:64, :]), vS, 1)
                            bank_free_s[PVH[j_]] = sS.n
                            bank_free_v[PVH[j_]] = vS.n
                            if j_ % 2 == 1:
                                div_done = vS.n
                                y.wait(vS, div_done)
                                p3_sp[j_ // 2] = ydma_on(
                                    yS[j_ // 2],
                                    dr["apair"][2 * g_ + j_ // 2, :,
                                                512 * qt:512 * qt + 512],
                                    AR[:, outb:outb + 512])
                        dn_ev = vS.n
                        y.wait(vS, dn_ev)
                        for j_ in range(4):
                            dnsp[qt % 2] = ydma_on(
                                yX[qt % 2],
                                dr["dn3"][4 * g_ + j_:4 * g_ + j_ + 1,
                                          512 * qt:512 * qt + 512],
                                ARF[32 * j_:32 * j_ + 1, dcol:dcol + 512])
                A3_WO, A3_BP = 0, 16384
                g.wait(tS, tS.n)
                g.wait(sS, sS.n)
                g.wait(vS, vS.n)
                ld_wo = gdma(AR[:, A3_WO:A3_WO + 16384]
                             .rearrange("p (r n) -> p r n", n=1024),
                             io["wout"].rearrange("(r p) n -> p r n", p=128))
                y.wait(yX[0], yX[0].n)
                y.wait(yX[1], yX[1].n)
                y.wait(vS, vS.n)
                ld_dn = ydma(ARF[0:32, 0:2048], dr["dn3"])
                v.wait(yD, ld_dn)
                vbar(v.reciprocal(ARF[0:32, 2048:4096], ARF[0:32, 0:2048]))
                v.inc(v.tensor_copy(RC3[0:32, :], ARF[0:32, 2048:4096]), vS, 1)
                rc3_done = vS.n
                for qt in range(QT):
                    qtw = slice(512 * qt, 512 * qt + 512)
                    y.wait(tS, tS.n)
                    ld_bp = 0
                    for p_ in range(16):
                        ld_bp = ydma(AR[:, A3_BP + 512 * p_:A3_BP + 512 * p_ + 512],
                                     dr["apair"][p_, :, qtw])
                    for p_ in range(16):
                        bank = 4 + (p_ % 2)
                        t.wait(yD, ld_bp)
                        t.wait(vS, bank_free_v[bank])
                        t.wait(vS, rc3_done)
                        t.inc(t.matmul(PS[bank][:],
                                       SELP3[:, 128 * p_:128 * p_ + 128],
                                       RC3[0:32, 512 * qt:512 * qt + 512],
                                       start=True, stop=True), tS, 1)
                        bmm = tS.n
                        v.wait(tS, bmm)
                        v.inc(v.tensor_mul(AR[:, A3_BP + 512 * p_:A3_BP + 512 * p_ + 512],
                                           AR[:, A3_BP + 512 * p_:A3_BP + 512 * p_ + 512],
                                           PS[bank][:]), vS, 1)
                        bank_free_v[bank] = vS.n
                    norm_done = vS.n
                    for c_ in range(NCH):
                        bank = c_ % 4
                        t.wait(gD, ld_wo)
                        t.wait(vS, norm_done)
                        t.wait(vS, bank_free_v[bank])
                        t.wait(sS, bank_free_s[bank])
                        for p_ in range(16):
                            t.inc(t.matmul(
                                PS[bank][:],
                                AR[:, A3_WO + 1024 * p_ + 128 * c_:
                                   A3_WO + 1024 * p_ + 128 * c_ + 128],
                                AR[:, A3_BP + 512 * p_:A3_BP + 512 * p_ + 512],
                                start=(p_ == 0), stop=(p_ == 15)), tS, 1)
                        mm = tS.n
                        v.wait(tS, mm)
                        v.inc(v.tensor_add(fm(ACC, c_, qtw), fm(ACC, c_, qtw),
                                           PS[bank][:]), vS, 1)
                        bank_free_v[bank] = vS.n
                if "p3" in _DEBUG:
                    y.wait(vS, vS.n)
                    ydma(dbg["acc3"].rearrange("(c p) t -> p c t", p=128),
                         ACC[:].rearrange("p (c t) -> p c t", c=NCH))

            if "p4" in _PHASES:
                A4_H2, A4_WV, A4_PJ, A4_MB = 0, 16384, 20480, 22528

                def abuf(jp16, t_):
                    base = 2048 * (jp16 % 8) + 512 * t_
                    tile = XNBF if jp16 < 8 else HC
                    return tile[:, base:base + 512]

                y.wait(tS, tS.n)
                y.wait(vS, vS.n)
                y.wait(sS, sS.n)
                if "p2" in _PHASES:
                    y.wait(gD, p2_mask_gd)
                    y.wait(yS[0], p2_h2_free[0])
                    y.wait(yS[1], p2_h2_free[1])
                slab_free[0] = slab_free[1] = tS.n
                ld_h2 = 0
                for c_ in range(NCH):
                    ld_h2 = ydma(AR[:, A4_H2 + 2048 * c_:A4_H2 + 2048 * (c_ + 1)],
                                 dr["h2"][128 * c_:128 * c_ + 128, :])
                wv_src = io["wv"].rearrange("e (kc p) n -> e p kc n", p=128)
                pj_src = io["wp"].rearrange("e (r p) n -> e p r n", p=128)
                pj_free = [tS.n, tS.n]
                mrow_ld = 0
                for e_ in range(_P4LIM[0]):
                    y.wait(tS, tS.n)
                    mrow_ld = ydma(RC3[0:1, 0:T], dr["masks"][e_:e_ + 1, :])
                    for t_ in range(4):
                        t.wait(yD, mrow_ld)
                        t.wait(sS, bank_free_s[6])
                        t.wait(vS, bank_free_v[6])
                        t.inc(t.matmul(PS[6][:], ONE1B[:],
                                       RC3[0:1, 512 * t_:512 * t_ + 512],
                                       start=True, stop=True), tS, 1)
                        mm = tS.n
                        s.wait(tS, mm)
                        s.inc(s.copy(AR[:, A4_MB + 512 * t_:A4_MB + 512 * t_ + 512],
                                     PS[6][:]), sS, 1)
                        bank_free_s[6] = sS.n
                    mb_done = sS.n
                    for half in range(_P4LIM[1]):
                        for jp16 in range(_P4LIM[2]):
                            jp = 16 * half + jp16
                            slot = jp16 % 2
                            wsem = gW[0] if slot == 0 else gW[2]
                            g.wait(tS, slab_free[slot])
                            g.inc(g.dma_start(
                                AR[:, A4_WV + 2048 * slot:A4_WV + 2048 * slot + 1024]
                                .rearrange("p (kc n) -> p kc n", n=128),
                                wv_src[e_, :, :, 128 * jp:128 * jp + 128]), wsem, 16)
                            g.inc(g.dma_start(
                                AR[:, A4_WV + 2048 * slot + 1024:A4_WV + 2048 * slot + 2048]
                                .rearrange("p (kc n) -> p kc n", n=128),
                                wv_src[e_, :, :, 4096 + 128 * jp:4096 + 128 * jp + 128]),
                                wsem, 16)
                            ldw = wsem.n
                            for t_ in range(4):
                                gb, vb = t_ % 2, 2 + t_ % 2
                                t.wait(wsem, ldw)
                                t.wait(yD, ld_h2)
                                t.wait(sS, bank_free_s[gb])
                                t.wait(vS, bank_free_v[vb])
                                for kc in range(NCH):
                                    t.inc(t.matmul(
                                        PS[gb][:],
                                        AR[:, A4_WV + 2048 * slot + 128 * kc:
                                           A4_WV + 2048 * slot + 128 * kc + 128],
                                        AR[:, A4_H2 + 2048 * kc + 512 * t_:
                                           A4_H2 + 2048 * kc + 512 * t_ + 512],
                                        start=(kc == 0), stop=(kc == NCH - 1)), tS, 1)
                                for kc in range(NCH):
                                    t.inc(t.matmul(
                                        PS[vb][:],
                                        AR[:, A4_WV + 2048 * slot + 1024 + 128 * kc:
                                           A4_WV + 2048 * slot + 1024 + 128 * kc + 128],
                                        AR[:, A4_H2 + 2048 * kc + 512 * t_:
                                           A4_H2 + 2048 * kc + 512 * t_ + 512],
                                        start=(kc == 0), stop=(kc == NCH - 1)), tS, 1)
                                gv_mm = tS.n
                                s.wait(tS, gv_mm)
                                s.wait(vS, bank_free_v[gb])
                                s.inc(s.activation(abuf(jp16, t_), PS[gb][:],
                                                   AF.Silu), sS, 1)
                                silu_done = sS.n
                                bank_free_s[gb] = sS.n
                                v.wait(sS, silu_done)
                                v.wait(sS, mb_done)
                                v.inc(v.tensor_mul(abuf(jp16, t_), abuf(jp16, t_),
                                                   PS[vb][:]), vS, 1)
                                vbar(v.tensor_mul(abuf(jp16, t_), abuf(jp16, t_),
                                                  AR[:, A4_MB + 512 * t_:
                                                     A4_MB + 512 * t_ + 512]))
                                bank_free_v[vb] = vS.n
                            slab_free[slot] = tS.n
                        a_done = vS.n
                        pj_free[1] = max(pj_free[1], tS.n)
                        pj_base = [A4_PJ, A4_WV]
                        pjsem = [gW[1], gD]
                        for c_ in range(NCH):
                            sl = c_ % 2
                            g.wait(tS, pj_free[sl])
                            g.inc(g.dma_start(
                                AR[:, pj_base[sl]:pj_base[sl] + 2048]
                                .rearrange("p (r n) -> p r n", n=128),
                                pj_src[e_, :, 16 * half:16 * half + 16,
                                       128 * c_:128 * c_ + 128]), pjsem[sl], 16)
                            ld_pj = pjsem[sl].n
                            for t_ in range(4):
                                bank = 4 + (t_ % 4)
                                t.wait(pjsem[sl], ld_pj)
                                t.wait(vS, a_done)
                                t.wait(vS, bank_free_v[bank])
                                for jp16 in range(16):
                                    t.inc(t.matmul(
                                        PS[bank][:],
                                        AR[:, pj_base[sl] + 128 * jp16:
                                           pj_base[sl] + 128 * jp16 + 128],
                                        abuf(jp16, t_),
                                        start=(jp16 == 0), stop=(jp16 == 15)), tS, 1)
                                mm = tS.n
                                v.wait(tS, mm)
                                v.inc(v.tensor_add(
                                    fm(ACC, c_, slice(512 * t_, 512 * t_ + 512)),
                                    fm(ACC, c_, slice(512 * t_, 512 * t_ + 512)),
                                    PS[bank][:]), vS, 1)
                                bank_free_v[bank] = vS.n
                            pj_free[sl] = tS.n
                        slab_free[0] = slab_free[1] = tS.n
                if "p4" in _DEBUG:
                    y.wait(vS, vS.n)
                    ydma(dbg["acc4"].rearrange("(c p) t -> p c t", p=128),
                         ACC[:].rearrange("p (c t) -> p c t", c=NCH))

            if "p5" in _SKIP:
                y.wait(yD, yD.n)
                y.wait(yX[0], yX[0].n)
                y.wait(yX[1], yX[1].n)
                y.wait(vS, vS.n)
                y.wait(sS, sS.n)
                y.wait(tS, tS.n)
                return
            OUTS = [ARF[:, 2048:3072], ARF[:, 3072:4096]]
            p5_s0 = sS.n
            p5_v0 = vS.n
            st5 = [0] * NTT
            ev5 = [0] * NTT
            for i in range(NTT):
                xt = XT[i % 2]
                if i < 2:
                    y.wait(sS, p5_s0)
                    y.wait(vS, p5_v0)
                else:
                    y.wait(vS, ev5[i - 2])
                ld = ydma_on(yX[i % 2], xt[:], io["x"][128 * i:128 * i + 128, :])
                if i == 0:
                    t.wait(vS, p5_v0)
                    t.wait(sS, p5_s0)
                else:
                    t.wait(vS, ev5[i - 1])
                for half, bank in [(0, PS[2]), (1, PS[3])]:
                    for k in range(4):
                        ch = 4 * half + k
                        t.inc(t.matmul(bank[:, 128 * k:128 * k + 128],
                                       fm(ACC, ch, slice(128 * i, 128 * i + 128)),
                                       IDF[:], start=(k == 0), stop=(k == 3)), tS, 1)
                tp = tS.n
                v.wait(tS, tp)
                v.wait(yX[i % 2], ld)
                if i >= 2:
                    v.wait(yS[i % 2], st5[i - 2])
                outt = OUTS[i % 2]
                v.inc(v.tensor_add(outt[:, 0:512], PS[2][:], xt[:, 0:512]), vS, 1)
                v.inc(v.tensor_add(outt[:, 512:1024], PS[3][:], xt[:, 512:1024]), vS, 1)
                ev5[i] = vS.n
                y.wait(vS, ev5[i])
                st5[i] = ydma_on(yS[i % 2], io["out"][128 * i:128 * i + 128, :], outt[:])
            y.wait(yD, yD.n)
            for s_ in (yS[0], yS[1], yX[0], yX[1], gW[0], gW[1], gW[2], gD):
                y.wait(s_, s_.n)

        with nc.Block() as block:
            @block.tensor
            def _(eng):
                plan(Plan("tensor", eng))

            @block.vector
            def _(eng):
                plan(Plan("vector", eng))

            @block.scalar
            def _(eng):
                plan(Plan("scalar", eng))

            @block.gpsimd
            def _(eng):
                plan(Plan("gpsimd", eng))

            @block.sync
            def _(eng):
                plan(Plan("sync", eng))

    return nc


def _in_maps(inputs):
    w = {
        "wqkv": np.ascontiguousarray(inputs["L0_attn_W_qkv"]),
        "wout": np.ascontiguousarray(inputs["L0_attn_W_out"]),
        "ln1w": np.ascontiguousarray(inputs["L0_ffn_ln1_w"]).reshape(C),
        "ln1b": np.ascontiguousarray(inputs["L0_ffn_ln1_b"]).reshape(C),
        "ln2w": np.ascontiguousarray(inputs["L0_ffn_ln2_w"]).reshape(C),
        "ln2b": np.ascontiguousarray(inputs["L0_ffn_ln2_b"]).reshape(C),
        "convw": np.ascontiguousarray(inputs["L0_ffn_conv_w"]).reshape(C, 3),
        "wattn": np.ascontiguousarray(inputs["L0_ffn_attn_w"]),
        "wproj": np.ascontiguousarray(inputs["L0_ffn_proj_w"]),
        "wrout": np.ascontiguousarray(inputs["L0_ffn_router_w"]),
        "wv": np.ascontiguousarray(inputs["L0_ffn_expert_wv"]),
        "wp": np.ascontiguousarray(inputs["L0_ffn_expert_proj"]),
    }
    x = np.ascontiguousarray(inputs["L0_attn_inputs"])
    return [dict(w, x=x[b]) for b in range(B)]


def kernel(**inputs):
    nc = build_nc()
    maps = _in_maps(inputs)
    res = run_bass_kernel_spmd(nc, maps, core_ids=list(range(B)))
    return np.stack([np.asarray(r["out"], dtype=np.float32) for r in res.results])
